# version 01
# baseline (speedup 1.0000x reference)
import numpy as np
import ml_dtypes
from contextlib import ExitStack
import concourse.bass as bass
import concourse.mybir as mybir
from concourse.bass_utils import run_bass_kernel_spmd

F32 = mybir.dt.float32
BF16 = mybir.dt.bfloat16
U8 = mybir.dt.uint8
I32 = mybir.dt.int32
AF = mybir.ActivationFunctionType
ALU = mybir.AluOpType
AX = mybir.AxisListType

NEG = -1e30
BF = ml_dtypes.bfloat16
ID128B = np.eye(128, dtype=BF)
ID128F = np.eye(128, dtype=np.float32)


class Ctr:
    def __init__(self, h, step=1):
        self.h = h
        self.n = 0
        self.step = step

    def inc(self):
        self.n += self.step
        return (self.h, self.step)

    def at(self):
        return (self.h, self.n)


class Prog:
    ENGS = ("sync", "tensor", "vector", "scalar", "gpsimd")

    def __init__(self, nc, ctx):
        self.nc = nc
        self.ops = {e: [] for e in self.ENGS}
        self.ctr = {}
        for e in self.ENGS:
            h = ctx.enter_context(nc.semaphore("sem_" + e))
            self.ctr[e] = Ctr(h, 1)
        self.dma = Ctr(ctx.enter_context(nc.semaphore("sem_dma")), 16)
        self.dmo = Ctr(ctx.enter_context(nc.semaphore("sem_dmo")), 16)

    def emit(self, eng, fn, after=(), ctr=None):
        waits = [c.at() for c in after]
        c = ctr if ctr is not None else self.ctr[eng]
        inc = c.inc()
        self.ops[eng].append((fn, waits, inc))
        return c

    def t(self, fn, after=()):
        return self.emit("tensor", fn, after)

    def v(self, fn, after=()):
        return self.emit("vector", fn, after)

    def s(self, fn, after=()):
        return self.emit("scalar", fn, after)

    def g(self, fn, after=()):
        return self.emit("gpsimd", fn, after)

    def ld(self, fn, after=()):
        return self.emit("sync", fn, after, ctr=self.dma)

    def st(self, fn, after=()):
        return self.emit("sync", fn, after, ctr=self.dmo)

    def barrier(self):
        marks = [self.ctr[e].at() for e in self.ENGS] + [self.dma.at(), self.dmo.at()]
        for e in self.ENGS:
            waits = [m for m in marks if m[0] is not self.ctr[e].h]

            def fn(eng):
                return eng.nop()
            self.ops[e].append((fn, waits, self.ctr[e].inc()))

    def finish(self):
        self.ops["sync"].append((lambda e: None, [self.dmo.at()], None))

    def play(self, block):
        for e in self.ENGS:
            deco = getattr(block, e)
            ops = self.ops[e]

            def make(ops):
                def body(eng):
                    for fn, waits, inc in ops:
                        for h, v in waits:
                            eng.wait_ge(h, v)
                        inst = fn(eng)
                        if inst is not None and inc is not None:
                            inst.then_inc(*inc)
                return body
            deco(make(ops))


def ALL(P):
    return [P.ctr[e] for e in ("tensor", "vector", "scalar", "gpsimd")] + [P.dma]


def emit_ln(P, out, x, dump, mcol, ssq, var, rstd, eps_col, n=1024):
    P.s(lambda e: e.activation(dump[:], x[:], AF.Square, accum_out=ssq[:]),
        after=[P.ctr["vector"]])
    P.v(lambda e: e.tensor_reduce(mcol[:], x[:], AX.X, ALU.add))
    P.v(lambda e: e.tensor_scalar(mcol[:], mcol[:], 1.0 / n, 0.0, ALU.mult, ALU.add),
        after=[P.ctr["vector"]])
    P.v(lambda e: e.tensor_tensor(var[:], mcol[:], mcol[:], ALU.mult),
        after=[P.ctr["scalar"], P.ctr["vector"]])
    P.v(lambda e: e.scalar_tensor_tensor(var[:], ssq[:], 1.0 / n, var[:], ALU.mult, ALU.subtract),
        after=[P.ctr["vector"]])
    P.v(lambda e: e.tensor_scalar(var[:], var[:], 0.0, 0.0, ALU.max, ALU.add),
        after=[P.ctr["vector"]])
    P.s(lambda e: e.activation(rstd[:], var[:], AF.Sqrt, bias=eps_col[:]), after=[P.ctr["vector"]])
    P.v(lambda e: e.reciprocal(rstd[:], rstd[:]), after=[P.ctr["scalar"]])
    P.v(lambda e: e.tensor_scalar(out[:], x[:], mcol[:], 0.0, ALU.subtract, ALU.add),
        after=[P.ctr["vector"]])
    P.v(lambda e: e.tensor_scalar(out[:], out[:], rstd[:], 0.0, ALU.mult, ALU.add))


def emit_tr(P, dst, src, ptr, idn, ncols, after=()):
    for c in range(ncols):
        P.t(lambda e, c=c: e.transpose(ptr[:], src[:, c * 128:(c + 1) * 128], idn[:]),
            after=list(after) + [P.ctr["vector"]])
        P.v(lambda e, c=c: e.tensor_copy(dst(c), ptr[:]), after=[P.ctr["tensor"]])


def emit_quant(P, out_bf, x, a, u, sc, nb):
    x3 = x.rearrange("p (b s) -> p b s", s=16)
    u3 = u.rearrange("p (b s) -> p b s", s=16)
    P.v(lambda e: e.tensor_reduce(sc[:, 0:nb], x3, AX.X, ALU.max, apply_absolute_value=True),
        after=[P.ctr["scalar"]])
    P.v(lambda e: e.tensor_scalar(sc[:, 0:nb], sc[:, 0:nb], 1.0 / 6.0, 1e-12, ALU.mult, ALU.max),
        after=[P.ctr["vector"]])
    P.v(lambda e: e.reciprocal(sc[:, nb:2 * nb], sc[:, 0:nb]), after=[P.ctr["vector"]])
    P.v(lambda e: e.nop(), after=[P.ctr["vector"]])
    for j in range(16):
        P.v(lambda e, j=j: e.tensor_copy(u3[:, :, j], sc[:, nb:2 * nb]))
    P.s(lambda e: e.activation(a, x, AF.Abs), after=[P.ctr["vector"]])
    P.v(lambda e: e.tensor_tensor(a, a, u, ALU.mult), after=[P.ctr["scalar"], P.ctr["vector"]])
    P.v(lambda e: e.tensor_scalar(u, a, 5.0, 2.0, ALU.is_gt, ALU.mult))
    P.v(lambda e: e.scalar_tensor_tensor(u, a, 3.5, u, ALU.is_gt, ALU.add))
    P.v(lambda e: e.scalar_tensor_tensor(u, a, 2.5, u, ALU.is_gt, ALU.add))
    P.v(lambda e: e.tensor_scalar(u, u, 2.0, 0.0, ALU.mult, ALU.add))
    P.v(lambda e: e.scalar_tensor_tensor(u, a, 1.75, u, ALU.is_gt, ALU.add))
    P.v(lambda e: e.scalar_tensor_tensor(u, a, 1.25, u, ALU.is_gt, ALU.add))
    P.v(lambda e: e.scalar_tensor_tensor(u, a, 0.75, u, ALU.is_gt, ALU.add))
    P.v(lambda e: e.scalar_tensor_tensor(u, a, 0.25, u, ALU.is_gt, ALU.add))
    P.s(lambda e: e.activation(a, x, AF.Sign), after=[P.ctr["vector"]])
    P.v(lambda e: e.scalar_tensor_tensor(a, a, 0.5, u, ALU.mult, ALU.mult),
        after=[P.ctr["scalar"]])
    P.v(lambda e: e.nop(), after=[P.ctr["vector"]])
    for j in range(16):
        P.v(lambda e, j=j: e.tensor_copy(u3[:, :, j], sc[:, 0:nb]))
    P.v(lambda e: e.tensor_tensor(out_bf, a, u, ALU.mult), after=[P.ctr["vector"]])


def build_ffn():
    nc = bass.Bass("TRN2", target_bir_lowering=False)
    X = nc.dram_tensor("X", [2048, 1024], F32, kind="ExternalInput").ap()
    WG = nc.dram_tensor("WG", [2816, 1024], F32, kind="ExternalInput").ap()
    WU = nc.dram_tensor("WU", [2816, 1024], F32, kind="ExternalInput").ap()
    WD = nc.dram_tensor("WD", [1024, 2816], F32, kind="ExternalInput").ap()
    IDB = nc.dram_tensor("IDB", [128, 128], BF16, kind="ExternalInput").ap()
    Y = nc.dram_tensor("Y", [2048, 1024], F32, kind="ExternalOutput").ap()
    SEC = [(i * 512, min(512, 2816 - i * 512)) for i in range(6)]
    with ExitStack() as ctx:
        E = ctx.enter_context
        P = Prog(nc, ctx)
        wgq = E(nc.sbuf_tensor("wgq", [128, 8 * 2816], BF16))
        wuq = E(nc.sbuf_tensor("wuq", [128, 8 * 2816], BF16))
        wdq = E(nc.sbuf_tensor("wdq", [128, 22 * 1024], BF16))
        idb = E(nc.sbuf_tensor("idb", [128, 128], BF16))
        wt = E(nc.sbuf_tensor("wt", [128, 2816], F32))
        qa = E(nc.sbuf_tensor("qa", [128, 2816], F32))
        qu = E(nc.sbuf_tensor("qu", [128, 2816], F32))
        qs = E(nc.sbuf_tensor("qs", [128, 2 * 176], F32))
        qt = E(nc.sbuf_tensor("qt", [128, 2816], BF16))
        xf = E(nc.sbuf_tensor("xf", [128, 1024], F32))
        h = E(nc.sbuf_tensor("h", [128, 1024], F32))
        hq = E(nc.sbuf_tensor("hq", [128, 1024], BF16))
        hqf = E(nc.sbuf_tensor("hqf", [128, 8 * 128], BF16))
        sg = E(nc.sbuf_tensor("sg", [128, 512], F32))
        mid = E(nc.sbuf_tensor("mid", [128, 512], F32))
        midq = E(nc.sbuf_tensor("midq", [128, 512], BF16))
        midqf = E(nc.sbuf_tensor("midqf", [128, 4 * 128], BF16))
        ys = E(nc.sbuf_tensor("ys", [128, 1024], F32))
        mcol = E(nc.sbuf_tensor("mcol", [128, 1], F32))
        ssq = E(nc.sbuf_tensor("ssq", [128, 1], F32))
        var = E(nc.sbuf_tensor("var", [128, 1], F32))
        rstd = E(nc.sbuf_tensor("rstd", [128, 1], F32))
        eps5 = E(nc.sbuf_tensor("eps5", [128, 1], F32))
        ptr = E(nc.psum_tensor("ptr", [128, 128], BF16))
        pg = E(nc.psum_tensor("pg", [128, 512], F32))
        pu = E(nc.psum_tensor("pu", [128, 512], F32))
        py0 = E(nc.psum_tensor("py0", [128, 512], F32))
        py1 = E(nc.psum_tensor("py1", [128, 512], F32))
        block = E(nc.Block())

        P.ld(lambda e: e.dma_start(out=idb[:], in_=IDB))
        P.v(lambda e: e.memset(eps5[:], 1e-5))
        for wi, (W, dst, ntile, K, fmcols) in enumerate([
                (WG, wgq, 22, 1024, 2816), (WU, wuq, 22, 1024, 2816), (WD, wdq, 8, 2816, 1024)]):
            for r in range(ntile):
                P.ld(lambda e, W=W, r=r, K=K: e.dma_start(out=wt[:, 0:K], in_=W[r * 128:(r + 1) * 128, :]),
                     after=ALL(P))
                P.v(lambda e: e.nop(), after=[P.dma])
                emit_quant(P, qt[:, 0:K], wt[:, 0:K], qa[:, 0:K], qu[:, 0:K], qs[:], K // 16)
                emit_tr(P, (lambda c, dst=dst, r=r, fmcols=fmcols:
                            dst[:, c * fmcols + r * 128: c * fmcols + r * 128 + 128]),
                        qt, ptr, idb, K // 128)
        P.barrier()
        for i in range(16):
            P.ld(lambda e, i=i: e.dma_start(out=xf[:], in_=X[i * 128:(i + 1) * 128, :]),
                 after=ALL(P))
            P.v(lambda e: e.nop(), after=[P.dma])
            emit_ln(P, h, xf, qa[:, 0:1024], mcol, ssq, var, rstd, eps5)
            emit_quant(P, hq[:], h[:], qa[:, 0:1024], qu[:, 0:1024], qs[:], 64)
            emit_tr(P, (lambda c: hqf[:, c * 128:(c + 1) * 128]), hq, ptr, idb, 8)
            for si, (s0, sn) in enumerate(SEC):
                tmark = [P.ctr["vector"], P.ctr["scalar"]]
                for c in range(8):
                    P.t(lambda e, c=c, s0=s0, sn=sn: e.matmul(
                        pg[:, 0:sn], hqf[:, c * 128:(c + 1) * 128],
                        wgq[:, c * 2816 + s0:c * 2816 + s0 + sn],
                        start=(c == 0), stop=(c == 7)), after=tmark)
                for c in range(8):
                    P.t(lambda e, c=c, s0=s0, sn=sn: e.matmul(
                        pu[:, 0:sn], hqf[:, c * 128:(c + 1) * 128],
                        wuq[:, c * 2816 + s0:c * 2816 + s0 + sn],
                        start=(c == 0), stop=(c == 7)), after=tmark)
                P.s(lambda e, sn=sn: e.activation(sg[:, 0:sn], pg[:, 0:sn], AF.Silu),
                    after=[P.ctr["tensor"]])
                P.v(lambda e, sn=sn: e.tensor_tensor(mid[:, 0:sn], sg[:, 0:sn], pu[:, 0:sn],
                                                     ALU.mult),
                    after=[P.ctr["scalar"], P.ctr["tensor"]])
                emit_quant(P, midq[:, 0:sn], mid[:, 0:sn], qa[:, 0:sn], qu[:, 0:sn],
                           qs[:], sn // 16)
                emit_tr(P, (lambda c: midqf[:, c * 128:(c + 1) * 128]), midq, ptr, idb,
                        sn // 128)
                for c4 in range(sn // 128):
                    cg = si * 4 + c4
                    st = (si == 0 and c4 == 0)
                    sp = (si == 5 and c4 == (sn // 128) - 1)
                    P.t(lambda e, c4=c4, cg=cg, st=st, sp=sp: e.matmul(
                        py0[:], midqf[:, c4 * 128:(c4 + 1) * 128],
                        wdq[:, cg * 1024:cg * 1024 + 512],
                        start=st, stop=sp), after=[P.ctr["vector"]])
                    P.t(lambda e, c4=c4, cg=cg, st=st, sp=sp: e.matmul(
                        py1[:], midqf[:, c4 * 128:(c4 + 1) * 128],
                        wdq[:, cg * 1024 + 512:cg * 1024 + 1024],
                        start=st, stop=sp))
            P.v(lambda e: e.tensor_tensor(ys[:, 0:512], py0[:], xf[:, 0:512], ALU.add),
                after=[P.ctr["tensor"]])
            P.v(lambda e: e.tensor_tensor(ys[:, 512:1024], py1[:], xf[:, 512:1024], ALU.add))
            P.st(lambda e, i=i: e.dma_start(out=Y[i * 128:(i + 1) * 128, :], in_=ys[:]),
                 after=[P.ctr["vector"]])
        P.finish()
        P.play(block)
    return nc




def build_attn_a(dbg=False):
    nc = bass.Bass("TRN2", target_bir_lowering=False)
    XKV = nc.dram_tensor("XKV", [2560, 1024], F32, kind="ExternalInput").ap()
    XSK = nc.dram_tensor("XSK", [4, 1024], F32, kind="ExternalInput").ap()
    WQT = nc.dram_tensor("WQT", [1024, 1024], F32, kind="ExternalInput").ap()
    WKT = nc.dram_tensor("WKT", [1024, 1024], F32, kind="ExternalInput").ap()
    WVT = nc.dram_tensor("WVT", [1024, 1024], F32, kind="ExternalInput").ap()
    WOT = nc.dram_tensor("WOT", [1024, 1024], F32, kind="ExternalInput").ap()
    LNG = nc.dram_tensor("LNG", [1, 1024], F32, kind="ExternalInput").ap()
    LNB = nc.dram_tensor("LNB", [1, 1024], F32, kind="ExternalInput").ap()
    WMT = nc.dram_tensor("WMT", [128, 640], F32, kind="ExternalInput").ap()
    KM = nc.dram_tensor("KM", [128, 20], F32, kind="ExternalInput").ap()
    SM = nc.dram_tensor("SM", [4, 2048], BF16, kind="ExternalInput").ap()
    IDB = nc.dram_tensor("IDB", [128, 128], BF16, kind="ExternalInput").ap()
    IDF = nc.dram_tensor("IDF", [128, 128], F32, kind="ExternalInput").ap()
    Y = nc.dram_tensor("Y", [2048, 1024], F32, kind="ExternalOutput").ap()
    if dbg:
        KD = nc.dram_tensor("KD", [128, 2560], F32, kind="ExternalOutput").ap()
        QD = nc.dram_tensor("QD", [128, 2048], F32, kind="ExternalOutput").ap()
        PD = nc.dram_tensor("PD", [128, 640], F32, kind="ExternalOutput").ap()
        OD = nc.dram_tensor("OD", [128, 65], F32, kind="ExternalOutput").ap()
        AD = nc.dram_tensor("AD", [128, 16384], F32, kind="ExternalOutput").ap()
        Y1D = nc.dram_tensor("Y1D", [2048, 1024], F32, kind="ExternalOutput").ap()
    with ExitStack() as ctx:
        E = ctx.enter_context
        P = Prog(nc, ctx)
        x_fm = E(nc.sbuf_tensor("x_fm", [128, 8 * 2560], BF16))
        att_fm = E(nc.sbuf_tensor("att_fm", [128, 8 * 2048], BF16))
        wot = E(nc.sbuf_tensor("wot", [128, 8 * 1024], BF16))
        wpr = E(nc.sbuf_tensor("wpr", [128, 3 * 1024], BF16))
        wld = E(nc.sbuf_tensor("wld", [128, 1024], F32))
        kA = E(nc.sbuf_tensor("kA", [128, 2560], BF16))
        kB = E(nc.sbuf_tensor("kB", [128, 2560], BF16))
        qA = E(nc.sbuf_tensor("qA", [128, 2048], BF16))
        qB = E(nc.sbuf_tensor("qB", [128, 2048], BF16))
        vA = E(nc.sbuf_tensor("vA", [128, 20 * 65], BF16))
        vB = E(nc.sbuf_tensor("vB", [128, 20 * 65], BF16))
        kskA = E(nc.sbuf_tensor("kskA", [128, 4], BF16))
        kskB = E(nc.sbuf_tensor("kskB", [128, 4], BF16))
        vskA = E(nc.sbuf_tensor("vskA", [4, 65], BF16))
        vskB = E(nc.sbuf_tensor("vskB", [4, 65], BF16))
        xsf = E(nc.sbuf_tensor("xsf", [4, 1024], F32))
        xsb = E(nc.sbuf_tensor("xsb", [4, 1024], BF16))
        xs_fm = E(nc.sbuf_tensor("xs_fm", [128, 8 * 4], BF16))
        wmt = E(nc.sbuf_tensor("wmt", [128, 640], BF16))
        km = E(nc.sbuf_tensor("km", [128, 20], F32))
        sm = E(nc.sbuf_tensor("sm", [4, 2048], BF16))
        mskf = E(nc.sbuf_tensor("mskf", [128, 640], F32))
        pt = E(nc.sbuf_tensor("pt", [128, 640], BF16))
        ptsk = E(nc.sbuf_tensor("ptsk", [4, 128], BF16))
        xt = E(nc.sbuf_tensor("xt", [128, 1024], F32))
        lnx = E(nc.sbuf_tensor("lnx", [128, 1024], F32))
        lnb_ = E(nc.sbuf_tensor("lnb_", [128, 1024], BF16))
        dump = E(nc.sbuf_tensor("dump", [128, 1024], F32))
        ao = E(nc.sbuf_tensor("ao", [128, 64], BF16))
        mx = E(nc.sbuf_tensor("mx", [128, 1], F32))
        mx2 = E(nc.sbuf_tensor("mx2", [128, 1], F32))
        rc = E(nc.sbuf_tensor("rc", [128, 1], F32))
        g_bc = E(nc.sbuf_tensor("g_bc", [128, 1024], F32))
        b_bc = E(nc.sbuf_tensor("b_bc", [128, 1024], F32))
        row1 = E(nc.sbuf_tensor("row1", [1, 1024], F32))
        ones1 = E(nc.sbuf_tensor("ones1", [1, 128], F32))
        ys = E(nc.sbuf_tensor("ys", [128, 8 * 1024], F32))
        yln = E(nc.sbuf_tensor("yln", [128, 1024], F32))
        mcol = E(nc.sbuf_tensor("mcol", [128, 1], F32))
        ssq = E(nc.sbuf_tensor("ssq", [128, 1], F32))
        var = E(nc.sbuf_tensor("var", [128, 1], F32))
        rstd = E(nc.sbuf_tensor("rstd", [128, 1], F32))
        eps5 = E(nc.sbuf_tensor("eps5", [128, 1], F32))
        idb = E(nc.sbuf_tensor("idb", [128, 128], BF16))
        idf = E(nc.sbuf_tensor("idf", [128, 128], F32))
        psA = E(nc.psum_tensor("psA", [128, 512], F32))
        psB = E(nc.psum_tensor("psB", [128, 384], F32))
        p2 = E(nc.psum_tensor("p2", [128, 128], F32))
        po = E(nc.psum_tensor("po", [128, 65], F32))
        ptr = E(nc.psum_tensor("ptr", [128, 128], BF16))
        ptrf = E(nc.psum_tensor("ptrf", [128, 128], F32))
        py0 = E(nc.psum_tensor("py0", [128, 512], F32))
        py1 = E(nc.psum_tensor("py1", [128, 512], F32))
        block = E(nc.Block())

        P.ld(lambda e: e.dma_start(out=idb[:], in_=IDB))
        P.ld(lambda e: e.dma_start(out=idf[:], in_=IDF))
        P.ld(lambda e: e.dma_start(out=mskf[:], in_=WMT))
        P.ld(lambda e: e.dma_start(out=km[:], in_=KM))
        P.ld(lambda e: e.dma_start(out=sm[:], in_=SM))
        P.ld(lambda e: e.dma_start(out=row1[:], in_=LNG))
        P.v(lambda e: e.memset(eps5[:], 1e-5), after=[P.dma])
        P.v(lambda e: e.memset(ones1[:], 1.0))
        P.v(lambda e: e.tensor_copy(wmt[:], mskf[:]))
        P.t(lambda e: e.matmul(py0[:], ones1[:], row1[:, 0:512], start=True, stop=True),
            after=[P.ctr["vector"], P.dma])
        P.t(lambda e: e.matmul(py1[:], ones1[:], row1[:, 512:1024], start=True, stop=True))
        P.v(lambda e: e.tensor_copy(g_bc[:, 0:512], py0[:]), after=[P.ctr["tensor"]])
        P.v(lambda e: e.tensor_copy(g_bc[:, 512:1024], py1[:]))
        P.ld(lambda e: e.dma_start(out=row1[:], in_=LNB), after=[P.ctr["tensor"]])
        P.t(lambda e: e.matmul(py0[:], ones1[:], row1[:, 0:512], start=True, stop=True),
            after=[P.ctr["vector"], P.dma])
        P.t(lambda e: e.matmul(py1[:], ones1[:], row1[:, 512:1024], start=True, stop=True))
        P.v(lambda e: e.tensor_copy(b_bc[:, 0:512], py0[:]), after=[P.ctr["tensor"]])
        P.v(lambda e: e.tensor_copy(b_bc[:, 512:1024], py1[:]))

        for jt in range(20):
            P.ld(lambda e, jt=jt: e.dma_start(out=xt[:], in_=XKV[jt * 128:(jt + 1) * 128, :]),
                 after=ALL(P))
            P.v(lambda e: e.nop(), after=[P.dma])
            emit_ln(P, lnx, xt, dump, mcol, ssq, var, rstd, eps5)
            P.v(lambda e: e.tensor_copy(lnb_[:], lnx[:]), after=[P.ctr["vector"]])
            emit_tr(P, (lambda c, jt=jt: x_fm[:, c * 2560 + jt * 128: c * 2560 + jt * 128 + 128]),
                    lnb_, ptr, idb, 8)
        P.v(lambda e: e.memset(xt[:], 0.0), after=ALL(P))
        P.v(lambda e: e.nop(), after=[P.ctr["vector"]])
        P.ld(lambda e: e.dma_start(out=xt[0:4, :], in_=XSK), after=[P.ctr["vector"]])
        P.v(lambda e: e.nop(), after=[P.dma])
        emit_ln(P, lnx, xt, dump, mcol, ssq, var, rstd, eps5)
        P.v(lambda e: e.tensor_copy(xsb[:], lnx[0:4, :]), after=[P.ctr["vector"]])
        for c in range(8):
            P.t(lambda e, c=c: e.transpose(ptr[:, 0:4], xsb[:, c * 128:(c + 1) * 128], idb[0:4, 0:4]),
                after=[P.ctr["vector"]])
            P.v(lambda e, c=c: e.tensor_copy(xs_fm[:, c * 4:(c + 1) * 4],
                                             ptr[0:128, 0:4]), after=[P.ctr["tensor"]])
        for c in range(8):
            P.ld(lambda e, c=c: e.dma_start(out=wld[:], in_=WOT[c * 128:(c + 1) * 128, :]),
                 after=ALL(P))
            P.v(lambda e, c=c: e.tensor_copy(wot[:, c * 1024:(c + 1) * 1024], wld[:]),
                after=[P.dma])

        for hp in range(8):
            pair_t = [P.ctr["vector"], P.ctr["scalar"], P.ctr["tensor"]]
            for wi, W in enumerate([WQT, WKT, WVT]):
                P.ld(lambda e, wi=wi, W=W, hp=hp: e.dma_start(
                    out=wld[:].rearrange("p (c n) -> p c n", n=128),
                    in_=W[:, hp * 128:(hp + 1) * 128].rearrange("(c p) n -> p c n", p=128)),
                    after=ALL(P))
                P.v(lambda e, wi=wi: e.tensor_copy(wpr[:, wi * 1024:(wi + 1) * 1024], wld[:]),
                    after=[P.dma])
            P.v(lambda e: e.memset(kA[64:128, :], 0.0), after=[P.ctr["tensor"]])
            P.v(lambda e: e.memset(kB[64:128, :], 0.0))
            P.v(lambda e: e.memset(kA[64:65, :], 1.0))
            P.v(lambda e: e.memset(kB[64:65, :], 1.0))
            P.v(lambda e: e.memset(qA[64:128, :], 0.0))
            P.v(lambda e: e.memset(qB[64:128, :], 0.0))
            P.v(lambda e: e.memset(kskA[64:128, :], 0.0))
            P.v(lambda e: e.memset(kskB[64:128, :], 0.0))
            P.v(lambda e: e.memset(kskA[64:65, :], 1.0))
            P.v(lambda e: e.memset(kskB[64:65, :], 1.0))
            v3a = vA[:].rearrange("p (j c) -> p j c", c=65)
            v3b = vB[:].rearrange("p (j c) -> p j c", c=65)
            P.v(lambda e: e.memset(v3a[:, :, 64], 1.0))
            P.v(lambda e: e.memset(v3b[:, :, 64], 1.0))
            P.v(lambda e: e.memset(vskA[:, 64:65], 1.0))
            P.v(lambda e: e.memset(vskB[:, 64:65], 1.0))
            for j5 in range(5):
                for c in range(8):
                    P.t(lambda e, j5=j5, c=c: e.matmul(
                        psA[:], wpr[:, 1024 + c * 128:1024 + c * 128 + 128],
                        x_fm[:, c * 2560 + j5 * 512: c * 2560 + j5 * 512 + 512],
                        start=(c == 0), stop=(c == 7)), after=[P.ctr["vector"]])
                P.v(lambda e, j5=j5: e.tensor_copy(kA[0:64, j5 * 512:(j5 + 1) * 512],
                                                   psA[0:64, :]), after=[P.ctr["tensor"]])
                P.v(lambda e, j5=j5: e.tensor_copy(kB[0:64, j5 * 512:(j5 + 1) * 512],
                                                   psA[64:128, :]))
            for j4 in range(4):
                for c in range(8):
                    P.t(lambda e, j4=j4, c=c: e.matmul(
                        psA[:], wpr[:, c * 128:c * 128 + 128],
                        x_fm[:, c * 2560 + 512 + j4 * 512: c * 2560 + 512 + j4 * 512 + 512],
                        start=(c == 0), stop=(c == 7)), after=[P.ctr["vector"]])
                P.v(lambda e, j4=j4: e.tensor_copy(qA[0:64, j4 * 512:(j4 + 1) * 512],
                                                   psA[0:64, :]), after=[P.ctr["tensor"]])
                P.v(lambda e, j4=j4: e.tensor_copy(qB[0:64, j4 * 512:(j4 + 1) * 512],
                                                   psA[64:128, :]))
            for jt in range(20):
                for c in range(8):
                    P.t(lambda e, jt=jt, c=c: e.matmul(
                        psB[:, 0:128], x_fm[:, c * 2560 + jt * 128: c * 2560 + jt * 128 + 128],
                        wpr[:, 2048 + c * 128: 2048 + c * 128 + 128],
                        start=(c == 0), stop=(c == 7)), after=[P.ctr["vector"]])
                P.v(lambda e, jt=jt: e.tensor_copy(vA[:, jt * 65:jt * 65 + 64], psB[:, 0:64]),
                    after=[P.ctr["tensor"]])
                P.v(lambda e, jt=jt: e.tensor_copy(vB[:, jt * 65:jt * 65 + 64], psB[:, 64:128]))
            for c in range(8):
                P.t(lambda e, c=c: e.matmul(
                    psB[:, 128:132], wpr[:, 1024 + c * 128:1024 + c * 128 + 128],
                    xs_fm[:, c * 4:(c + 1) * 4], start=(c == 0), stop=(c == 7)),
                    after=[P.ctr["vector"]])
            P.v(lambda e: e.tensor_copy(kskA[0:64, :], psB[0:64, 128:132]),
                after=[P.ctr["tensor"]])
            P.v(lambda e: e.tensor_copy(kskB[0:64, :], psB[64:128, 128:132]))
            for c in range(8):
                P.t(lambda e, c=c: e.matmul(
                    psB[0:4, 256:384], xs_fm[:, c * 4:(c + 1) * 4],
                    wpr[:, 2048 + c * 128: 2048 + c * 128 + 128],
                    start=(c == 0), stop=(c == 7)), after=[P.ctr["vector"]])
            P.v(lambda e: e.tensor_copy(vskA[:, 0:64], psB[0:4, 256:320]),
                after=[P.ctr["tensor"]])
            P.v(lambda e: e.tensor_copy(vskB[:, 0:64], psB[0:4, 320:384]))
            for hh, (kX, qX, vX, v3X, kskX, vskX) in enumerate(
                    [(kA, qA, vA, v3a, kskA, vskA), (kB, qB, vB, v3b, kskB, vskB)]):
                h = hp * 2 + hh
                for i in range(16):
                    T0 = i * 128
                    P.t(lambda e, qX=qX, kX=kX, T0=T0: e.matmul(
                        psA[:], qX[:, T0:T0 + 128], kX[:, T0:T0 + 512],
                        start=True, stop=True), after=[P.ctr["vector"]])
                    P.t(lambda e, qX=qX, kX=kX, T0=T0: e.matmul(
                        psB[:, 0:128], qX[:, T0:T0 + 128], kX[:, T0 + 512:T0 + 640],
                        start=True, stop=True))
                    P.t(lambda e, qX=qX, kskX=kskX, T0=T0: e.matmul(
                        psB[:, 128:132], qX[:, T0:T0 + 128], kskX[:],
                        start=True, stop=True))
                    P.v(lambda e: e.tensor_reduce(mx[:], psA[:], AX.X, ALU.max),
                        after=[P.ctr["tensor"]])
                    P.v(lambda e: e.tensor_reduce(mx2[:], psB[:, 0:132], AX.X, ALU.max))
                    P.v(lambda e: e.tensor_tensor(mx[:], mx[:], mx2[:], ALU.max),
                        after=[P.ctr["vector"]])
                    P.v(lambda e: e.tensor_scalar(mx[:], mx[:], 0.125, 0.0, ALU.mult, ALU.add),
                        after=[P.ctr["vector"]])
                    P.t(lambda e: e.transpose(ptrf[0:1, 0:128], mx[:], idf[:]),
                        after=[P.ctr["vector"]])
                    P.v(lambda e, qX=qX, T0=T0: e.tensor_scalar(
                        qX[64:65, T0:T0 + 128], ptrf[0:1, 0:128], -8.0, 0.0,
                        ALU.mult, ALU.add), after=[P.ctr["tensor"]])
                    for cjj in range(5):
                        P.t(lambda e, kX=kX, qX=qX, T0=T0, cjj=cjj: e.matmul(
                            p2[:], kX[:, T0 + cjj * 128:T0 + cjj * 128 + 128],
                            qX[:, T0:T0 + 128], start=True, stop=True),
                            after=[P.ctr["vector"], P.ctr["scalar"]])
                        P.s(lambda e, cjj=cjj: e.activation(
                            pt[:, cjj * 128:(cjj + 1) * 128], p2[:], AF.Exp, scale=0.125),
                            after=[P.ctr["tensor"]])
                        P.v(lambda e, cjj=cjj: e.tensor_tensor(
                            pt[:, cjj * 128:(cjj + 1) * 128], pt[:, cjj * 128:(cjj + 1) * 128],
                            wmt[:, cjj * 128:(cjj + 1) * 128], ALU.mult),
                            after=[P.ctr["scalar"]])
                        P.v(lambda e, cjj=cjj, i=i: e.tensor_scalar(
                            pt[:, cjj * 128:(cjj + 1) * 128], pt[:, cjj * 128:(cjj + 1) * 128],
                            km[:, i + cjj:i + cjj + 1], 0.0, ALU.mult, ALU.add))
                    P.t(lambda e, kskX=kskX, qX=qX, T0=T0: e.matmul(
                        ptrf[0:4, :], kskX[:], qX[:, T0:T0 + 128], start=True, stop=True),
                        after=[P.ctr["vector"], P.ctr["scalar"]])
                    P.s(lambda e: e.activation(ptsk[:], ptrf[0:4, :], AF.Exp, scale=0.125),
                        after=[P.ctr["tensor"]])
                    P.v(lambda e, T0=T0: e.tensor_tensor(
                        ptsk[:], ptsk[:], sm[:, T0:T0 + 128], ALU.mult),
                        after=[P.ctr["scalar"]])
                    for cjj in range(5):
                        P.t(lambda e, cjj=cjj, vX=vX, i=i: e.matmul(
                            po[:], pt[:, cjj * 128:(cjj + 1) * 128],
                            vX[:, (i + cjj) * 65:(i + cjj) * 65 + 65],
                            start=(cjj == 0), stop=False), after=[P.ctr["vector"]])
                    P.t(lambda e, vskX=vskX: e.matmul(
                        po[:], ptsk[:], vskX[:], start=False, stop=True))
                    P.v(lambda e: e.reciprocal(rc[:], po[:, 64:65]), after=[P.ctr["tensor"]])
                    P.v(lambda e: e.tensor_scalar(ao[:], po[:, 0:64], rc[:], 0.0,
                                                  ALU.mult, ALU.add), after=[P.ctr["vector"]])
                    P.t(lambda e: e.transpose(ptr[0:64, :], ao[:], idb[:]),
                        after=[P.ctr["vector"]])
                    P.v(lambda e, h=h, T0=T0: e.tensor_copy(
                        att_fm[(h % 2) * 64:(h % 2) * 64 + 64,
                               (h // 2) * 2048 + T0:(h // 2) * 2048 + T0 + 128],
                        ptr[0:64, :]), after=[P.ctr["tensor"]])
                    if dbg and h == 0 and i == 3:
                        P.v(lambda e: e.tensor_copy(dump[:, 0:640], pt[:]),
                            after=[P.ctr["vector"]])
                        P.st(lambda e: e.dma_start(out=PD, in_=dump[:, 0:640]),
                             after=[P.ctr["vector"]])
                        P.v(lambda e: e.tensor_copy(dump[:, 0:65], po[:]))
                        P.st(lambda e: e.dma_start(out=OD, in_=dump[:, 0:65]),
                             after=[P.ctr["vector"]])
            if dbg and hp == 0:
                for seg in range(2):
                    P.v(lambda e, seg=seg: e.tensor_copy(dump[:], kA[:, seg * 1024:(seg + 1) * 1024]),
                        after=[P.ctr["vector"], P.dmo])
                    P.st(lambda e, seg=seg: e.dma_start(out=KD[:, seg * 1024:(seg + 1) * 1024],
                         in_=dump[:]), after=[P.ctr["vector"]])
                P.v(lambda e: e.tensor_copy(dump[:, 0:512], kA[:, 2048:2560]),
                    after=[P.ctr["vector"], P.dmo])
                P.st(lambda e: e.dma_start(out=KD[:, 2048:2560], in_=dump[:, 0:512]),
                     after=[P.ctr["vector"]])
                for seg in range(2):
                    P.v(lambda e, seg=seg: e.tensor_copy(dump[:], qA[:, seg * 1024:(seg + 1) * 1024]),
                        after=[P.ctr["vector"], P.dmo])
                    P.st(lambda e, seg=seg: e.dma_start(out=QD[:, seg * 1024:(seg + 1) * 1024],
                         in_=dump[:]), after=[P.ctr["vector"]])
        if dbg:
            P.barrier()
            for seg in range(16):
                P.v(lambda e, seg=seg: e.tensor_copy(dump[:], att_fm[:, seg * 1024:(seg + 1) * 1024]),
                    after=[P.ctr["vector"], P.dmo])
                P.st(lambda e, seg=seg: e.dma_start(out=AD[:, seg * 1024:(seg + 1) * 1024],
                     in_=dump[:]), after=[P.ctr["vector"]])
        for i in range(16):
            T0 = i * 128
            P.barrier()
            P.ld(lambda e, T0=T0: e.dma_start(out=xt[:], in_=XKV[512 + T0:512 + T0 + 128, :]),
                 after=ALL(P))
            for c in range(8):
                P.t(lambda e, c=c, T0=T0: e.matmul(
                    py0[:], att_fm[:, c * 2048 + T0:c * 2048 + T0 + 128],
                    wot[:, c * 1024:c * 1024 + 512],
                    start=(c == 0), stop=(c == 7)), after=[P.ctr["vector"]])
                P.t(lambda e, c=c, T0=T0: e.matmul(
                    py1[:], att_fm[:, c * 2048 + T0:c * 2048 + T0 + 128],
                    wot[:, c * 1024 + 512:c * 1024 + 1024],
                    start=(c == 0), stop=(c == 7)))
            yss = ys[:, (i % 8) * 1024:(i % 8) * 1024 + 1024]
            P.v(lambda e: e.nop(), after=[P.dma, P.ctr["tensor"], P.dmo])
            P.v(lambda e, yss=yss: e.tensor_tensor(yss[:, 0:512], py0[:], xt[:, 0:512], ALU.add))
            P.v(lambda e, yss=yss: e.tensor_tensor(yss[:, 512:1024], py1[:], xt[:, 512:1024], ALU.add))
            P.v(lambda e: e.nop(), after=[P.ctr["vector"]])
            emit_ln(P, yln, yss, dump, mcol, ssq, var, rstd, eps5)
            P.v(lambda e: e.tensor_tensor(yln[:], yln[:], g_bc[:], ALU.mult),
                after=[P.ctr["vector"]])
            P.v(lambda e: e.tensor_tensor(yln[:], yln[:], b_bc[:], ALU.add))
            P.v(lambda e, yss=yss: e.tensor_tensor(yss[:], yss[:], yln[:], ALU.add),
                after=[P.ctr["vector"]])
            P.st(lambda e, T0=T0, yss=yss: e.dma_start(out=Y[T0:T0 + 128, :], in_=yss[:]),
                 after=[P.ctr["vector"]])
        P.finish()
        P.play(block)
    return nc


def run_attn_a(x, Wq, Wk, Wv, Wo, ln_g, ln_b):
    nc = get_nc("attn_a")
    B, S, D = x.shape
    cc, tt = np.meshgrid(np.arange(640), np.arange(128), indexing="ij")
    wmt0 = ((cc - tt >= 1) & (cc - tt <= 512)).astype(np.float32)
    wmt = np.ascontiguousarray(
        wmt0.reshape(5, 128, 128).transpose(1, 0, 2).reshape(128, 640))
    in_maps = []
    for c in range(8):
        b, half = c // 2, c % 2
        qstart = half * 2048
        xkv = np.zeros((2560, 1024), np.float32)
        lo = qstart - 512
        xkv[max(0, -lo):] = x[b, max(lo, 0):qstart + 2048]
        km = np.ascontiguousarray(
            (np.arange(lo, lo + 2560) >= 0).astype(np.float32).reshape(20, 128).T)
        gq = qstart + np.arange(2048)
        smv = ((gq[None, :] >= np.arange(4)[:, None]) &
               (gq[None, :] - np.arange(4)[:, None] > 511)).astype(np.float32)
        in_maps.append({"XKV": np.ascontiguousarray(xkv), "XSK": np.ascontiguousarray(x[b, 0:4]),
                        "WQT": np.ascontiguousarray(Wq.T), "WKT": np.ascontiguousarray(Wk.T),
                        "WVT": np.ascontiguousarray(Wv.T), "WOT": np.ascontiguousarray(Wo.T),
                        "LNG": ln_g.reshape(1, 1024), "LNB": ln_b.reshape(1, 1024),
                        "WMT": wmt, "KM": km, "SM": smv.astype(BF),
                        "IDB": ID128B, "IDF": ID128F})
    res = run_bass_kernel_spmd(nc, in_maps, core_ids=list(range(8)))
    y = np.stack([res.results[c]["Y"] for c in range(8)], 0).reshape(4, 4096, 1024)
    return y




def build_mha(QOFF0):
    nc = bass.Bass("TRN2", target_bir_lowering=False)
    XF = nc.dram_tensor("XF", [4096, 1024], F32, kind="ExternalInput").ap()
    WQT = nc.dram_tensor("WQT", [1024, 1024], F32, kind="ExternalInput").ap()
    WKT = nc.dram_tensor("WKT", [1024, 1024], F32, kind="ExternalInput").ap()
    WVT = nc.dram_tensor("WVT", [1024, 1024], F32, kind="ExternalInput").ap()
    WOT = nc.dram_tensor("WOT", [1024, 1024], F32, kind="ExternalInput").ap()
    BQ = nc.dram_tensor("BQ", [128, 8], F32, kind="ExternalInput").ap()
    BK = nc.dram_tensor("BK", [128, 8], F32, kind="ExternalInput").ap()
    BVR = nc.dram_tensor("BVR", [1, 1024], F32, kind="ExternalInput").ap()
    BO = nc.dram_tensor("BO", [1, 1024], F32, kind="ExternalInput").ap()
    IDB = nc.dram_tensor("IDB", [128, 128], BF16, kind="ExternalInput").ap()
    IDF = nc.dram_tensor("IDF", [128, 128], F32, kind="ExternalInput").ap()
    Y = nc.dram_tensor("Y", [2048, 1024], F32, kind="ExternalOutput").ap()
    with ExitStack() as ctx:
        E = ctx.enter_context
        P = Prog(nc, ctx)
        x_fm = E(nc.sbuf_tensor("x_fm", [128, 8 * 4096], BF16))
        att_fm = E(nc.sbuf_tensor("att_fm", [128, 8 * 2048], BF16))
        wot = E(nc.sbuf_tensor("wot", [128, 8 * 1024], BF16))
        wpr = E(nc.sbuf_tensor("wpr", [128, 3 * 1024], BF16))
        wld = E(nc.sbuf_tensor("wld", [128, 1024], F32))
        kA = E(nc.sbuf_tensor("kA", [128, 4096], BF16))
        kB = E(nc.sbuf_tensor("kB", [128, 4096], BF16))
        qA = E(nc.sbuf_tensor("qA", [128, 2048], BF16))
        qB = E(nc.sbuf_tensor("qB", [128, 2048], BF16))
        vA = E(nc.sbuf_tensor("vA", [128, 32 * 65], BF16))
        vB = E(nc.sbuf_tensor("vB", [128, 32 * 65], BF16))
        bq = E(nc.sbuf_tensor("bq", [128, 8], F32))
        bk = E(nc.sbuf_tensor("bk", [128, 8], F32))
        bvrow = E(nc.sbuf_tensor("bvrow", [1, 1024], F32))
        bvA_bc = E(nc.sbuf_tensor("bvA_bc", [128, 64], F32))
        bvB_bc = E(nc.sbuf_tensor("bvB_bc", [128, 64], F32))
        pt = E(nc.sbuf_tensor("pt", [128, 4096], BF16))
        xt = E(nc.sbuf_tensor("xt", [128, 1024], F32))
        lnx = E(nc.sbuf_tensor("lnx", [128, 1024], F32))
        lnb_ = E(nc.sbuf_tensor("lnb_", [128, 1024], BF16))
        dump = E(nc.sbuf_tensor("dump", [128, 1024], F32))
        ao = E(nc.sbuf_tensor("ao", [128, 64], BF16))
        mx = E(nc.sbuf_tensor("mx", [128, 1], F32))
        mx2 = E(nc.sbuf_tensor("mx2", [128, 1], F32))
        rc = E(nc.sbuf_tensor("rc", [128, 1], F32))
        bo_bc = E(nc.sbuf_tensor("bo_bc", [128, 1024], F32))
        row1 = E(nc.sbuf_tensor("row1", [1, 1024], F32))
        ones1 = E(nc.sbuf_tensor("ones1", [1, 128], F32))
        ys = E(nc.sbuf_tensor("ys", [128, 2 * 1024], F32))
        mcol = E(nc.sbuf_tensor("mcol", [128, 1], F32))
        ssq = E(nc.sbuf_tensor("ssq", [128, 1], F32))
        var = E(nc.sbuf_tensor("var", [128, 1], F32))
        rstd = E(nc.sbuf_tensor("rstd", [128, 1], F32))
        eps5 = E(nc.sbuf_tensor("eps5", [128, 1], F32))
        idb = E(nc.sbuf_tensor("idb", [128, 128], BF16))
        idf = E(nc.sbuf_tensor("idf", [128, 128], F32))
        psA = E(nc.psum_tensor("psA", [128, 512], F32))
        p2 = E(nc.psum_tensor("p2", [128, 512], F32))
        po = E(nc.psum_tensor("po", [128, 65], F32))
        ptr = E(nc.psum_tensor("ptr", [128, 128], BF16))
        ptrf = E(nc.psum_tensor("ptrf", [128, 128], F32))
        py0 = E(nc.psum_tensor("py0", [128, 512], F32))
        py1 = E(nc.psum_tensor("py1", [128, 512], F32))
        block = E(nc.Block())

        P.ld(lambda e: e.dma_start(out=idb[:], in_=IDB))
        P.ld(lambda e: e.dma_start(out=idf[:], in_=IDF))
        P.ld(lambda e: e.dma_start(out=bq[:], in_=BQ))
        P.ld(lambda e: e.dma_start(out=bk[:], in_=BK))
        P.ld(lambda e: e.dma_start(out=bvrow[:], in_=BVR))
        P.ld(lambda e: e.dma_start(out=row1[:], in_=BO))
        P.v(lambda e: e.memset(eps5[:], 1e-5), after=[P.dma])
        P.v(lambda e: e.memset(ones1[:], 1.0))
        P.t(lambda e: e.matmul(py0[:], ones1[:], row1[:, 0:512], start=True, stop=True),
            after=[P.ctr["vector"], P.dma])
        P.t(lambda e: e.matmul(py1[:], ones1[:], row1[:, 512:1024], start=True, stop=True))
        P.v(lambda e: e.tensor_copy(bo_bc[:, 0:512], py0[:]), after=[P.ctr["tensor"]])
        P.v(lambda e: e.tensor_copy(bo_bc[:, 512:1024], py1[:]))

        for jt in range(32):
            P.ld(lambda e, jt=jt: e.dma_start(out=xt[:], in_=XF[jt * 128:(jt + 1) * 128, :]),
                 after=ALL(P))
            P.v(lambda e: e.nop(), after=[P.dma])
            emit_ln(P, lnx, xt, dump, mcol, ssq, var, rstd, eps5)
            P.v(lambda e: e.tensor_copy(lnb_[:], lnx[:]), after=[P.ctr["vector"]])
            emit_tr(P, (lambda c, jt=jt: x_fm[:, c * 4096 + jt * 128: c * 4096 + jt * 128 + 128]),
                    lnb_, ptr, idb, 8)
        for c in range(8):
            P.ld(lambda e, c=c: e.dma_start(out=wld[:], in_=WOT[c * 128:(c + 1) * 128, :]),
                 after=ALL(P))
            P.v(lambda e, c=c: e.tensor_copy(wot[:, c * 1024:(c + 1) * 1024], wld[:]),
                after=[P.dma])

        for hp in range(8):
            P.t(lambda e, hp=hp: e.matmul(
                ptrf[:, 0:64], ones1[:], bvrow[:, hp * 128:hp * 128 + 64],
                start=True, stop=True), after=[P.ctr["vector"], P.ctr["scalar"]])
            P.v(lambda e: e.tensor_copy(bvA_bc[:], ptrf[:, 0:64]), after=[P.ctr["tensor"]])
            P.t(lambda e, hp=hp: e.matmul(
                ptrf[:, 64:128], ones1[:], bvrow[:, hp * 128 + 64:hp * 128 + 128],
                start=True, stop=True), after=[P.ctr["vector"]])
            P.v(lambda e: e.tensor_copy(bvB_bc[:], ptrf[:, 64:128]), after=[P.ctr["tensor"]])
            for wi, W in enumerate([WQT, WKT, WVT]):
                P.ld(lambda e, wi=wi, W=W, hp=hp: e.dma_start(
                    out=wld[:].rearrange("p (c n) -> p c n", n=128),
                    in_=W[:, hp * 128:(hp + 1) * 128].rearrange("(c p) n -> p c n", p=128)),
                    after=ALL(P))
                P.v(lambda e, wi=wi: e.tensor_copy(wpr[:, wi * 1024:(wi + 1) * 1024], wld[:]),
                    after=[P.dma])
            P.v(lambda e: e.memset(kA[64:128, :], 0.0), after=[P.ctr["tensor"]])
            P.v(lambda e: e.memset(kB[64:128, :], 0.0))
            P.v(lambda e: e.memset(kA[64:65, :], 1.0))
            P.v(lambda e: e.memset(kB[64:65, :], 1.0))
            P.v(lambda e: e.memset(qA[64:128, :], 0.0))
            P.v(lambda e: e.memset(qB[64:128, :], 0.0))
            v3a = vA[:].rearrange("p (j c) -> p j c", c=65)
            v3b = vB[:].rearrange("p (j c) -> p j c", c=65)
            P.v(lambda e, v3a=v3a: e.memset(v3a[:, :, 64], 1.0))
            P.v(lambda e, v3b=v3b: e.memset(v3b[:, :, 64], 1.0))
            for j5 in range(8):
                for c in range(8):
                    P.t(lambda e, j5=j5, c=c: e.matmul(
                        psA[:], wpr[:, 1024 + c * 128:1024 + c * 128 + 128],
                        x_fm[:, c * 4096 + j5 * 512: c * 4096 + j5 * 512 + 512],
                        start=(c == 0), stop=(c == 7)), after=[P.ctr["vector"]])
                P.v(lambda e, j5=j5, hp=hp: e.tensor_scalar(
                    kA[0:64, j5 * 512:(j5 + 1) * 512], psA[0:64, :],
                    bk[0:64, hp:hp + 1], 0.0, ALU.add, ALU.add), after=[P.ctr["tensor"]])
                P.v(lambda e, j5=j5, hp=hp: e.tensor_scalar(
                    kB[0:64, j5 * 512:(j5 + 1) * 512], psA[64:128, :],
                    bk[64:128, hp:hp + 1], 0.0, ALU.add, ALU.add))
            for j4 in range(4):
                for c in range(8):
                    P.t(lambda e, j4=j4, c=c: e.matmul(
                        psA[:], wpr[:, c * 128:c * 128 + 128],
                        x_fm[:, c * 4096 + QOFF0 + j4 * 512: c * 4096 + QOFF0 + j4 * 512 + 512],
                        start=(c == 0), stop=(c == 7)), after=[P.ctr["vector"]])
                P.v(lambda e, j4=j4, hp=hp: e.tensor_scalar(
                    qA[0:64, j4 * 512:(j4 + 1) * 512], psA[0:64, :],
                    bq[0:64, hp:hp + 1], 0.0, ALU.add, ALU.add), after=[P.ctr["tensor"]])
                P.v(lambda e, j4=j4, hp=hp: e.tensor_scalar(
                    qB[0:64, j4 * 512:(j4 + 1) * 512], psA[64:128, :],
                    bq[64:128, hp:hp + 1], 0.0, ALU.add, ALU.add))
            for jt in range(32):
                for c in range(8):
                    P.t(lambda e, jt=jt, c=c: e.matmul(
                        p2[:, 0:128], x_fm[:, c * 4096 + jt * 128: c * 4096 + jt * 128 + 128],
                        wpr[:, 2048 + c * 128: 2048 + c * 128 + 128],
                        start=(c == 0), stop=(c == 7)), after=[P.ctr["vector"]])
                P.v(lambda e, jt=jt: e.tensor_tensor(
                    vA[:, jt * 65:jt * 65 + 64], p2[:, 0:64], bvA_bc[:, 0:64], ALU.add),
                    after=[P.ctr["tensor"]])
                P.v(lambda e, jt=jt: e.tensor_tensor(
                    vB[:, jt * 65:jt * 65 + 64], p2[:, 64:128], bvB_bc[:, 0:64], ALU.add))
            for hh in range(2):
                kX, qX, vX = (kA, qA, vA) if hh == 0 else (kB, qB, vB)
                for i in range(16):
                    T0 = i * 128
                    for j5 in range(8):
                        P.t(lambda e, qX=qX, kX=kX, T0=T0, j5=j5: e.matmul(
                            psA[:], qX[:, T0:T0 + 128], kX[:, j5 * 512:(j5 + 1) * 512],
                            start=True, stop=True), after=[P.ctr["vector"]])
                        if j5 == 0:
                            P.v(lambda e: e.tensor_reduce(mx[:], psA[:], AX.X, ALU.max),
                                after=[P.ctr["tensor"]])
                        else:
                            P.v(lambda e: e.tensor_reduce(mx2[:], psA[:], AX.X, ALU.max),
                                after=[P.ctr["tensor"]])
                            P.v(lambda e: e.tensor_tensor(mx[:], mx[:], mx2[:], ALU.max),
                                after=[P.ctr["vector"]])
                    P.v(lambda e: e.nop(), after=[P.ctr["vector"]])
                    P.t(lambda e: e.transpose(ptrf[0:1, 0:128], mx[:], idf[:]),
                        after=[P.ctr["vector"]])
                    P.v(lambda e, qX=qX, T0=T0: e.tensor_scalar(
                        qX[64:65, T0:T0 + 128], ptrf[0:1, 0:128], -8.0, 0.0,
                        ALU.mult, ALU.add), after=[P.ctr["tensor"]])
                    for g4 in range(8):
                        for q4 in range(4):
                            cjj = g4 * 4 + q4
                            P.t(lambda e, kX=kX, qX=qX, T0=T0, cjj=cjj, q4=q4: e.matmul(
                                p2[:, q4 * 128:(q4 + 1) * 128],
                                kX[:, cjj * 128:cjj * 128 + 128],
                                qX[:, T0:T0 + 128], start=True, stop=True),
                                after=[P.ctr["vector"], P.ctr["scalar"]])
                        P.s(lambda e, g4=g4: e.activation(
                            pt[:, g4 * 512:(g4 + 1) * 512], p2[:], AF.Exp, scale=0.125),
                            after=[P.ctr["tensor"]])
                    for cjj in range(32):
                        P.t(lambda e, cjj=cjj, vX=vX: e.matmul(
                            po[:], pt[:, cjj * 128:(cjj + 1) * 128],
                            vX[:, cjj * 65:cjj * 65 + 65],
                            start=(cjj == 0), stop=(cjj == 31)),
                            after=[P.ctr["scalar"]] if cjj == 0 else [])
                    P.v(lambda e: e.reciprocal(rc[:], po[:, 64:65]), after=[P.ctr["tensor"]])
                    P.v(lambda e: e.tensor_scalar(ao[:], po[:, 0:64], rc[:], 0.0,
                                                  ALU.mult, ALU.add), after=[P.ctr["vector"]])
                    P.t(lambda e: e.transpose(ptr[0:64, :], ao[:], idb[:]),
                        after=[P.ctr["vector"]])
                    h = hp * 2 + hh
                    P.v(lambda e, h=h, T0=T0: e.tensor_copy(
                        att_fm[(h % 2) * 64:(h % 2) * 64 + 64,
                               (h // 2) * 2048 + T0:(h // 2) * 2048 + T0 + 128],
                        ptr[0:64, :]), after=[P.ctr["tensor"]])
        for i in range(16):
            T0 = i * 128
            P.ld(lambda e, T0=T0: e.dma_start(out=xt[:], in_=XF[QOFF0 + T0:QOFF0 + T0 + 128, :]),
                 after=ALL(P))
            for c in range(8):
                P.t(lambda e, c=c, T0=T0: e.matmul(
                    py0[:], att_fm[:, c * 2048 + T0:c * 2048 + T0 + 128],
                    wot[:, c * 1024:c * 1024 + 512],
                    start=(c == 0), stop=(c == 7)), after=[P.ctr["vector"]])
                P.t(lambda e, c=c, T0=T0: e.matmul(
                    py1[:], att_fm[:, c * 2048 + T0:c * 2048 + T0 + 128],
                    wot[:, c * 1024 + 512:c * 1024 + 1024],
                    start=(c == 0), stop=(c == 7)))
            yss = ys[:, (i % 2) * 1024:(i % 2) * 1024 + 1024]
            P.v(lambda e: e.nop(), after=[P.dma, P.ctr["tensor"], P.dmo])
            P.v(lambda e, yss=yss: e.tensor_tensor(yss[:, 0:512], py0[:], xt[:, 0:512], ALU.add))
            P.v(lambda e, yss=yss: e.tensor_tensor(yss[:, 512:1024], py1[:], xt[:, 512:1024],
                                                   ALU.add))
            P.v(lambda e, yss=yss: e.tensor_tensor(yss[:], yss[:], bo_bc[:], ALU.add),
                after=[P.ctr["vector"]])
            P.st(lambda e, T0=T0, yss=yss: e.dma_start(out=Y[T0:T0 + 128, :], in_=yss[:]),
                 after=[P.ctr["vector"]])
        P.finish()
        P.play(block)
    return nc


def run_mha(x, Wq, bq, Wk, bk, Wv, bv, Wo, bo):
    WQT = np.ascontiguousarray(Wq.reshape(1024, 1024).T)
    WKT = np.ascontiguousarray(Wk.reshape(1024, 1024).T)
    WVT = np.ascontiguousarray(Wv.reshape(1024, 1024).T)
    bqf = np.ascontiguousarray(bq.reshape(1024).reshape(8, 128).T)
    bkf = np.ascontiguousarray(bk.reshape(1024).reshape(8, 128).T)
    in_maps = []
    for c in range(8):
        b, half = c // 2, c % 2
        nc_name = "mha0" if half == 0 else "mha1"
        in_maps.append((nc_name, {"XF": np.ascontiguousarray(x[b]),
                        "WQT": WQT, "WKT": WKT, "WVT": WVT,
                        "WOT": np.ascontiguousarray(Wo.T),
                        "BQ": bqf, "BK": bkf,
                        "BVR": np.ascontiguousarray(bv.reshape(1, 1024)),
                        "BO": bo.reshape(1, 1024),
                        "IDB": ID128B, "IDF": ID128F}))
    y = np.zeros((8, 2048, 1024), np.float32)
    for name in ("mha0", "mha1"):
        ids = [c for c in range(8) if in_maps[c][0] == name]
        res = run_bass_kernel_spmd(get_nc(name), [in_maps[c][1] for c in ids], core_ids=ids)
        for k, c in enumerate(ids):
            y[c] = res.results[k]["Y"]
    return y.reshape(4, 4096, 1024)




def build_gla(dbg=False):
    nc = bass.Bass("TRN2", target_bir_lowering=False)
    XB = nc.dram_tensor("XB", [4096, 1024], F32, kind="ExternalInput").ap()
    UPT = nc.dram_tensor("UPT", [1024, 2048], F32, kind="ExternalInput").ap()
    GATT = nc.dram_tensor("GATT", [1024, 2048], F32, kind="ExternalInput").ap()
    DWT = nc.dram_tensor("DWT", [2048, 1024], F32, kind="ExternalInput").ap()
    QWT = nc.dram_tensor("QWT", [2048, 64], F32, kind="ExternalInput").ap()
    KWT = nc.dram_tensor("KWT", [2048, 64], F32, kind="ExternalInput").ap()
    GWT = nc.dram_tensor("GWT", [2048, 64], F32, kind="ExternalInput").ap()
    TWT = nc.dram_tensor("TWT", [2048, 64], F32, kind="ExternalInput").ap()
    IG1T = nc.dram_tensor("IG1T", [2048, 16], F32, kind="ExternalInput").ap()
    IG2T = nc.dram_tensor("IG2T", [16, 2048], F32, kind="ExternalInput").ap()
    CW = nc.dram_tensor("CW", [128, 64], F32, kind="ExternalInput").ap()
    CB = nc.dram_tensor("CB", [128, 16], F32, kind="ExternalInput").ap()
    IGB2 = nc.dram_tensor("IGB2", [128, 16], F32, kind="ExternalInput").ap()
    RW = nc.dram_tensor("RW", [128, 16], F32, kind="ExternalInput").ap()
    NW = nc.dram_tensor("NW", [128, 16], F32, kind="ExternalInput").ap()
    GB = nc.dram_tensor("GB", [64, 1], F32, kind="ExternalInput").ap()
    TB = nc.dram_tensor("TB", [64, 1], F32, kind="ExternalInput").ap()
    LNG = nc.dram_tensor("LNG", [1, 1024], F32, kind="ExternalInput").ap()
    LNB = nc.dram_tensor("LNB", [1, 1024], F32, kind="ExternalInput").ap()
    IDB = nc.dram_tensor("IDB", [128, 128], BF16, kind="ExternalInput").ap()
    IDF = nc.dram_tensor("IDF", [128, 128], F32, kind="ExternalInput").ap()
    Y = nc.dram_tensor("Y", [4096, 1024], F32, kind="ExternalOutput").ap()
    if dbg:
        DSH = nc.dram_tensor("DSH", [128, 4096], F32, kind="ExternalOutput").ap()
        DXU = nc.dram_tensor("DXU", [128, 4096], F32, kind="ExternalOutput").ap()
        DQ4 = nc.dram_tensor("DQ4", [64, 4 * 256], F32, kind="ExternalOutput").ap()
        DB = nc.dram_tensor("DB", [64, 256], F32, kind="ExternalOutput").ap()
        DOF = nc.dram_tensor("DOF", [128, 4096], F32, kind="ExternalOutput").ap()
        DXG = nc.dram_tensor("DXG", [128, 4096], F32, kind="ExternalOutput").ap()
        DMB = nc.dram_tensor("DMB", [1, 256], F32, kind="ExternalOutput").ap()
        DON = nc.dram_tensor("DON", [128, 4096], F32, kind="ExternalOutput").ap()
    NT = 16
    with ExitStack() as ctx:
        E = ctx.enter_context
        P = Prog(nc, ctx)
        upt = E(nc.sbuf_tensor("upt", [128, 8 * 2048], BF16))
        gatt = E(nc.sbuf_tensor("gatt", [128, 8 * 2048], BF16))
        dwt = E(nc.sbuf_tensor("dwt", [128, 16 * 1024], BF16))
        qw4 = E(nc.sbuf_tensor("qw4", [128, 16 * 256], BF16))
        ig1 = E(nc.sbuf_tensor("ig1", [128, 16 * 16], BF16))
        ig2 = E(nc.sbuf_tensor("ig2", [16, 2048], BF16))
        cw = E(nc.sbuf_tensor("cw", [128, 64], F32))
        cb = E(nc.sbuf_tensor("cb", [128, 16], F32))
        igb2 = E(nc.sbuf_tensor("igb2", [128, 16], F32))
        rw = E(nc.sbuf_tensor("rw", [128, 16], F32))
        nw = E(nc.sbuf_tensor("nw", [128, 16], F32))
        gb = E(nc.sbuf_tensor("gb", [64, 1], F32))
        tb = E(nc.sbuf_tensor("tb", [64, 1], F32))
        wld = E(nc.sbuf_tensor("wld", [128, 512], F32))
        x_fm = E(nc.sbuf_tensor("x_fm", [128, 8 * 256], BF16))
        xup = E(nc.sbuf_tensor("xup", [128, 16 * 259], BF16))
        tails = E(nc.sbuf_tensor("tails", [128, 16 * 3], BF16))
        shf = E(nc.sbuf_tensor("shf", [128, 16 * 256], BF16))
        gat = E(nc.sbuf_tensor("gat", [128, 16 * 256], BF16))
        ofm = E(nc.sbuf_tensor("ofm", [128, 16 * 256], BF16))
        vt = E(nc.sbuf_tensor("vt", [64, 2048], BF16))
        ovb = E(nc.sbuf_tensor("ovb", [64, 512], BF16))
        S = E(nc.sbuf_tensor("S", [64, 2048], F32))
        Sb = E(nc.sbuf_tensor("Sb", [64, 2048], BF16))
        qf = E(nc.sbuf_tensor("qf", [64, 256], F32))
        kf = E(nc.sbuf_tensor("kf", [64, 256], F32))
        gf = E(nc.sbuf_tensor("gf", [64, 256], F32))
        tf = E(nc.sbuf_tensor("tf", [64, 256], F32))
        b0 = E(nc.sbuf_tensor("b0", [64, 256], F32))
        b1 = E(nc.sbuf_tensor("b1", [64, 256], F32))
        qe = E(nc.sbuf_tensor("qe", [64, 256], BF16))
        ke = E(nc.sbuf_tensor("ke", [64, 256], BF16))
        keS = E(nc.sbuf_tensor("keS", [64, 64], BF16))
        keT = E(nc.sbuf_tensor("keT", [64, 64], BF16))
        ATm = E(nc.sbuf_tensor("ATm", [64, 64], BF16))
        m01f = E(nc.sbuf_tensor("m01f", [64, 64], F32))
        sc1 = E(nc.sbuf_tensor("sc1", [64, 256], F32))
        sc2 = E(nc.sbuf_tensor("sc2", [64, 256], F32))
        sc3 = E(nc.sbuf_tensor("sc3", [64, 256], F32))
        ebt = E(nc.sbuf_tensor("ebt", [64, 4], F32))
        pre1s = E(nc.sbuf_tensor("pre1s", [128, 16], F32))
        pre1t = E(nc.sbuf_tensor("pre1t", [16, 256], BF16))
        sig = E(nc.sbuf_tensor("sig", [128, 256], F32))
        mrow = E(nc.sbuf_tensor("mrow", [1, 256], F32))
        mbc = E(nc.sbuf_tensor("mbc", [128, 256], BF16))
        xt = E(nc.sbuf_tensor("xt", [128, 1024], F32))
        lnx = E(nc.sbuf_tensor("lnx", [128, 1024], F32))
        mcol = E(nc.sbuf_tensor("mcol", [128, 1], F32))
        ssq = E(nc.sbuf_tensor("ssq", [128, 1], F32))
        var = E(nc.sbuf_tensor("var", [128, 1], F32))
        rstd = E(nc.sbuf_tensor("rstd", [128, 1], F32))
        eps5 = E(nc.sbuf_tensor("eps5", [128, 1], F32))
        g_bc = E(nc.sbuf_tensor("g_bc", [128, 1024], F32))
        b_bc = E(nc.sbuf_tensor("b_bc", [128, 1024], F32))
        row1 = E(nc.sbuf_tensor("row1", [1, 1024], F32))
        ones1 = E(nc.sbuf_tensor("ones1", [1, 128], F32))
        onesc = E(nc.sbuf_tensor("onesc", [128, 1], F32))
        ioti = E(nc.sbuf_tensor("ioti", [64, 64], I32))
        ys = E(nc.sbuf_tensor("ys", [128, 1024], F32))
        yln = E(nc.sbuf_tensor("yln", [128, 1024], F32))
        idb = E(nc.sbuf_tensor("idb", [128, 128], BF16))
        idf = E(nc.sbuf_tensor("idf", [128, 128], F32))
        pA = E(nc.psum_tensor("pA", [128, 256], F32))
        po = E(nc.psum_tensor("po", [64, 512], F32))
        pkv = E(nc.psum_tensor("pkv", [64, 512], F32))
        ptr = E(nc.psum_tensor("ptr", [128, 128], BF16))
        ptrf = E(nc.psum_tensor("ptrf", [128, 256], F32))
        py0 = E(nc.psum_tensor("py0", [128, 512], F32))
        py1 = E(nc.psum_tensor("py1", [128, 512], F32))
        block = E(nc.Block())

        P.ld(lambda e: e.dma_start(out=idb[:], in_=IDB))
        P.ld(lambda e: e.dma_start(out=idf[:], in_=IDF))
        for dst, src in [(cw, CW), (cb, CB), (igb2, IGB2), (rw, RW), (nw, NW),
                         (gb, GB), (tb, TB)]:
            P.ld(lambda e, dst=dst, src=src: e.dma_start(out=dst[:], in_=src))
        P.v(lambda e: e.memset(eps5[:], 1e-5), after=[P.dma])
        P.v(lambda e: e.memset(ones1[:], 1.0))
        P.v(lambda e: e.memset(onesc[:], 1.0))
        P.v(lambda e: e.memset(S[:], 0.0))
        P.v(lambda e: e.memset(Sb[:], 0.0))
        P.v(lambda e: e.memset(tails[:], 0.0))
        P.g(lambda e: e.iota(ioti[:], [[1, 64]], base=0, channel_multiplier=-1), after=[P.dma])
        P.v(lambda e: e.tensor_copy(sc1[:, 0:64], ioti[:]), after=[P.ctr["gpsimd"]])
        P.v(lambda e: e.tensor_scalar(m01f[:], sc1[:, 0:64], 0.0, 0.0, ALU.is_ge, ALU.add),
            after=[P.ctr["vector"]])
        P.ld(lambda e: e.dma_start(out=row1[:], in_=LNG), after=[P.ctr["vector"]])
        P.t(lambda e: e.matmul(py0[:], ones1[:], row1[:, 0:512], start=True, stop=True),
            after=[P.ctr["vector"], P.dma])
        P.t(lambda e: e.matmul(py1[:], ones1[:], row1[:, 512:1024], start=True, stop=True))
        P.v(lambda e: e.tensor_copy(g_bc[:, 0:512], py0[:]), after=[P.ctr["tensor"]])
        P.v(lambda e: e.tensor_copy(g_bc[:, 512:1024], py1[:]))
        P.ld(lambda e: e.dma_start(out=row1[:], in_=LNB), after=[P.ctr["tensor"], P.ctr["vector"]])
        P.t(lambda e: e.matmul(py0[:], ones1[:], row1[:, 0:512], start=True, stop=True),
            after=[P.ctr["vector"], P.dma])
        P.t(lambda e: e.matmul(py1[:], ones1[:], row1[:, 512:1024], start=True, stop=True))
        P.v(lambda e: e.tensor_copy(b_bc[:, 0:512], py0[:]), after=[P.ctr["tensor"]])
        P.v(lambda e: e.tensor_copy(b_bc[:, 512:1024], py1[:]))
        for c in range(8):
            for hf in range(4):
                P.ld(lambda e, c=c, hf=hf: e.dma_start(
                    out=wld[:], in_=UPT[c * 128:(c + 1) * 128, hf * 512:(hf + 1) * 512]),
                    after=ALL(P))
                P.v(lambda e, c=c, hf=hf: e.tensor_copy(
                    upt[:, c * 2048 + hf * 512:c * 2048 + hf * 512 + 512], wld[:]),
                    after=[P.dma])
        for c in range(8):
            for hf in range(4):
                P.ld(lambda e, c=c, hf=hf: e.dma_start(
                    out=wld[:], in_=GATT[c * 128:(c + 1) * 128, hf * 512:(hf + 1) * 512]),
                    after=ALL(P))
                P.v(lambda e, c=c, hf=hf: e.tensor_copy(
                    gatt[:, c * 2048 + hf * 512:c * 2048 + hf * 512 + 512], wld[:]),
                    after=[P.dma])
        for c in range(16):
            for hf in range(2):
                P.ld(lambda e, c=c, hf=hf: e.dma_start(
                    out=wld[:], in_=DWT[c * 128:(c + 1) * 128, hf * 512:(hf + 1) * 512]),
                    after=ALL(P))
                P.v(lambda e, c=c, hf=hf: e.tensor_copy(
                    dwt[:, c * 1024 + hf * 512:c * 1024 + hf * 512 + 512], wld[:]),
                    after=[P.dma])
            P.v(lambda e, c=c: e.tensor_scalar(dwt[:, c * 1024:(c + 1) * 1024],
                                               dwt[:, c * 1024:(c + 1) * 1024],
                                               nw[:, c:c + 1], 0.0, ALU.mult, ALU.add),
                after=[P.ctr["vector"]])
        for wi, W in enumerate([QWT, KWT, GWT, TWT]):
            for c in range(16):
                P.ld(lambda e, W=W, c=c: e.dma_start(out=wld[:, 0:64],
                                                     in_=W[c * 128:(c + 1) * 128, :]),
                     after=ALL(P))
                P.v(lambda e, wi=wi, c=c: e.tensor_copy(
                    qw4[:, c * 256 + wi * 64:c * 256 + wi * 64 + 64], wld[:, 0:64]),
                    after=[P.dma])
        for c in range(16):
            P.ld(lambda e, c=c: e.dma_start(out=wld[:, 0:16], in_=IG1T[c * 128:(c + 1) * 128, :]),
                 after=ALL(P))
            P.v(lambda e, c=c: e.tensor_copy(ig1[:, c * 16:(c + 1) * 16], wld[:, 0:16]),
                after=[P.dma])
        for hf in range(4):
            P.ld(lambda e, hf=hf: e.dma_start(out=wld[0:16, :],
                                              in_=IG2T[:, hf * 512:(hf + 1) * 512]),
                 after=ALL(P))
            P.v(lambda e, hf=hf: e.tensor_copy(ig2[:, hf * 512:(hf + 1) * 512],
                                               wld[0:16, :]), after=[P.dma])

        for tt in range(NT):
            R0 = tt * 256
            for c in range(16):
                P.v(lambda e, c=c: e.tensor_copy(xup[:, c * 259:c * 259 + 3],
                                                 tails[:, c * 3:(c + 1) * 3]),
                    after=([P.ctr["tensor"], P.ctr["scalar"], P.ctr["vector"]] if c == 0 else []))
            for half in range(2):
                P.ld(lambda e, R0=R0, half=half: e.dma_start(
                    out=xt[:], in_=XB[R0 + half * 128:R0 + half * 128 + 128, :]), after=ALL(P))
                P.v(lambda e: e.nop(), after=[P.dma])
                emit_ln(P, lnx, xt, yln, mcol, ssq, var, rstd, eps5)
                for c8 in range(8):
                    P.t(lambda e, c8=c8: e.transpose(ptrf[:, 0:128],
                                                     lnx[:, c8 * 128:(c8 + 1) * 128], idf[:]),
                        after=[P.ctr["vector"]])
                    P.v(lambda e, c8=c8, half=half: e.tensor_copy(
                        x_fm[:, c8 * 256 + half * 128:c8 * 256 + half * 128 + 128],
                        ptrf[:, 0:128]), after=[P.ctr["tensor"]])
            for c in range(16):
                for d in range(8):
                    P.t(lambda e, c=c, d=d: e.matmul(
                        pA[:], upt[:, d * 2048 + c * 128:d * 2048 + c * 128 + 128],
                        x_fm[:, d * 256:(d + 1) * 256],
                        start=(d == 0), stop=(d == 7)), after=[P.ctr["vector"]])
                P.v(lambda e, c=c: e.tensor_copy(xup[:, c * 259 + 3:c * 259 + 259], pA[:]),
                    after=[P.ctr["tensor"]])
                for d in range(8):
                    P.t(lambda e, c=c, d=d: e.matmul(
                        pA[:], gatt[:, d * 2048 + c * 128:d * 2048 + c * 128 + 128],
                        x_fm[:, d * 256:(d + 1) * 256],
                        start=(d == 0), stop=(d == 7)), after=[P.ctr["vector"]])
                P.s(lambda e, c=c: e.activation(gat[:, c * 256:(c + 1) * 256], pA[:], AF.Silu),
                    after=[P.ctr["tensor"], P.ctr["vector"]])
                P.v(lambda e, c=c: e.tensor_scalar(sig[:], xup[:, c * 259:c * 259 + 256],
                                                   cw[:, c * 4:c * 4 + 1], 0.0,
                                                   ALU.mult, ALU.add),
                    after=[P.ctr["vector"], P.ctr["scalar"]])
                for j in range(1, 4):
                    P.v(lambda e, c=c, j=j: e.scalar_tensor_tensor(
                        sig[:], xup[:, c * 259 + j:c * 259 + j + 256],
                        cw[:, c * 4 + j:c * 4 + j + 1], sig[:], ALU.mult, ALU.add))
                P.s(lambda e, c=c: e.activation(shf[:, c * 256:(c + 1) * 256], sig[:], AF.Silu,
                                                bias=cb[:, c:c + 1]),
                    after=[P.ctr["vector"]])
                P.v(lambda e, c=c: e.tensor_copy(tails[:, c * 3:(c + 1) * 3],
                                                 xup[:, c * 259 + 256:c * 259 + 259]))
            for wi, dst in enumerate([qf, kf, gf, tf]):
                for c in range(16):
                    P.t(lambda e, wi=wi, c=c: e.matmul(
                        po[0:64, 0:256], qw4[:, c * 256 + wi * 64:c * 256 + wi * 64 + 64],
                        shf[:, c * 256:(c + 1) * 256],
                        start=(c == 0), stop=(c == 15)),
                        after=[P.ctr["vector"], P.ctr["scalar"]])
                if wi < 2:
                    P.v(lambda e, dst=dst: e.tensor_copy(dst[:], po[0:64, 0:256]),
                        after=[P.ctr["tensor"]])
                else:
                    bias = gb if wi == 2 else tb
                    P.v(lambda e, dst=dst, bias=bias: e.tensor_scalar(
                        dst[:], po[0:64, 0:256], bias[:], 0.0, ALU.add, ALU.add),
                        after=[P.ctr["tensor"]])
            P.s(lambda e: e.activation(sc1[:], gf[:], AF.Abs), after=[P.ctr["vector"]])
            P.s(lambda e: e.activation(sc1[:], sc1[:], AF.Exp, scale=-1.0))
            P.v(lambda e: e.tensor_scalar(sc1[:], sc1[:], 1.0, 0.0, ALU.add, ALU.add),
                after=[P.ctr["scalar"]])
            P.s(lambda e: e.activation(sc1[:], sc1[:], AF.Ln), after=[P.ctr["vector"]])
            P.s(lambda e: e.activation(sc2[:], gf[:], AF.Relu))
            P.v(lambda e: e.tensor_tensor(gf[:], sc1[:], sc2[:], ALU.add),
                after=[P.ctr["scalar"]])
            P.s(lambda e: e.activation(tf[:], tf[:], AF.Sigmoid), after=[P.ctr["vector"]])
            P.v(lambda e: e.tensor_scalar(sc1[:], gf[:], 1e-18, 0.0, ALU.max, ALU.add),
                after=[P.ctr["scalar"]])
            P.s(lambda e: e.activation(sc1[:], sc1[:], AF.Ln), after=[P.ctr["vector"]])
            P.v(lambda e: e.tensor_tensor(sc1[:], sc1[:], tf[:], ALU.mult),
                after=[P.ctr["scalar"]])
            P.s(lambda e: e.activation(sc1[:], sc1[:], AF.Exp), after=[P.ctr["vector"]])
            P.v(lambda e: e.tensor_tensor(b0[:], gf[:], tf[:], ALU.mult),
                after=[P.ctr["scalar"]])
            P.v(lambda e: e.tensor_scalar(b0[:], b0[:], -1.0, 0.0, ALU.mult, ALU.add),
                after=[P.ctr["vector"]])
            P.s(lambda e: e.activation(sc2[:], kf[:], AF.Square), after=[P.ctr["vector"]])
            P.t(lambda e: e.matmul(ptrf[0:1, 0:256], onesc[0:64, :], sc2[:],
                                   start=True, stop=True),
                after=[P.ctr["scalar"], P.ctr["vector"]])
            P.s(lambda e: e.activation(mrow[:], ptrf[0:1, 0:256], AF.Sqrt),
                after=[P.ctr["tensor"]])
            P.v(lambda e: e.tensor_scalar(mrow[:], mrow[:], 1e-12, 0.0, ALU.max, ALU.add),
                after=[P.ctr["scalar"]])
            P.v(lambda e: e.reciprocal(mrow[:], mrow[:]), after=[P.ctr["vector"]])
            P.v(lambda e: e.nop(), after=[P.ctr["vector"]])
            P.t(lambda e: e.matmul(ptrf[0:64, 0:256], ones1[:, 0:64], mrow[:],
                                   start=True, stop=True), after=[P.ctr["vector"]])
            P.v(lambda e: e.tensor_tensor(kf[:], kf[:], ptrf[0:64, 0:256], ALU.mult),
                after=[P.ctr["tensor"]])
            P.v(lambda e: e.tensor_tensor(kf[:], kf[:], sc1[:], ALU.mult),
                after=[P.ctr["vector"]])
            src_, dst_ = b0, b1
            for st in [1, 2, 4, 8, 16, 32]:
                s3 = src_[:].rearrange("k (c t) -> k c t", t=64)
                d3 = dst_[:].rearrange("k (c t) -> k c t", t=64)
                P.v(lambda e, s3=s3, d3=d3, st=st: e.tensor_copy(d3[:, :, 0:st],
                                                                 s3[:, :, 0:st]),
                    after=[P.ctr["vector"]])
                P.v(lambda e, s3=s3, d3=d3, st=st: e.tensor_tensor(
                    d3[:, :, st:64], s3[:, :, st:64], s3[:, :, 0:64 - st], ALU.add))
                src_, dst_ = dst_, src_
            bb = src_
            P.v(lambda e: e.nop(), after=[P.ctr["vector"]])
            P.s(lambda e, bb=bb: e.activation(sc2[:], bb[:], AF.Exp),
                after=[P.ctr["vector"], P.ctr["tensor"]])
            P.s(lambda e, bb=bb: e.activation(sc3[:], bb[:], AF.Exp, scale=-1.0))
            P.v(lambda e: e.tensor_scalar(qf[:], qf[:], 0.125, 0.0, ALU.mult, ALU.add),
                after=[P.ctr["scalar"]])
            P.v(lambda e: e.tensor_tensor(qe[:], qf[:], sc2[:], ALU.mult),
                after=[P.ctr["vector"]])
            P.v(lambda e: e.tensor_tensor(ke[:], kf[:], sc3[:], ALU.mult))
            b3 = bb[:].rearrange("k (c t) -> k c t", t=64)
            P.s(lambda e, b3=b3: e.activation(ebt[:], b3[:, :, 63], AF.Exp),
                after=[P.ctr["vector"]])
            for half in range(2):
                for c in range(16):
                    P.t(lambda e, half=half, c=c: e.matmul(
                        pA[0:128, 0:16],
                        xup[:, c * 259 + 3 + half * 128:c * 259 + 3 + half * 128 + 128],
                        ig1[:, c * 16:(c + 1) * 16],
                        start=(c == 0), stop=(c == 15)),
                        after=[P.ctr["vector"], P.ctr["scalar"]])
                P.v(lambda e: e.tensor_copy(pre1s[:], pA[0:128, 0:16]), after=[P.ctr["tensor"]])
                P.v(lambda e: e.nop(), after=[P.ctr["vector"]])
                P.t(lambda e: e.transpose(ptrf[0:16, 0:128], pre1s[:], idf[:]),
                    after=[P.ctr["vector"]])
                P.v(lambda e, half=half: e.tensor_copy(pre1t[:, half * 128:half * 128 + 128],
                                                       ptrf[0:16, 0:128]),
                    after=[P.ctr["tensor"]])
            for c in range(16):
                P.t(lambda e, c=c: e.matmul(pA[:], ig2[:, c * 128:(c + 1) * 128], pre1t[:],
                                            start=True, stop=True),
                    after=[P.ctr["vector"], P.ctr["scalar"]])
                P.s(lambda e, c=c: e.activation(sig[:], pA[:], AF.Sigmoid,
                                                bias=igb2[:, c:c + 1]),
                    after=[P.ctr["tensor"], P.ctr["vector"]])
                P.v(lambda e, c=c: e.tensor_tensor(xup[:, c * 259 + 3:c * 259 + 259],
                                                   sig[:], xup[:, c * 259 + 3:c * 259 + 259],
                                                   ALU.mult),
                    after=[P.ctr["scalar"]])
            for ch in range(4):
                C0 = ch * 64
                for c in range(16):
                    P.t(lambda e, c=c, C0=C0: e.transpose(
                        ptr[0:64, :], xup[:, c * 259 + 3 + C0:c * 259 + 3 + C0 + 64],
                        idb[:]), after=[P.ctr["vector"]])
                    P.v(lambda e, c=c: e.tensor_copy(vt[:, c * 128:(c + 1) * 128],
                                                     ptr[0:64, 0:128]),
                        after=[P.ctr["tensor"]])
                P.v(lambda e, C0=C0: e.tensor_copy(keS[:], ke[:, C0:C0 + 64]),
                    after=[P.ctr["vector"]])
                P.v(lambda e: e.nop(), after=[P.ctr["vector"]])
                P.t(lambda e: e.transpose(ptr[0:64, 0:64], keS[:], idb[0:64, 0:64]),
                    after=[P.ctr["vector"]])
                P.v(lambda e: e.tensor_copy(keT[:], ptr[0:64, 0:64]), after=[P.ctr["tensor"]])
                P.t(lambda e, C0=C0: e.matmul(ptrf[0:64, 0:64], ke[:, C0:C0 + 64],
                                              qe[:, C0:C0 + 64], start=True, stop=True),
                    after=[P.ctr["vector"]])
                P.v(lambda e: e.tensor_tensor(ATm[:], ptrf[0:64, 0:64], m01f[:], ALU.mult),
                    after=[P.ctr["tensor"]])
                P.v(lambda e: e.nop(), after=[P.ctr["vector"]])
                for vtl in range(4):
                    V0 = vtl * 512
                    P.t(lambda e, C0=C0, V0=V0: e.matmul(
                        po[:], qe[:, C0:C0 + 64], Sb[:, V0:V0 + 512],
                        start=True, stop=False), after=[P.ctr["vector"]])
                    P.t(lambda e, V0=V0: e.matmul(
                        po[:], ATm[:], vt[:, V0:V0 + 512], start=False, stop=True))
                    P.t(lambda e, C0=C0, V0=V0: e.matmul(
                        pkv[:], keT[:], vt[:, V0:V0 + 512], start=True, stop=True))
                    P.v(lambda e, V0=V0: e.tensor_copy(ovb[:], po[:]), after=[P.ctr["tensor"]])
                    P.v(lambda e, V0=V0: e.tensor_tensor(S[:, V0:V0 + 512], S[:, V0:V0 + 512],
                                                         pkv[:], ALU.add))
                    P.v(lambda e, V0=V0, ch=ch: e.tensor_scalar(
                        S[:, V0:V0 + 512], S[:, V0:V0 + 512], ebt[:, ch:ch + 1], 0.0,
                        ALU.mult, ALU.add), after=[P.ctr["vector"]])
                    P.v(lambda e, V0=V0: e.tensor_copy(Sb[:, V0:V0 + 512], S[:, V0:V0 + 512]),
                        after=[P.ctr["vector"]])
                    for q4 in range(4):
                        P.t(lambda e, q4=q4: e.transpose(
                            ptr[0:128, 0:64], ovb[:, q4 * 128:(q4 + 1) * 128],
                            idb[0:64, 0:64]), after=[P.ctr["vector"]])
                        P.v(lambda e, q4=q4, V0=V0, C0=C0: e.tensor_copy(
                            ofm[:, (V0 // 128 + q4) * 256 + C0:(V0 // 128 + q4) * 256 + C0 + 64],
                            ptr[0:128, 0:64]), after=[P.ctr["tensor"]])
            if dbg and tt == 0:
                P.barrier()
                for c in range(16):
                    P.v(lambda e, c=c: e.tensor_copy(lnx[:, 0:256], ofm[:, c * 256:(c + 1) * 256]),
                        after=[P.ctr["vector"], P.dmo])
                    P.st(lambda e, c=c: e.dma_start(out=DOF[:, c * 256:(c + 1) * 256],
                                                    in_=lnx[:, 0:256]), after=[P.ctr["vector"]])
                    P.v(lambda e, c=c: e.tensor_copy(lnx[:, 0:256], shf[:, c * 256:(c + 1) * 256]),
                        after=[P.ctr["vector"], P.dmo])
                    P.st(lambda e, c=c: e.dma_start(out=DSH[:, c * 256:(c + 1) * 256],
                                                    in_=lnx[:, 0:256]), after=[P.ctr["vector"]])
                    P.v(lambda e, c=c: e.tensor_copy(lnx[:, 0:256],
                                                     xup[:, c * 259 + 3:c * 259 + 259]),
                        after=[P.ctr["vector"], P.dmo])
                    P.st(lambda e, c=c: e.dma_start(out=DXU[:, c * 256:(c + 1) * 256],
                                                    in_=lnx[:, 0:256]), after=[P.ctr["vector"]])
                for wi, srcq in enumerate([qf, kf, gf, tf]):
                    P.v(lambda e, srcq=srcq: e.tensor_copy(sc2[:], srcq[:]),
                        after=[P.ctr["vector"], P.dmo])
                    P.st(lambda e, wi=wi: e.dma_start(out=DQ4[:, wi * 256:(wi + 1) * 256],
                                                      in_=sc2[:]), after=[P.ctr["vector"]])
                P.v(lambda e, bb=bb: e.tensor_copy(sc2[:], bb[:]),
                    after=[P.ctr["vector"], P.dmo])
                P.st(lambda e: e.dma_start(out=DB, in_=sc2[:]), after=[P.ctr["vector"]])
                P.barrier()
            for c in range(16):
                P.v(lambda e, c=c: e.scalar_tensor_tensor(
                    ofm[:, c * 256:(c + 1) * 256],
                    shf[:, c * 256:(c + 1) * 256], rw[:, c:c + 1],
                    ofm[:, c * 256:(c + 1) * 256], ALU.mult, ALU.add),
                    after=[P.ctr["vector"], P.ctr["tensor"]] if c == 0 else [])
                P.v(lambda e, c=c: e.tensor_tensor(ofm[:, c * 256:(c + 1) * 256],
                                                   ofm[:, c * 256:(c + 1) * 256],
                                                   gat[:, c * 256:(c + 1) * 256], ALU.mult))
            P.v(lambda e: e.nop(), after=[P.ctr["vector"]])
            if dbg and tt == 0:
                P.barrier()
                for c in range(16):
                    P.v(lambda e, c=c: e.tensor_copy(lnx[:, 0:256], ofm[:, c * 256:(c + 1) * 256]),
                        after=[P.ctr["vector"], P.dmo])
                    P.st(lambda e, c=c: e.dma_start(out=DXG[:, c * 256:(c + 1) * 256],
                                                    in_=lnx[:, 0:256]), after=[P.ctr["vector"]])
                P.barrier()
            for c in range(16):
                P.s(lambda e, c=c: e.activation(sig[:], ofm[:, c * 256:(c + 1) * 256],
                                                AF.Square),
                    after=[P.ctr["vector"], P.ctr["tensor"]])
                P.t(lambda e, c=c: e.matmul(ptrf[0:1, 0:256], onesc[:], sig[:],
                                            start=(c == 0), stop=(c == 15)),
                    after=[P.ctr["scalar"]])
            P.s(lambda e: e.activation(mrow[:], ptrf[0:1, 0:256], AF.Sqrt, scale=1.0 / 2048,
                                       bias=eps5[0:1, :]), after=[P.ctr["tensor"]])
            P.v(lambda e: e.reciprocal(mrow[:], mrow[:]), after=[P.ctr["scalar"]])
            P.v(lambda e: e.nop(), after=[P.ctr["vector"]])
            P.t(lambda e: e.matmul(ptrf[:, 0:256], ones1[:], mrow[:], start=True, stop=True),
                after=[P.ctr["vector"]])
            P.v(lambda e: e.tensor_copy(mbc[:], ptrf[:, 0:256]), after=[P.ctr["tensor"]])
            P.v(lambda e: e.nop(), after=[P.ctr["vector"]])
            for c in range(16):
                P.v(lambda e, c=c: e.tensor_tensor(ofm[:, c * 256:(c + 1) * 256],
                                                   ofm[:, c * 256:(c + 1) * 256],
                                                   mbc[:], ALU.mult))
            P.v(lambda e: e.nop(), after=[P.ctr["vector"]])
            if dbg and tt == 0:
                P.barrier()
                P.v(lambda e: e.tensor_copy(mrow[:], mrow[:]), after=[P.dmo])
                P.st(lambda e: e.dma_start(out=DMB, in_=mrow[:]), after=[P.ctr["vector"]])
                for c in range(16):
                    P.v(lambda e, c=c: e.tensor_copy(lnx[:, 0:256], ofm[:, c * 256:(c + 1) * 256]),
                        after=[P.ctr["vector"], P.dmo])
                    P.st(lambda e, c=c: e.dma_start(out=DON[:, c * 256:(c + 1) * 256],
                                                    in_=lnx[:, 0:256]), after=[P.ctr["vector"]])
                P.barrier()
            for half in range(2):
                for c in range(16):
                    P.t(lambda e, half=half, c=c: e.matmul(
                        py0[:], ofm[:, c * 256 + half * 128:c * 256 + half * 128 + 128],
                        dwt[:, c * 1024:c * 1024 + 512],
                        start=(c == 0), stop=(c == 15)), after=[P.ctr["vector"]])
                    P.t(lambda e, half=half, c=c: e.matmul(
                        py1[:], ofm[:, c * 256 + half * 128:c * 256 + half * 128 + 128],
                        dwt[:, c * 1024 + 512:c * 1024 + 1024],
                        start=(c == 0), stop=(c == 15)))
                P.ld(lambda e, R0=R0, half=half: e.dma_start(
                    out=xt[:], in_=XB[R0 + half * 128:R0 + half * 128 + 128, :]),
                    after=ALL(P))
                yss = ys[:, 0:1024]
                P.v(lambda e: e.nop(), after=[P.dma, P.ctr["tensor"], P.dmo])
                P.v(lambda e, yss=yss: e.tensor_tensor(yss[:, 0:512], py0[:], xt[:, 0:512],
                                                       ALU.add))
                P.v(lambda e, yss=yss: e.tensor_tensor(yss[:, 512:1024], py1[:],
                                                       xt[:, 512:1024], ALU.add))
                P.v(lambda e: e.nop(), after=[P.ctr["vector"]])
                emit_ln(P, yln, yss, lnx, mcol, ssq, var, rstd, eps5)
                P.v(lambda e: e.tensor_tensor(yln[:], yln[:], g_bc[:], ALU.mult),
                    after=[P.ctr["vector"]])
                P.v(lambda e: e.tensor_tensor(yln[:], yln[:], b_bc[:], ALU.add))
                P.v(lambda e, yss=yss: e.tensor_tensor(yss[:], yss[:], yln[:], ALU.add),
                    after=[P.ctr["vector"]])
                P.st(lambda e, R0=R0, half=half, yss=yss: e.dma_start(
                    out=Y[R0 + half * 128:R0 + half * 128 + 128, :], in_=yss[:]),
                    after=[P.ctr["vector"]])
        P.finish()
        P.play(block)
    return nc


def run_gla(x, d):
    def cs(a):
        return np.ascontiguousarray(a.astype(np.float32))
    pre = "L0_attn_b_"
    cw = d[pre + "conv_w"].reshape(16, 128, 4).transpose(1, 0, 2).reshape(128, 64)
    im = {"UPT": cs(d[pre + "up_w"].T), "GATT": cs(d[pre + "gate_w"].T),
          "DWT": cs(d[pre + "down_w"].T),
          "QWT": cs(d[pre + "q_w"].T), "KWT": cs(d[pre + "k_w"].T),
          "GWT": cs(d[pre + "g_w"].T), "TWT": cs(d[pre + "tau_w"].T),
          "IG1T": cs(d[pre + "ig_w1"].T), "IG2T": cs(d[pre + "ig_w2"].T),
          "CW": cs(cw), "CB": cs(d[pre + "conv_b"].reshape(16, 128).T),
          "IGB2": cs(d[pre + "ig_b2"].reshape(16, 128).T),
          "RW": cs(d[pre + "residual_weight"].reshape(16, 128).T),
          "NW": cs(d[pre + "norm_weight"].reshape(16, 128).T),
          "GB": cs(d[pre + "g_b"].reshape(64, 1)), "TB": cs(d[pre + "tau_b"].reshape(64, 1)),
          "LNG": cs(d["L0_ffn_ln_g"].reshape(1, 1024)),
          "LNB": cs(d["L0_ffn_ln_b"].reshape(1, 1024)),
          "IDB": ID128B, "IDF": ID128F}
    in_maps = []
    for c in range(8):
        m = dict(im)
        m["XB"] = np.ascontiguousarray(x[c // 2])
        in_maps.append(m)
    res = run_bass_kernel_spmd(get_nc("gla"), in_maps, core_ids=list(range(8)))
    return np.stack([res.results[2 * b]["Y"] for b in range(4)], 0)




def build_geg():
    nc = bass.Bass("TRN2", target_bir_lowering=False)
    XF = nc.dram_tensor("XF", [4096, 1024], F32, kind="ExternalInput").ap()
    XQ = nc.dram_tensor("XQ", [2048, 1024], F32, kind="ExternalInput").ap()
    WQT = nc.dram_tensor("WQT", [1024, 1024], F32, kind="ExternalInput").ap()
    WKT = nc.dram_tensor("WKT", [1024, 1024], F32, kind="ExternalInput").ap()
    WVT = nc.dram_tensor("WVT", [1024, 1024], F32, kind="ExternalInput").ap()
    WOT = nc.dram_tensor("WOT", [1024, 1024], F32, kind="ExternalInput").ap()
    NWT = nc.dram_tensor("NWT", [128, 8], F32, kind="ExternalInput").ap()
    PW = nc.dram_tensor("PW", [16, 9], F32, kind="ExternalInput").ap()
    HD = nc.dram_tensor("HD", [16, 1], F32, kind="ExternalInput").ap()
    GEGM = nc.dram_tensor("GEGM", [9, 9], F32, kind="ExternalInput").ap()
    QOFF = nc.dram_tensor("QOFF", [1, 16], F32, kind="ExternalInput").ap()
    IDB = nc.dram_tensor("IDB", [128, 128], BF16, kind="ExternalInput").ap()
    IDF = nc.dram_tensor("IDF", [128, 128], F32, kind="ExternalInput").ap()
    Y = nc.dram_tensor("Y", [2048, 1024], F32, kind="ExternalOutput").ap()
    CSC = nc.dram_tensor("CSC", [16, 9], F32, kind="Internal").ap()
    with ExitStack() as ctx:
        E = ctx.enter_context
        P = Prog(nc, ctx)
        x_fm = E(nc.sbuf_tensor("x_fm", [128, 8 * 4096], BF16))
        xq_fm = E(nc.sbuf_tensor("xq_fm", [128, 8 * 2048], BF16))
        att_fm = E(nc.sbuf_tensor("att_fm", [128, 8 * 2048], BF16))
        wot = E(nc.sbuf_tensor("wot", [128, 8 * 1024], BF16))
        wpr = E(nc.sbuf_tensor("wpr", [128, 3 * 1024], BF16))
        wld = E(nc.sbuf_tensor("wld", [128, 1024], F32))
        kP = E(nc.sbuf_tensor("kP", [128, 4096], BF16))
        qP = E(nc.sbuf_tensor("qP", [128, 2048], BF16))
        vP = E(nc.sbuf_tensor("vP", [128, 32 * 128], BF16))
        msks = E(nc.sbuf_tensor("msks", [128, 32 * 128], BF16))
        kraw = E(nc.sbuf_tensor("kraw", [128, 512], F32))
        sq = E(nc.sbuf_tensor("sq", [128, 512], F32))
        rn2 = E(nc.sbuf_tensor("rn2", [2, 512], F32))
        onesAB = E(nc.sbuf_tensor("onesAB", [128, 2], F32))
        selAB = E(nc.sbuf_tensor("selAB", [2, 128], F32))
        selI = E(nc.sbuf_tensor("selI", [2, 128], I32))
        sb = E(nc.sbuf_tensor("sb", [128, 512], BF16))
        acc = E(nc.sbuf_tensor("acc", [128, 512], BF16))
        cb = E(nc.sbuf_tensor("cb", [128, 144], F32))
        cwk = E(nc.sbuf_tensor("cwk", [16, 16], F32))
        qbc = E(nc.sbuf_tensor("qbc", [128, 4], F32))
        dvc = E(nc.sbuf_tensor("dvc", [128, 4], F32))
        ioc = E(nc.sbuf_tensor("ioc", [128, 1], I32))
        iocf = E(nc.sbuf_tensor("iocf", [128, 1], F32))
        ioi = E(nc.sbuf_tensor("ioi", [128, 128], I32))
        ioif = E(nc.sbuf_tensor("ioif", [128, 128], F32))
        ao = E(nc.sbuf_tensor("ao", [128, 64], F32))
        aob = E(nc.sbuf_tensor("aob", [128, 64], BF16))
        xt = E(nc.sbuf_tensor("xt", [128, 1024], F32))
        lnx = E(nc.sbuf_tensor("lnx", [128, 1024], F32))
        lnb_ = E(nc.sbuf_tensor("lnb_", [128, 512], BF16))
        mcol = E(nc.sbuf_tensor("mcol", [128, 1], F32))
        ssq = E(nc.sbuf_tensor("ssq", [128, 1], F32))
        var = E(nc.sbuf_tensor("var", [128, 1], F32))
        rstd = E(nc.sbuf_tensor("rstd", [128, 1], F32))
        eps5 = E(nc.sbuf_tensor("eps5", [128, 1], F32))
        eps6 = E(nc.sbuf_tensor("eps6", [128, 1], F32))
        nwc = E(nc.sbuf_tensor("nwc", [128, 8], F32))
        qoffs = E(nc.sbuf_tensor("qoffs", [1, 16], F32))
        ones1 = E(nc.sbuf_tensor("ones1", [1, 128], F32))
        ys = E(nc.sbuf_tensor("ys", [128, 1024], F32))
        idb = E(nc.sbuf_tensor("idb", [128, 128], BF16))
        idf = E(nc.sbuf_tensor("idf", [128, 128], F32))
        psA = E(nc.psum_tensor("psA", [128, 512], F32))
        pn = E(nc.psum_tensor("pn", [2, 512], F32))
        poQ = [E(nc.psum_tensor("poq%d" % i, [128, 64], F32)) for i in range(4)]
        ptr = E(nc.psum_tensor("ptr", [128, 128], BF16))
        ptrf = E(nc.psum_tensor("ptrf", [128, 512], F32))
        py0 = psA
        py1 = ptrf
        block = E(nc.Block())

        P.ld(lambda e: e.dma_start(out=idb[:], in_=IDB))
        P.ld(lambda e: e.dma_start(out=idf[:], in_=IDF))
        P.ld(lambda e: e.dma_start(out=nwc[:], in_=NWT))
        P.ld(lambda e: e.dma_start(out=qoffs[:], in_=QOFF))
        P.ld(lambda e: e.dma_start(out=cwk[0:16, 0:9], in_=PW))
        P.ld(lambda e: e.dma_start(out=cwk[0:16, 9:10], in_=HD))
        P.ld(lambda e: e.dma_start(out=wld[0:9, 0:9], in_=GEGM))
        P.v(lambda e: e.memset(eps5[:], 1e-5), after=[P.dma])
        P.v(lambda e: e.memset(eps6[:], 1e-6))
        P.v(lambda e: e.memset(ones1[:], 1.0))
        P.v(lambda e: e.memset(onesAB[:], 0.0))
        P.v(lambda e: e.memset(onesAB[0:64, 0:1], 1.0))
        P.v(lambda e: e.memset(onesAB[64:128, 1:2], 1.0))
        P.g(lambda e: e.iota(selI[:], [[1, 128]], base=0, channel_multiplier=-64),
            after=[P.dma])
        P.v(lambda e: e.tensor_copy(ioif[0:2, :], selI[:]), after=[P.ctr["gpsimd"]])
        P.v(lambda e: e.tensor_scalar(selAB[:], ioif[0:2, :], 0.0, 0.0, ALU.is_ge, ALU.add),
            after=[P.ctr["vector"]])
        P.v(lambda e: e.scalar_tensor_tensor(selAB[:], ioif[0:2, :], 63.0, selAB[:],
                                             ALU.is_le, ALU.mult),
            after=[P.ctr["vector"]])
        P.g(lambda e: e.iota(ioi[0:16, 0:9], [[1, 9]], base=0, channel_multiplier=0),
            after=[P.dma])
        P.v(lambda e: e.tensor_copy(ioif[0:16, 0:9], ioi[0:16, 0:9]), after=[P.ctr["gpsimd"]])
        P.v(lambda e: e.tensor_scalar(cwk[0:16, 10:11], cwk[0:16, 9:10], 1.0, 0.0,
                                      ALU.add, ALU.add), after=[P.ctr["vector"]])
        P.v(lambda e: e.tensor_scalar(ioif[0:16, 0:9], ioif[0:16, 0:9], -1.0,
                                      cwk[0:16, 10:11], ALU.mult, ALU.add),
            after=[P.ctr["vector"]])
        P.v(lambda e: e.tensor_scalar(ioif[0:16, 0:9], ioif[0:16, 0:9], 0.0, 1.0,
                                      ALU.max, ALU.min), after=[P.ctr["vector"]])
        P.v(lambda e: e.tensor_tensor(ioif[0:16, 0:9], ioif[0:16, 0:9], cwk[0:16, 0:9],
                                      ALU.mult), after=[P.ctr["vector"]])
        P.v(lambda e: e.nop(), after=[P.ctr["vector"]])
        P.t(lambda e: e.transpose(ptrf[0:9, 0:16], ioif[0:16, 0:9], idf[0:16, 0:16]),
            after=[P.ctr["vector"]])
        P.v(lambda e: e.tensor_copy(ioif[0:9, 16:32], ptrf[0:9, 0:16]), after=[P.ctr["tensor"]])
        P.v(lambda e: e.nop(), after=[P.ctr["vector"]])
        P.t(lambda e: e.matmul(ptrf[0:16, 0:9], ioif[0:9, 16:32], wld[0:9, 0:9],
                               start=True, stop=True), after=[P.ctr["vector"]])
        P.v(lambda e: e.tensor_copy(ioif[0:16, 32:41], ptrf[0:16, 0:9]), after=[P.ctr["tensor"]])
        P.st(lambda e: e.dma_start(out=CSC, in_=ioif[0:16, 32:41]),
             after=[P.ctr["vector"]])
        P.ld(lambda e: e.dma_start(out=wld[0:1, 0:144],
                                   in_=CSC.rearrange("(o h) m -> o (h m)", o=1)), after=[P.dmo])
        P.t(lambda e: e.matmul(ptrf[:, 0:144], ones1[:], wld[0:1, 0:144],
                               start=True, stop=True), after=[P.dma, P.ctr["vector"]])
        P.v(lambda e: e.tensor_copy(cb[:], ptrf[:, 0:144]), after=[P.ctr["tensor"]])
        for jt in range(32):
            P.ld(lambda e, jt=jt: e.dma_start(out=xt[:], in_=XF[jt * 128:(jt + 1) * 128, :]),
                 after=ALL(P))
            P.v(lambda e: e.nop(), after=[P.dma])
            emit_ln(P, lnx, xt, wld, mcol, ssq, var, rstd, eps5)
            for hf in range(2):
                P.v(lambda e, hf=hf: e.tensor_copy(lnb_[:], lnx[:, hf * 512:(hf + 1) * 512]),
                    after=[P.ctr["vector"], P.ctr["tensor"]])
                emit_tr(P, (lambda c, jt=jt, hf=hf: x_fm[:, (hf * 4 + c) * 4096 + jt * 128:
                                                         (hf * 4 + c) * 4096 + jt * 128 + 128]),
                        lnb_, ptr, idb, 4)
        for jt in range(16):
            P.ld(lambda e, jt=jt: e.dma_start(out=xt[:], in_=XQ[jt * 128:(jt + 1) * 128, :]),
                 after=ALL(P))
            P.v(lambda e: e.nop(), after=[P.dma])
            emit_ln(P, lnx, xt, wld, mcol, ssq, var, rstd, eps5)
            for hf in range(2):
                P.v(lambda e, hf=hf: e.tensor_copy(lnb_[:], lnx[:, hf * 512:(hf + 1) * 512]),
                    after=[P.ctr["vector"], P.ctr["tensor"]])
                emit_tr(P, (lambda c, jt=jt, hf=hf: xq_fm[:, (hf * 4 + c) * 2048 + jt * 128:
                                                          (hf * 4 + c) * 2048 + jt * 128 + 128]),
                        lnb_, ptr, idb, 4)
        for c in range(8):
            P.ld(lambda e, c=c: e.dma_start(out=wld[:], in_=WOT[c * 128:(c + 1) * 128, :]),
                 after=ALL(P))
            P.v(lambda e, c=c: e.tensor_copy(wot[:, c * 1024:(c + 1) * 1024], wld[:]),
                after=[P.dma])
            P.v(lambda e, c=c: e.tensor_scalar(wot[:, c * 1024:(c + 1) * 1024],
                                               wot[:, c * 1024:(c + 1) * 1024],
                                               nwc[:, c:c + 1], 0.0, ALU.mult, ALU.add),
                after=[P.ctr["vector"]])
        for hp in range(8):
            for wi, W in enumerate([WQT, WKT, WVT]):
                P.ld(lambda e, wi=wi, W=W, hp=hp: e.dma_start(
                    out=wld[:].rearrange("p (c n) -> p c n", n=128),
                    in_=W[:, hp * 128:(hp + 1) * 128].rearrange("(c p) n -> p c n", p=128)),
                    after=ALL(P))
                P.v(lambda e, wi=wi: e.tensor_copy(wpr[:, wi * 1024:(wi + 1) * 1024], wld[:]),
                    after=[P.dma])
            for which, n5, src_fm, dstP in [(1, 8, x_fm, kP), (0, 4, xq_fm, qP)]:
                W0 = 4096 if which == 1 else 2048
                for j5 in range(n5):
                    for c in range(8):
                        P.t(lambda e, which=which, j5=j5, c=c, src_fm=src_fm, W0=W0: e.matmul(
                            psA[:], wpr[:, which * 1024 + c * 128:which * 1024 + c * 128 + 128],
                            src_fm[:, c * W0 + j5 * 512: c * W0 + j5 * 512 + 512],
                            start=(c == 0), stop=(c == 7)), after=[P.ctr["vector"]])
                    P.v(lambda e: e.tensor_copy(kraw[:], psA[:]), after=[P.ctr["tensor"]])
                    P.s(lambda e: e.activation(sq[:], kraw[:], AF.Square),
                        after=[P.ctr["vector"]])
                    P.t(lambda e: e.matmul(pn[:], onesAB[:], sq[:], start=True, stop=True),
                        after=[P.ctr["scalar"]])
                    P.s(lambda e: e.activation(rn2[:], pn[:], AF.Sqrt), after=[P.ctr["tensor"]])
                    P.v(lambda e: e.tensor_scalar(rn2[:], rn2[:], 1e-12, 0.0, ALU.max, ALU.add),
                        after=[P.ctr["scalar"]])
                    P.v(lambda e: e.reciprocal(rn2[:], rn2[:]), after=[P.ctr["vector"]])
                    P.v(lambda e: e.nop(), after=[P.ctr["vector"]])
                    P.t(lambda e: e.matmul(ptrf[:], selAB[:], rn2[:],
                                           start=True, stop=True), after=[P.ctr["vector"]])
                    P.v(lambda e, dstP=dstP, j5=j5: e.tensor_tensor(
                        dstP[:, j5 * 512:(j5 + 1) * 512], kraw[:], ptrf[:], ALU.mult),
                        after=[P.ctr["tensor"]])
            for jt in range(32):
                for c in range(8):
                    P.t(lambda e, jt=jt, c=c: e.matmul(
                        psA[:, 0:128], x_fm[:, c * 4096 + jt * 128: c * 4096 + jt * 128 + 128],
                        wpr[:, 2048 + c * 128: 2048 + c * 128 + 128],
                        start=(c == 0), stop=(c == 7)), after=[P.ctr["vector"]])
                P.v(lambda e, jt=jt: e.tensor_copy(vP[:, jt * 128:(jt + 1) * 128],
                                                   psA[:, 0:128]), after=[P.ctr["tensor"]])
            for st in range(4):
                nch = 8 * st + 8
                for q4 in range(4):
                    P.t(lambda e, st=st, q4=q4: e.matmul(
                        ptrf[:, 0:1], ones1[:], qoffs[:, st * 4 + q4:st * 4 + q4 + 1],
                        start=True, stop=True), after=[P.ctr["vector"], P.ctr["scalar"]])
                    P.v(lambda e, q4=q4: e.tensor_copy(qbc[:, q4:q4 + 1], ptrf[:, 0:1]),
                        after=[P.ctr["tensor"]])
                P.g(lambda e: e.iota(ioc[:], [[0, 1]], base=1, channel_multiplier=1),
                    after=[P.ctr["vector"]])
                P.v(lambda e: e.tensor_copy(iocf[:], ioc[:]), after=[P.ctr["gpsimd"]])
                for q4 in range(4):
                    P.v(lambda e, q4=q4: e.scalar_tensor_tensor(
                        dvc[:, q4:q4 + 1], qbc[:, q4:q4 + 1], 1.0, iocf[:],
                        ALU.mult, ALU.add), after=[P.ctr["vector"]])
                P.v(lambda e: e.reciprocal(dvc[:], dvc[:]), after=[P.ctr["vector"]])
                for mi in range(8):
                    cjj = 8 * st + mi
                    P.g(lambda e, cjj=cjj: e.iota(ioi[:], [[-1, 128]], base=cjj * 128,
                                                  channel_multiplier=1),
                        after=[P.ctr["vector"]])
                    P.v(lambda e: e.tensor_copy(ioif[:], ioi[:]), after=[P.ctr["gpsimd"]])
                    P.v(lambda e: e.nop(), after=[P.ctr["vector"]])
                    for q4 in range(4):
                        P.v(lambda e, mi=mi, q4=q4: e.tensor_scalar(
                            msks[:, (mi * 4 + q4) * 128:(mi * 4 + q4) * 128 + 128],
                            ioif[:], qbc[:, q4:q4 + 1], 0.0, ALU.is_le, ALU.add),
                            after=[P.ctr["vector"]])
                for hh in range(2):
                    H0 = hh * 64
                    for cjj in range(nch):
                        P.t(lambda e, cjj=cjj, H0=H0, st=st: e.matmul(
                            psA[:], kP[H0:H0 + 64, cjj * 128:(cjj + 1) * 128],
                            qP[H0:H0 + 64, st * 512:(st + 1) * 512],
                            start=True, stop=True),
                            after=[P.ctr["vector"], P.ctr["scalar"]])
                        P.s(lambda e: e.activation(sb[:], psA[:], AF.Copy),
                            after=[P.ctr["tensor"], P.ctr["vector"]])
                        hb = (hp * 2 + hh) * 9
                        P.v(lambda e, hb=hb: e.tensor_scalar(
                            acc[:], sb[:], cb[:, hb + 8:hb + 9], 0.0, ALU.mult, ALU.add),
                            after=[P.ctr["scalar"], P.ctr["tensor"]])
                        for m in range(7, 0, -1):
                            P.v(lambda e, hb=hb, m=m: e.scalar_tensor_tensor(
                                acc[:], acc[:], cb[:, hb + m:hb + m + 1], sb[:],
                                ALU.add, ALU.mult))
                        P.v(lambda e, hb=hb: e.tensor_scalar(
                            acc[:], acc[:], cb[:, hb:hb + 1], 0.0, ALU.add, ALU.add))
                        if cjj >= 8 * st:
                            mi = cjj - 8 * st
                            for q4 in range(4):
                                P.v(lambda e, mi=mi, q4=q4: e.tensor_tensor(
                                    acc[:, q4 * 128:(q4 + 1) * 128],
                                    acc[:, q4 * 128:(q4 + 1) * 128],
                                    msks[:, (mi * 4 + q4) * 128:(mi * 4 + q4) * 128 + 128],
                                    ALU.mult))
                        P.v(lambda e: e.nop(), after=[P.ctr["vector"]])
                        for q4 in range(4):
                            P.t(lambda e, cjj=cjj, q4=q4, H0=H0, nch=nch: e.matmul(
                                poQ[q4][:],
                                acc[:, q4 * 128:(q4 + 1) * 128],
                                vP[:, cjj * 128 + H0:cjj * 128 + H0 + 64],
                                start=(cjj == 0), stop=(cjj == nch - 1)),
                                after=[P.ctr["vector"]] if q4 == 0 else [])
                    for q4 in range(4):
                        P.v(lambda e, q4=q4: e.tensor_scalar(
                            ao[:], poQ[q4][:], dvc[:, q4:q4 + 1], 0.0,
                            ALU.mult, ALU.add), after=[P.ctr["tensor"], P.ctr["vector"]])
                        P.s(lambda e: e.activation(sq[:, 0:64], ao[:], AF.Square,
                                                   accum_out=ssq[:]), after=[P.ctr["vector"]])
                        P.s(lambda e: e.activation(rstd[:], ssq[:], AF.Sqrt, scale=1.0 / 64,
                                                   bias=eps6[:]))
                        P.v(lambda e: e.reciprocal(rstd[:], rstd[:]), after=[P.ctr["scalar"]])
                        P.v(lambda e: e.tensor_scalar(aob[:], ao[:], rstd[:], 0.0,
                                                      ALU.mult, ALU.add),
                            after=[P.ctr["vector"]])
                        P.v(lambda e: e.nop(), after=[P.ctr["vector"]])
                        P.t(lambda e: e.transpose(ptr[0:64, :], aob[:], idb[:]),
                            after=[P.ctr["vector"]])
                        h = hp * 2 + hh
                        i = st * 4 + q4
                        P.v(lambda e, h=h, i=i: e.tensor_copy(
                            att_fm[(h % 2) * 64:(h % 2) * 64 + 64,
                                   (h // 2) * 2048 + i * 128:(h // 2) * 2048 + i * 128 + 128],
                            ptr[0:64, :]), after=[P.ctr["tensor"]])
        for i in range(16):
            T0 = i * 128
            P.ld(lambda e, T0=T0: e.dma_start(out=xt[:], in_=XQ[T0:T0 + 128, :]),
                 after=ALL(P))
            for c in range(8):
                P.t(lambda e, c=c, T0=T0: e.matmul(
                    py0[:], att_fm[:, c * 2048 + T0:c * 2048 + T0 + 128],
                    wot[:, c * 1024:c * 1024 + 512],
                    start=(c == 0), stop=(c == 7)), after=[P.ctr["vector"]])
                P.t(lambda e, c=c, T0=T0: e.matmul(
                    py1[:], att_fm[:, c * 2048 + T0:c * 2048 + T0 + 128],
                    wot[:, c * 1024 + 512:c * 1024 + 1024],
                    start=(c == 0), stop=(c == 7)))
            yss = ys[:, 0:1024]
            P.v(lambda e: e.nop(), after=[P.dma, P.ctr["tensor"], P.dmo])
            P.v(lambda e, yss=yss: e.tensor_tensor(yss[:, 0:512], py0[:], xt[:, 0:512], ALU.add))
            P.v(lambda e, yss=yss: e.tensor_tensor(yss[:, 512:1024], py1[:], xt[:, 512:1024],
                                                   ALU.add))
            P.st(lambda e, T0=T0, yss=yss: e.dma_start(out=Y[T0:T0 + 128, :], in_=yss[:]),
                 after=[P.ctr["vector"]])
        P.finish()
        P.play(block)
    return nc


def _geg_monomial_const():
    LAM = (64 - 1) / 2.0
    M = np.zeros((9, 9), np.float64)
    M[0, 0] = 1.0
    M[1, 1] = 1.0
    Rp = M[0].copy()
    Rc = M[1].copy()
    for k in range(2, 9):
        c1 = 2.0 * (k + LAM - 1.0) / (k + 2.0 * LAM - 1.0)
        c2 = (k - 1.0) / (k + 2.0 * LAM - 1.0)
        Rn = c1 * np.roll(Rc, 1) - c2 * Rp
        Rn[0] = -c2 * Rp[0]
        M[k] = Rn
        Rp, Rc = Rc, Rn
    return M.astype(np.float32)


def run_geg(x, Wq, Wk, Wv, Wo, poly_weights, norm_w, head_degrees):
    GEGM = _geg_monomial_const()
    in_maps = []
    for c in range(8):
        b, half = c // 2, c % 2
        tiles = list(range(half, 32, 2))
        xq = np.concatenate([x[b, t * 128:(t + 1) * 128] for t in tiles], 0)
        qoff = np.array([[t * 128 for t in tiles]], np.float32)
        in_maps.append({"XF": np.ascontiguousarray(x[b]), "XQ": np.ascontiguousarray(xq),
                        "WQT": np.ascontiguousarray(Wq.T), "WKT": np.ascontiguousarray(Wk.T),
                        "WVT": np.ascontiguousarray(Wv.T), "WOT": np.ascontiguousarray(Wo.T),
                        "NWT": np.ascontiguousarray(np.tile(norm_w, 16).reshape(8, 128).T),
                        "PW": np.ascontiguousarray(poly_weights),
                        "HD": np.ascontiguousarray(head_degrees.reshape(16, 1)),
                        "GEGM": GEGM, "QOFF": qoff, "IDB": ID128B, "IDF": ID128F})
    res = run_bass_kernel_spmd(get_nc("geg"), in_maps, core_ids=list(range(8)))
    y = np.zeros((4, 4096, 1024), np.float32)
    for c in range(8):
        b, half = c // 2, c % 2
        out = res.results[c]["Y"]
        for li, t in enumerate(range(half, 32, 2)):
            y[b, t * 128:(t + 1) * 128] = out[li * 128:(li + 1) * 128]
    return y


_NCACHE = {}


def get_nc(name):
    if name not in _NCACHE:
        _NCACHE[name] = {"ffn": build_ffn, "attn_a": build_attn_a, "attn_a_dbg": (lambda: build_attn_a(True)),
                      "mha0": (lambda: build_mha(0)), "mha1": (lambda: build_mha(2048)),
                      "gla": build_gla, "gla_dbg": (lambda: build_gla(True)), "geg": build_geg}[name]()
    return _NCACHE[name]


def run_ffn(x, w_gate, w_up, w_down):
    nc = get_nc("ffn")
    xs = np.ascontiguousarray(x.reshape(16384, 1024))
    wg = np.ascontiguousarray(w_gate)
    wu = np.ascontiguousarray(w_up)
    wd = np.ascontiguousarray(w_down)
    in_maps = []
    for c in range(8):
        in_maps.append({"X": xs[c * 2048:(c + 1) * 2048], "WG": wg, "WU": wu,
                        "WD": wd, "IDB": ID128B})
    res = run_bass_kernel_spmd(nc, in_maps, core_ids=list(range(8)))
    y = np.concatenate([res.results[c]["Y"] for c in range(8)], 0)
    return y.reshape(4, 4096, 1024)


def kernel(**inputs):
    d = inputs
    x = d["L0_attn_a_x"]
    x = run_attn_a(x, d["L0_attn_a_Wq"], d["L0_attn_a_Wk"], d["L0_attn_a_Wv"],
                   d["L0_attn_a_Wo"], d["L0_ffn_ln_g"], d["L0_ffn_ln_b"])
    x = run_gla(x, d)
    x = run_geg(x, d["L1_attn_a_Wq"], d["L1_attn_a_Wk"], d["L1_attn_a_Wv"],
                d["L1_attn_a_Wo"], d["L1_attn_a_poly_weights"], d["L1_attn_a_norm_w"],
                d["L1_attn_a_head_degrees"])
    x = run_ffn(x, d["L1_ffn_w_gate"], d["L1_ffn_w_up"], d["L1_ffn_w_down"])
    x = run_mha(x, d["L1_attn_b_Wq"], d["L1_attn_b_bq"], d["L1_attn_b_Wk"],
                d["L1_attn_b_bk"], d["L1_attn_b_Wv"], d["L1_attn_b_bv"],
                d["L1_attn_b_Wo"], d["L1_attn_b_bo"])
    x = run_ffn(x, d["L1_ffn_w_gate"], d["L1_ffn_w_up"], d["L1_ffn_w_down"])
    return x


# version 03
# speedup vs baseline: 1.2192x; 1.2192x over previous
import numpy as np
import ml_dtypes
from contextlib import ExitStack
import concourse.bass as bass
import concourse.mybir as mybir
from concourse.bass_utils import run_bass_kernel_spmd

F32 = mybir.dt.float32
BF16 = mybir.dt.bfloat16
U8 = mybir.dt.uint8
I32 = mybir.dt.int32
AF = mybir.ActivationFunctionType
ALU = mybir.AluOpType
AX = mybir.AxisListType

NEG = -1e30
BF = ml_dtypes.bfloat16
ID128B = np.eye(128, dtype=BF)
ID128F = np.eye(128, dtype=np.float32)


class Ctr:
    def __init__(self, h, step=1):
        self.h = h
        self.n = 0
        self.step = step

    def inc(self):
        self.n += self.step
        return (self.h, self.step)

    def at(self):
        return (self.h, self.n)


class Prog:
    ENGS = ("sync", "tensor", "vector", "scalar", "gpsimd")

    def __init__(self, nc, ctx):
        self.nc = nc
        self.ops = {e: [] for e in self.ENGS}
        self.ctr = {}
        for e in self.ENGS:
            h = ctx.enter_context(nc.semaphore("sem_" + e))
            self.ctr[e] = Ctr(h, 1)
        self.dma = Ctr(ctx.enter_context(nc.semaphore("sem_dma")), 16)
        self.dmo = Ctr(ctx.enter_context(nc.semaphore("sem_dmo")), 16)

    def emit(self, eng, fn, after=(), ctr=None):
        waits = [c.at() for c in after]
        c = ctr if ctr is not None else self.ctr[eng]
        inc = c.inc()
        self.ops[eng].append((fn, waits, inc))
        return c

    def t(self, fn, after=()):
        return self.emit("tensor", fn, after)

    def v(self, fn, after=()):
        return self.emit("vector", fn, after)

    def s(self, fn, after=()):
        return self.emit("scalar", fn, after)

    def g(self, fn, after=()):
        return self.emit("gpsimd", fn, after)

    def ld(self, fn, after=()):
        return self.emit("sync", fn, after, ctr=self.dma)

    def st(self, fn, after=()):
        return self.emit("sync", fn, after, ctr=self.dmo)

    def barrier(self):
        marks = [self.ctr[e].at() for e in self.ENGS] + [self.dma.at(), self.dmo.at()]
        for e in self.ENGS:
            waits = [m for m in marks if m[0] is not self.ctr[e].h]

            def fn(eng):
                return eng.nop()
            self.ops[e].append((fn, waits, self.ctr[e].inc()))

    def finish(self):
        self.ops["sync"].append((lambda e: None, [self.dmo.at()], None))

    def play(self, block):
        for e in self.ENGS:
            deco = getattr(block, e)
            ops = self.ops[e]

            def make(ops):
                def body(eng):
                    for fn, waits, inc in ops:
                        for h, v in waits:
                            eng.wait_ge(h, v)
                        inst = fn(eng)
                        if inst is not None and inc is not None:
                            inst.then_inc(*inc)
                return body
            deco(make(ops))


def ALL(P):
    return [P.ctr[e] for e in ("tensor", "vector", "scalar", "gpsimd")] + [P.dma]


def emit_ln(P, out, x, dump, mcol, ssq, var, rstd, eps_col, n=1024):
    P.s(lambda e: e.activation(dump[:], x[:], AF.Square, accum_out=ssq[:]),
        after=[P.ctr["vector"]])
    P.v(lambda e: e.tensor_reduce(mcol[:], x[:], AX.X, ALU.add))
    P.v(lambda e: e.tensor_scalar(mcol[:], mcol[:], 1.0 / n, 0.0, ALU.mult, ALU.add),
        after=[P.ctr["vector"]])
    P.v(lambda e: e.tensor_tensor(var[:], mcol[:], mcol[:], ALU.mult),
        after=[P.ctr["scalar"], P.ctr["vector"]])
    P.v(lambda e: e.scalar_tensor_tensor(var[:], ssq[:], 1.0 / n, var[:], ALU.mult, ALU.subtract),
        after=[P.ctr["vector"]])
    P.v(lambda e: e.tensor_scalar(var[:], var[:], 0.0, 0.0, ALU.max, ALU.add),
        after=[P.ctr["vector"]])
    P.s(lambda e: e.activation(rstd[:], var[:], AF.Sqrt, bias=eps_col[:]), after=[P.ctr["vector"]])
    P.v(lambda e: e.reciprocal(rstd[:], rstd[:]), after=[P.ctr["scalar"]])
    P.v(lambda e: e.tensor_scalar(out[:], x[:], mcol[:], 0.0, ALU.subtract, ALU.add),
        after=[P.ctr["vector"]])
    P.v(lambda e: e.tensor_scalar(out[:], out[:], rstd[:], 0.0, ALU.mult, ALU.add))


def emit_tr(P, dst, src, ptr, idn, ncols, after=()):
    for c in range(ncols):
        P.t(lambda e, c=c: e.transpose(ptr[:], src[:, c * 128:(c + 1) * 128], idn[:]),
            after=list(after) + [P.ctr["vector"]])
        P.v(lambda e, c=c: e.tensor_copy(dst(c), ptr[:]), after=[P.ctr["tensor"]])


def emit_quant(P, out_bf, x, a, u, sc, nb):
    x3 = x.rearrange("p (b s) -> p b s", s=16)
    u3 = u.rearrange("p (b s) -> p b s", s=16)
    P.v(lambda e: e.tensor_reduce(sc[:, 0:nb], x3, AX.X, ALU.max, apply_absolute_value=True),
        after=[P.ctr["scalar"]])
    P.v(lambda e: e.tensor_scalar(sc[:, 0:nb], sc[:, 0:nb], 1.0 / 6.0, 1e-12, ALU.mult, ALU.max),
        after=[P.ctr["vector"]])
    P.v(lambda e: e.reciprocal(sc[:, nb:2 * nb], sc[:, 0:nb]), after=[P.ctr["vector"]])
    P.v(lambda e: e.nop(), after=[P.ctr["vector"]])
    for j in range(16):
        P.v(lambda e, j=j: e.tensor_copy(u3[:, :, j], sc[:, nb:2 * nb]))
    P.s(lambda e: e.activation(a, x, AF.Abs), after=[P.ctr["vector"]])
    P.v(lambda e: e.tensor_tensor(a, a, u, ALU.mult), after=[P.ctr["scalar"], P.ctr["vector"]])
    P.v(lambda e: e.tensor_scalar(u, a, 5.0, 2.0, ALU.is_gt, ALU.mult))
    P.v(lambda e: e.scalar_tensor_tensor(u, a, 3.5, u, ALU.is_gt, ALU.add))
    P.v(lambda e: e.scalar_tensor_tensor(u, a, 2.5, u, ALU.is_gt, ALU.add))
    P.v(lambda e: e.tensor_scalar(u, u, 2.0, 0.0, ALU.mult, ALU.add))
    P.v(lambda e: e.scalar_tensor_tensor(u, a, 1.75, u, ALU.is_gt, ALU.add))
    P.v(lambda e: e.scalar_tensor_tensor(u, a, 1.25, u, ALU.is_gt, ALU.add))
    P.v(lambda e: e.scalar_tensor_tensor(u, a, 0.75, u, ALU.is_gt, ALU.add))
    P.v(lambda e: e.scalar_tensor_tensor(u, a, 0.25, u, ALU.is_gt, ALU.add))
    P.s(lambda e: e.activation(a, x, AF.Sign), after=[P.ctr["vector"]])
    P.v(lambda e: e.scalar_tensor_tensor(a, a, 0.5, u, ALU.mult, ALU.mult),
        after=[P.ctr["scalar"]])
    P.v(lambda e: e.nop(), after=[P.ctr["vector"]])
    for j in range(16):
        P.v(lambda e, j=j: e.tensor_copy(u3[:, :, j], sc[:, 0:nb]))
    P.v(lambda e: e.tensor_tensor(out_bf, a, u, ALU.mult), after=[P.ctr["vector"]])


def build_ffn():
    nc = bass.Bass("TRN2", target_bir_lowering=False)
    X = nc.dram_tensor("X", [2048, 1024], F32, kind="ExternalInput").ap()
    WG = nc.dram_tensor("WG", [2816, 1024], F32, kind="ExternalInput").ap()
    WU = nc.dram_tensor("WU", [2816, 1024], F32, kind="ExternalInput").ap()
    WD = nc.dram_tensor("WD", [1024, 2816], F32, kind="ExternalInput").ap()
    IDB = nc.dram_tensor("IDB", [128, 128], BF16, kind="ExternalInput").ap()
    Y = nc.dram_tensor("Y", [2048, 1024], F32, kind="ExternalOutput").ap()
    SEC = [(i * 512, min(512, 2816 - i * 512)) for i in range(6)]
    with ExitStack() as ctx:
        E = ctx.enter_context
        P = Prog(nc, ctx)
        wgq = E(nc.sbuf_tensor("wgq", [128, 8 * 2816], BF16))
        wuq = E(nc.sbuf_tensor("wuq", [128, 8 * 2816], BF16))
        wdq = E(nc.sbuf_tensor("wdq", [128, 22 * 1024], BF16))
        idb = E(nc.sbuf_tensor("idb", [128, 128], BF16))
        wt = E(nc.sbuf_tensor("wt", [128, 2816], F32))
        qa = E(nc.sbuf_tensor("qa", [128, 2816], F32))
        qu = E(nc.sbuf_tensor("qu", [128, 2816], F32))
        qs = E(nc.sbuf_tensor("qs", [128, 2 * 176], F32))
        qt = E(nc.sbuf_tensor("qt", [128, 2816], BF16))
        xf = E(nc.sbuf_tensor("xf", [128, 1024], F32))
        h = E(nc.sbuf_tensor("h", [128, 1024], F32))
        hq = E(nc.sbuf_tensor("hq", [128, 1024], BF16))
        hqf = E(nc.sbuf_tensor("hqf", [128, 8 * 128], BF16))
        sg = E(nc.sbuf_tensor("sg", [128, 512], F32))
        mid = E(nc.sbuf_tensor("mid", [128, 512], F32))
        midq = E(nc.sbuf_tensor("midq", [128, 512], BF16))
        midqf = E(nc.sbuf_tensor("midqf", [128, 4 * 128], BF16))
        ys = E(nc.sbuf_tensor("ys", [128, 1024], F32))
        mcol = E(nc.sbuf_tensor("mcol", [128, 1], F32))
        ssq = E(nc.sbuf_tensor("ssq", [128, 1], F32))
        var = E(nc.sbuf_tensor("var", [128, 1], F32))
        rstd = E(nc.sbuf_tensor("rstd", [128, 1], F32))
        eps5 = E(nc.sbuf_tensor("eps5", [128, 1], F32))
        ptr = E(nc.psum_tensor("ptr", [128, 128], BF16))
        pg = E(nc.psum_tensor("pg", [128, 512], F32))
        pu = E(nc.psum_tensor("pu", [128, 512], F32))
        py0 = E(nc.psum_tensor("py0", [128, 512], F32))
        py1 = E(nc.psum_tensor("py1", [128, 512], F32))
        block = E(nc.Block())

        P.ld(lambda e: e.dma_start(out=idb[:], in_=IDB))
        P.v(lambda e: e.memset(eps5[:], 1e-5))
        for wi, (W, dst, ntile, K, fmcols) in enumerate([
                (WG, wgq, 22, 1024, 2816), (WU, wuq, 22, 1024, 2816), (WD, wdq, 8, 2816, 1024)]):
            for r in range(ntile):
                P.ld(lambda e, W=W, r=r, K=K: e.dma_start(out=wt[:, 0:K], in_=W[r * 128:(r + 1) * 128, :]),
                     after=ALL(P))
                P.v(lambda e: e.nop(), after=[P.dma])
                emit_quant(P, qt[:, 0:K], wt[:, 0:K], qa[:, 0:K], qu[:, 0:K], qs[:], K // 16)
                emit_tr(P, (lambda c, dst=dst, r=r, fmcols=fmcols:
                            dst[:, c * fmcols + r * 128: c * fmcols + r * 128 + 128]),
                        qt, ptr, idb, K // 128)
        P.barrier()
        for i in range(16):
            P.ld(lambda e, i=i: e.dma_start(out=xf[:], in_=X[i * 128:(i + 1) * 128, :]),
                 after=ALL(P))
            P.v(lambda e: e.nop(), after=[P.dma])
            emit_ln(P, h, xf, qa[:, 0:1024], mcol, ssq, var, rstd, eps5)
            emit_quant(P, hq[:], h[:], qa[:, 0:1024], qu[:, 0:1024], qs[:], 64)
            emit_tr(P, (lambda c: hqf[:, c * 128:(c + 1) * 128]), hq, ptr, idb, 8)
            for si, (s0, sn) in enumerate(SEC):
                tmark = [P.ctr["vector"], P.ctr["scalar"]]
                for c in range(8):
                    P.t(lambda e, c=c, s0=s0, sn=sn: e.matmul(
                        pg[:, 0:sn], hqf[:, c * 128:(c + 1) * 128],
                        wgq[:, c * 2816 + s0:c * 2816 + s0 + sn],
                        start=(c == 0), stop=(c == 7)), after=tmark)
                for c in range(8):
                    P.t(lambda e, c=c, s0=s0, sn=sn: e.matmul(
                        pu[:, 0:sn], hqf[:, c * 128:(c + 1) * 128],
                        wuq[:, c * 2816 + s0:c * 2816 + s0 + sn],
                        start=(c == 0), stop=(c == 7)), after=tmark)
                P.s(lambda e, sn=sn: e.activation(sg[:, 0:sn], pg[:, 0:sn], AF.Silu),
                    after=[P.ctr["tensor"]])
                P.v(lambda e, sn=sn: e.tensor_tensor(mid[:, 0:sn], sg[:, 0:sn], pu[:, 0:sn],
                                                     ALU.mult),
                    after=[P.ctr["scalar"], P.ctr["tensor"]])
                emit_quant(P, midq[:, 0:sn], mid[:, 0:sn], qa[:, 0:sn], qu[:, 0:sn],
                           qs[:], sn // 16)
                emit_tr(P, (lambda c: midqf[:, c * 128:(c + 1) * 128]), midq, ptr, idb,
                        sn // 128)
                for c4 in range(sn // 128):
                    cg = si * 4 + c4
                    st = (si == 0 and c4 == 0)
                    sp = (si == 5 and c4 == (sn // 128) - 1)
                    P.t(lambda e, c4=c4, cg=cg, st=st, sp=sp: e.matmul(
                        py0[:], midqf[:, c4 * 128:(c4 + 1) * 128],
                        wdq[:, cg * 1024:cg * 1024 + 512],
                        start=st, stop=sp), after=[P.ctr["vector"]])
                    P.t(lambda e, c4=c4, cg=cg, st=st, sp=sp: e.matmul(
                        py1[:], midqf[:, c4 * 128:(c4 + 1) * 128],
                        wdq[:, cg * 1024 + 512:cg * 1024 + 1024],
                        start=st, stop=sp))
            P.v(lambda e: e.tensor_tensor(ys[:, 0:512], py0[:], xf[:, 0:512], ALU.add),
                after=[P.ctr["tensor"]])
            P.v(lambda e: e.tensor_tensor(ys[:, 512:1024], py1[:], xf[:, 512:1024], ALU.add))
            P.st(lambda e, i=i: e.dma_start(out=Y[i * 128:(i + 1) * 128, :], in_=ys[:]),
                 after=[P.ctr["vector"]])
        P.finish()
        P.play(block)
    return nc




def build_attn_a(dbg=False):
    nc = bass.Bass("TRN2", target_bir_lowering=False)
    XKV = nc.dram_tensor("XKV", [2560, 1024], F32, kind="ExternalInput").ap()
    XSK = nc.dram_tensor("XSK", [4, 1024], F32, kind="ExternalInput").ap()
    WQT = nc.dram_tensor("WQT", [1024, 1024], F32, kind="ExternalInput").ap()
    WKT = nc.dram_tensor("WKT", [1024, 1024], F32, kind="ExternalInput").ap()
    WVT = nc.dram_tensor("WVT", [1024, 1024], F32, kind="ExternalInput").ap()
    WOT = nc.dram_tensor("WOT", [1024, 1024], F32, kind="ExternalInput").ap()
    LNG = nc.dram_tensor("LNG", [1, 1024], F32, kind="ExternalInput").ap()
    LNB = nc.dram_tensor("LNB", [1, 1024], F32, kind="ExternalInput").ap()
    WMT = nc.dram_tensor("WMT", [128, 640], F32, kind="ExternalInput").ap()
    KM = nc.dram_tensor("KM", [128, 20], F32, kind="ExternalInput").ap()
    SM = nc.dram_tensor("SM", [4, 2048], BF16, kind="ExternalInput").ap()
    IDB = nc.dram_tensor("IDB", [128, 128], BF16, kind="ExternalInput").ap()
    IDF = nc.dram_tensor("IDF", [128, 128], F32, kind="ExternalInput").ap()
    Y = nc.dram_tensor("Y", [2048, 1024], F32, kind="ExternalOutput").ap()
    if dbg:
        KD = nc.dram_tensor("KD", [128, 2560], F32, kind="ExternalOutput").ap()
        QD = nc.dram_tensor("QD", [128, 2048], F32, kind="ExternalOutput").ap()
        PD = nc.dram_tensor("PD", [128, 640], F32, kind="ExternalOutput").ap()
        OD = nc.dram_tensor("OD", [128, 65], F32, kind="ExternalOutput").ap()
        AD = nc.dram_tensor("AD", [128, 16384], F32, kind="ExternalOutput").ap()
        Y1D = nc.dram_tensor("Y1D", [2048, 1024], F32, kind="ExternalOutput").ap()
    with ExitStack() as ctx:
        E = ctx.enter_context
        P = Prog(nc, ctx)
        x_fm = E(nc.sbuf_tensor("x_fm", [128, 8 * 2560], BF16))
        att_fm = E(nc.sbuf_tensor("att_fm", [128, 8 * 2048], BF16))
        wot = E(nc.sbuf_tensor("wot", [128, 8 * 1024], BF16))
        wpr = E(nc.sbuf_tensor("wpr", [128, 3 * 1024], BF16))
        wld = E(nc.sbuf_tensor("wld", [128, 1024], F32))
        kA = E(nc.sbuf_tensor("kA", [128, 2560], BF16))
        kB = E(nc.sbuf_tensor("kB", [128, 2560], BF16))
        qA = E(nc.sbuf_tensor("qA", [128, 2048], BF16))
        qB = E(nc.sbuf_tensor("qB", [128, 2048], BF16))
        vA = E(nc.sbuf_tensor("vA", [128, 20 * 65], BF16))
        vB = E(nc.sbuf_tensor("vB", [128, 20 * 65], BF16))
        kskA = E(nc.sbuf_tensor("kskA", [128, 4], BF16))
        kskB = E(nc.sbuf_tensor("kskB", [128, 4], BF16))
        vskA = E(nc.sbuf_tensor("vskA", [4, 65], BF16))
        vskB = E(nc.sbuf_tensor("vskB", [4, 65], BF16))
        xsf = E(nc.sbuf_tensor("xsf", [4, 1024], F32))
        xsb = E(nc.sbuf_tensor("xsb", [4, 1024], BF16))
        xs_fm = E(nc.sbuf_tensor("xs_fm", [128, 8 * 4], BF16))
        wmt = E(nc.sbuf_tensor("wmt", [128, 640], BF16))
        km = E(nc.sbuf_tensor("km", [128, 20], F32))
        sm = E(nc.sbuf_tensor("sm", [4, 2048], BF16))
        mskf = E(nc.sbuf_tensor("mskf", [128, 640], F32))
        pt = E(nc.sbuf_tensor("pt", [128, 640], BF16))
        ptsk = E(nc.sbuf_tensor("ptsk", [4, 128], BF16))
        xt = E(nc.sbuf_tensor("xt", [128, 1024], F32))
        lnx = E(nc.sbuf_tensor("lnx", [128, 1024], F32))
        lnb_ = E(nc.sbuf_tensor("lnb_", [128, 1024], BF16))
        dump = E(nc.sbuf_tensor("dump", [128, 1024], F32))
        ao = E(nc.sbuf_tensor("ao", [128, 64], BF16))
        mx = E(nc.sbuf_tensor("mx", [128, 1], F32))
        mx2 = E(nc.sbuf_tensor("mx2", [128, 1], F32))
        rc = E(nc.sbuf_tensor("rc", [128, 1], F32))
        g_bc = E(nc.sbuf_tensor("g_bc", [128, 1024], F32))
        b_bc = E(nc.sbuf_tensor("b_bc", [128, 1024], F32))
        row1 = E(nc.sbuf_tensor("row1", [1, 1024], F32))
        ones1 = E(nc.sbuf_tensor("ones1", [1, 128], F32))
        ys = E(nc.sbuf_tensor("ys", [128, 8 * 1024], F32))
        yln = E(nc.sbuf_tensor("yln", [128, 1024], F32))
        mcol = E(nc.sbuf_tensor("mcol", [128, 1], F32))
        ssq = E(nc.sbuf_tensor("ssq", [128, 1], F32))
        var = E(nc.sbuf_tensor("var", [128, 1], F32))
        rstd = E(nc.sbuf_tensor("rstd", [128, 1], F32))
        eps5 = E(nc.sbuf_tensor("eps5", [128, 1], F32))
        idb = E(nc.sbuf_tensor("idb", [128, 128], BF16))
        idf = E(nc.sbuf_tensor("idf", [128, 128], F32))
        psA = E(nc.psum_tensor("psA", [128, 512], F32))
        psB = E(nc.psum_tensor("psB", [128, 384], F32))
        p2 = E(nc.psum_tensor("p2", [128, 128], F32))
        po = E(nc.psum_tensor("po", [128, 65], F32))
        ptr = E(nc.psum_tensor("ptr", [128, 128], BF16))
        ptrf = E(nc.psum_tensor("ptrf", [128, 128], F32))
        py0 = E(nc.psum_tensor("py0", [128, 512], F32))
        py1 = E(nc.psum_tensor("py1", [128, 512], F32))
        block = E(nc.Block())

        P.ld(lambda e: e.dma_start(out=idb[:], in_=IDB))
        P.ld(lambda e: e.dma_start(out=idf[:], in_=IDF))
        P.ld(lambda e: e.dma_start(out=mskf[:], in_=WMT))
        P.ld(lambda e: e.dma_start(out=km[:], in_=KM))
        P.ld(lambda e: e.dma_start(out=sm[:], in_=SM))
        P.ld(lambda e: e.dma_start(out=row1[:], in_=LNG))
        P.v(lambda e: e.memset(eps5[:], 1e-5), after=[P.dma])
        P.v(lambda e: e.memset(ones1[:], 1.0))
        P.v(lambda e: e.tensor_copy(wmt[:], mskf[:]))
        P.t(lambda e: e.matmul(py0[:], ones1[:], row1[:, 0:512], start=True, stop=True),
            after=[P.ctr["vector"], P.dma])
        P.t(lambda e: e.matmul(py1[:], ones1[:], row1[:, 512:1024], start=True, stop=True))
        P.v(lambda e: e.tensor_copy(g_bc[:, 0:512], py0[:]), after=[P.ctr["tensor"]])
        P.v(lambda e: e.tensor_copy(g_bc[:, 512:1024], py1[:]))
        P.ld(lambda e: e.dma_start(out=row1[:], in_=LNB), after=[P.ctr["tensor"]])
        P.t(lambda e: e.matmul(py0[:], ones1[:], row1[:, 0:512], start=True, stop=True),
            after=[P.ctr["vector"], P.dma])
        P.t(lambda e: e.matmul(py1[:], ones1[:], row1[:, 512:1024], start=True, stop=True))
        P.v(lambda e: e.tensor_copy(b_bc[:, 0:512], py0[:]), after=[P.ctr["tensor"]])
        P.v(lambda e: e.tensor_copy(b_bc[:, 512:1024], py1[:]))

        for jt in range(20):
            P.ld(lambda e, jt=jt: e.dma_start(out=xt[:], in_=XKV[jt * 128:(jt + 1) * 128, :]),
                 after=ALL(P))
            P.v(lambda e: e.nop(), after=[P.dma])
            emit_ln(P, lnx, xt, dump, mcol, ssq, var, rstd, eps5)
            P.v(lambda e: e.tensor_copy(lnb_[:], lnx[:]), after=[P.ctr["vector"]])
            emit_tr(P, (lambda c, jt=jt: x_fm[:, c * 2560 + jt * 128: c * 2560 + jt * 128 + 128]),
                    lnb_, ptr, idb, 8)
        P.v(lambda e: e.memset(xt[:], 0.0), after=ALL(P))
        P.v(lambda e: e.nop(), after=[P.ctr["vector"]])
        P.ld(lambda e: e.dma_start(out=xt[0:4, :], in_=XSK), after=[P.ctr["vector"]])
        P.v(lambda e: e.nop(), after=[P.dma])
        emit_ln(P, lnx, xt, dump, mcol, ssq, var, rstd, eps5)
        P.v(lambda e: e.tensor_copy(xsb[:], lnx[0:4, :]), after=[P.ctr["vector"]])
        for c in range(8):
            P.t(lambda e, c=c: e.transpose(ptr[:, 0:4], xsb[:, c * 128:(c + 1) * 128], idb[0:4, 0:4]),
                after=[P.ctr["vector"]])
            P.v(lambda e, c=c: e.tensor_copy(xs_fm[:, c * 4:(c + 1) * 4],
                                             ptr[0:128, 0:4]), after=[P.ctr["tensor"]])
        for c in range(8):
            P.ld(lambda e, c=c: e.dma_start(out=wld[:], in_=WOT[c * 128:(c + 1) * 128, :]),
                 after=ALL(P))
            P.v(lambda e, c=c: e.tensor_copy(wot[:, c * 1024:(c + 1) * 1024], wld[:]),
                after=[P.dma])

        for hp in range(8):
            pair_t = [P.ctr["vector"], P.ctr["scalar"], P.ctr["tensor"]]
            for wi, W in enumerate([WQT, WKT, WVT]):
                P.ld(lambda e, wi=wi, W=W, hp=hp: e.dma_start(
                    out=wld[:].rearrange("p (c n) -> p c n", n=128),
                    in_=W[:, hp * 128:(hp + 1) * 128].rearrange("(c p) n -> p c n", p=128)),
                    after=ALL(P))
                P.v(lambda e, wi=wi: e.tensor_copy(wpr[:, wi * 1024:(wi + 1) * 1024], wld[:]),
                    after=[P.dma])
            P.v(lambda e: e.memset(kA[64:128, :], 0.0), after=[P.ctr["tensor"]])
            P.v(lambda e: e.memset(kB[64:128, :], 0.0))
            P.v(lambda e: e.memset(kA[64:65, :], 1.0))
            P.v(lambda e: e.memset(kB[64:65, :], 1.0))
            P.v(lambda e: e.memset(qA[64:128, :], 0.0))
            P.v(lambda e: e.memset(qB[64:128, :], 0.0))
            P.v(lambda e: e.memset(kskA[64:128, :], 0.0))
            P.v(lambda e: e.memset(kskB[64:128, :], 0.0))
            P.v(lambda e: e.memset(kskA[64:65, :], 1.0))
            P.v(lambda e: e.memset(kskB[64:65, :], 1.0))
            v3a = vA[:].rearrange("p (j c) -> p j c", c=65)
            v3b = vB[:].rearrange("p (j c) -> p j c", c=65)
            P.v(lambda e: e.memset(v3a[:, :, 64], 1.0))
            P.v(lambda e: e.memset(v3b[:, :, 64], 1.0))
            P.v(lambda e: e.memset(vskA[:, 64:65], 1.0))
            P.v(lambda e: e.memset(vskB[:, 64:65], 1.0))
            for j5 in range(5):
                for c in range(8):
                    P.t(lambda e, j5=j5, c=c: e.matmul(
                        psA[:], wpr[:, 1024 + c * 128:1024 + c * 128 + 128],
                        x_fm[:, c * 2560 + j5 * 512: c * 2560 + j5 * 512 + 512],
                        start=(c == 0), stop=(c == 7)), after=[P.ctr["vector"]])
                P.v(lambda e, j5=j5: e.tensor_copy(kA[0:64, j5 * 512:(j5 + 1) * 512],
                                                   psA[0:64, :]), after=[P.ctr["tensor"]])
                P.v(lambda e, j5=j5: e.tensor_copy(kB[0:64, j5 * 512:(j5 + 1) * 512],
                                                   psA[64:128, :]))
            for j4 in range(4):
                for c in range(8):
                    P.t(lambda e, j4=j4, c=c: e.matmul(
                        psA[:], wpr[:, c * 128:c * 128 + 128],
                        x_fm[:, c * 2560 + 512 + j4 * 512: c * 2560 + 512 + j4 * 512 + 512],
                        start=(c == 0), stop=(c == 7)), after=[P.ctr["vector"]])
                P.v(lambda e, j4=j4: e.tensor_copy(qA[0:64, j4 * 512:(j4 + 1) * 512],
                                                   psA[0:64, :]), after=[P.ctr["tensor"]])
                P.v(lambda e, j4=j4: e.tensor_copy(qB[0:64, j4 * 512:(j4 + 1) * 512],
                                                   psA[64:128, :]))
            for jt in range(20):
                for c in range(8):
                    P.t(lambda e, jt=jt, c=c: e.matmul(
                        psB[:, 0:128], x_fm[:, c * 2560 + jt * 128: c * 2560 + jt * 128 + 128],
                        wpr[:, 2048 + c * 128: 2048 + c * 128 + 128],
                        start=(c == 0), stop=(c == 7)), after=[P.ctr["vector"]])
                P.v(lambda e, jt=jt: e.tensor_copy(vA[:, jt * 65:jt * 65 + 64], psB[:, 0:64]),
                    after=[P.ctr["tensor"]])
                P.v(lambda e, jt=jt: e.tensor_copy(vB[:, jt * 65:jt * 65 + 64], psB[:, 64:128]))
            for c in range(8):
                P.t(lambda e, c=c: e.matmul(
                    psB[:, 128:132], wpr[:, 1024 + c * 128:1024 + c * 128 + 128],
                    xs_fm[:, c * 4:(c + 1) * 4], start=(c == 0), stop=(c == 7)),
                    after=[P.ctr["vector"]])
            P.v(lambda e: e.tensor_copy(kskA[0:64, :], psB[0:64, 128:132]),
                after=[P.ctr["tensor"]])
            P.v(lambda e: e.tensor_copy(kskB[0:64, :], psB[64:128, 128:132]))
            for c in range(8):
                P.t(lambda e, c=c: e.matmul(
                    psB[0:4, 256:384], xs_fm[:, c * 4:(c + 1) * 4],
                    wpr[:, 2048 + c * 128: 2048 + c * 128 + 128],
                    start=(c == 0), stop=(c == 7)), after=[P.ctr["vector"]])
            P.v(lambda e: e.tensor_copy(vskA[:, 0:64], psB[0:4, 256:320]),
                after=[P.ctr["tensor"]])
            P.v(lambda e: e.tensor_copy(vskB[:, 0:64], psB[0:4, 320:384]))
            for hh, (kX, qX, vX, v3X, kskX, vskX) in enumerate(
                    [(kA, qA, vA, v3a, kskA, vskA), (kB, qB, vB, v3b, kskB, vskB)]):
                h = hp * 2 + hh
                for i in range(16):
                    T0 = i * 128
                    P.t(lambda e, qX=qX, kX=kX, T0=T0: e.matmul(
                        psA[:], qX[:, T0:T0 + 128], kX[:, T0:T0 + 512],
                        start=True, stop=True), after=[P.ctr["vector"]])
                    P.t(lambda e, qX=qX, kX=kX, T0=T0: e.matmul(
                        psB[:, 0:128], qX[:, T0:T0 + 128], kX[:, T0 + 512:T0 + 640],
                        start=True, stop=True))
                    P.t(lambda e, qX=qX, kskX=kskX, T0=T0: e.matmul(
                        psB[:, 128:132], qX[:, T0:T0 + 128], kskX[:],
                        start=True, stop=True))
                    P.v(lambda e: e.tensor_reduce(mx[:], psA[:], AX.X, ALU.max),
                        after=[P.ctr["tensor"]])
                    P.v(lambda e: e.tensor_reduce(mx2[:], psB[:, 0:132], AX.X, ALU.max))
                    P.v(lambda e: e.tensor_tensor(mx[:], mx[:], mx2[:], ALU.max),
                        after=[P.ctr["vector"]])
                    P.v(lambda e: e.tensor_scalar(mx[:], mx[:], 0.125, 0.0, ALU.mult, ALU.add),
                        after=[P.ctr["vector"]])
                    P.t(lambda e: e.transpose(ptrf[0:1, 0:128], mx[:], idf[:]),
                        after=[P.ctr["vector"]])
                    P.v(lambda e, qX=qX, T0=T0: e.tensor_scalar(
                        qX[64:65, T0:T0 + 128], ptrf[0:1, 0:128], -8.0, 0.0,
                        ALU.mult, ALU.add), after=[P.ctr["tensor"]])
                    for cjj in range(5):
                        P.t(lambda e, kX=kX, qX=qX, T0=T0, cjj=cjj: e.matmul(
                            p2[:], kX[:, T0 + cjj * 128:T0 + cjj * 128 + 128],
                            qX[:, T0:T0 + 128], start=True, stop=True),
                            after=[P.ctr["vector"], P.ctr["scalar"]])
                        P.s(lambda e, cjj=cjj: e.activation(
                            pt[:, cjj * 128:(cjj + 1) * 128], p2[:], AF.Exp, scale=0.125),
                            after=[P.ctr["tensor"]])
                        P.v(lambda e, cjj=cjj: e.tensor_tensor(
                            pt[:, cjj * 128:(cjj + 1) * 128], pt[:, cjj * 128:(cjj + 1) * 128],
                            wmt[:, cjj * 128:(cjj + 1) * 128], ALU.mult),
                            after=[P.ctr["scalar"]])
                        P.v(lambda e, cjj=cjj, i=i: e.tensor_scalar(
                            pt[:, cjj * 128:(cjj + 1) * 128], pt[:, cjj * 128:(cjj + 1) * 128],
                            km[:, i + cjj:i + cjj + 1], 0.0, ALU.mult, ALU.add))
                    P.t(lambda e, kskX=kskX, qX=qX, T0=T0: e.matmul(
                        ptrf[0:4, :], kskX[:], qX[:, T0:T0 + 128], start=True, stop=True),
                        after=[P.ctr["vector"], P.ctr["scalar"]])
                    P.s(lambda e: e.activation(ptsk[:], ptrf[0:4, :], AF.Exp, scale=0.125),
                        after=[P.ctr["tensor"]])
                    P.v(lambda e, T0=T0: e.tensor_tensor(
                        ptsk[:], ptsk[:], sm[:, T0:T0 + 128], ALU.mult),
                        after=[P.ctr["scalar"]])
                    for cjj in range(5):
                        P.t(lambda e, cjj=cjj, vX=vX, i=i: e.matmul(
                            po[:], pt[:, cjj * 128:(cjj + 1) * 128],
                            vX[:, (i + cjj) * 65:(i + cjj) * 65 + 65],
                            start=(cjj == 0), stop=False), after=[P.ctr["vector"]])
                    P.t(lambda e, vskX=vskX: e.matmul(
                        po[:], ptsk[:], vskX[:], start=False, stop=True))
                    P.v(lambda e: e.reciprocal(rc[:], po[:, 64:65]), after=[P.ctr["tensor"]])
                    P.v(lambda e: e.tensor_scalar(ao[:], po[:, 0:64], rc[:], 0.0,
                                                  ALU.mult, ALU.add), after=[P.ctr["vector"]])
                    P.t(lambda e: e.transpose(ptr[0:64, :], ao[:], idb[:]),
                        after=[P.ctr["vector"]])
                    P.v(lambda e, h=h, T0=T0: e.tensor_copy(
                        att_fm[(h % 2) * 64:(h % 2) * 64 + 64,
                               (h // 2) * 2048 + T0:(h // 2) * 2048 + T0 + 128],
                        ptr[0:64, :]), after=[P.ctr["tensor"]])
                    if dbg and h == 0 and i == 3:
                        P.v(lambda e: e.tensor_copy(dump[:, 0:640], pt[:]),
                            after=[P.ctr["vector"]])
                        P.st(lambda e: e.dma_start(out=PD, in_=dump[:, 0:640]),
                             after=[P.ctr["vector"]])
                        P.v(lambda e: e.tensor_copy(dump[:, 0:65], po[:]))
                        P.st(lambda e: e.dma_start(out=OD, in_=dump[:, 0:65]),
                             after=[P.ctr["vector"]])
            if dbg and hp == 0:
                for seg in range(2):
                    P.v(lambda e, seg=seg: e.tensor_copy(dump[:], kA[:, seg * 1024:(seg + 1) * 1024]),
                        after=[P.ctr["vector"], P.dmo])
                    P.st(lambda e, seg=seg: e.dma_start(out=KD[:, seg * 1024:(seg + 1) * 1024],
                         in_=dump[:]), after=[P.ctr["vector"]])
                P.v(lambda e: e.tensor_copy(dump[:, 0:512], kA[:, 2048:2560]),
                    after=[P.ctr["vector"], P.dmo])
                P.st(lambda e: e.dma_start(out=KD[:, 2048:2560], in_=dump[:, 0:512]),
                     after=[P.ctr["vector"]])
                for seg in range(2):
                    P.v(lambda e, seg=seg: e.tensor_copy(dump[:], qA[:, seg * 1024:(seg + 1) * 1024]),
                        after=[P.ctr["vector"], P.dmo])
                    P.st(lambda e, seg=seg: e.dma_start(out=QD[:, seg * 1024:(seg + 1) * 1024],
                         in_=dump[:]), after=[P.ctr["vector"]])
        if dbg:
            P.barrier()
            for seg in range(16):
                P.v(lambda e, seg=seg: e.tensor_copy(dump[:], att_fm[:, seg * 1024:(seg + 1) * 1024]),
                    after=[P.ctr["vector"], P.dmo])
                P.st(lambda e, seg=seg: e.dma_start(out=AD[:, seg * 1024:(seg + 1) * 1024],
                     in_=dump[:]), after=[P.ctr["vector"]])
        for i in range(16):
            T0 = i * 128
            P.barrier()
            P.ld(lambda e, T0=T0: e.dma_start(out=xt[:], in_=XKV[512 + T0:512 + T0 + 128, :]),
                 after=ALL(P))
            for c in range(8):
                P.t(lambda e, c=c, T0=T0: e.matmul(
                    py0[:], att_fm[:, c * 2048 + T0:c * 2048 + T0 + 128],
                    wot[:, c * 1024:c * 1024 + 512],
                    start=(c == 0), stop=(c == 7)), after=[P.ctr["vector"]])
                P.t(lambda e, c=c, T0=T0: e.matmul(
                    py1[:], att_fm[:, c * 2048 + T0:c * 2048 + T0 + 128],
                    wot[:, c * 1024 + 512:c * 1024 + 1024],
                    start=(c == 0), stop=(c == 7)))
            yss = ys[:, (i % 8) * 1024:(i % 8) * 1024 + 1024]
            P.v(lambda e: e.nop(), after=[P.dma, P.ctr["tensor"], P.dmo])
            P.v(lambda e, yss=yss: e.tensor_tensor(yss[:, 0:512], py0[:], xt[:, 0:512], ALU.add))
            P.v(lambda e, yss=yss: e.tensor_tensor(yss[:, 512:1024], py1[:], xt[:, 512:1024], ALU.add))
            P.v(lambda e: e.nop(), after=[P.ctr["vector"]])
            emit_ln(P, yln, yss, dump, mcol, ssq, var, rstd, eps5)
            P.v(lambda e: e.tensor_tensor(yln[:], yln[:], g_bc[:], ALU.mult),
                after=[P.ctr["vector"]])
            P.v(lambda e: e.tensor_tensor(yln[:], yln[:], b_bc[:], ALU.add))
            P.v(lambda e, yss=yss: e.tensor_tensor(yss[:], yss[:], yln[:], ALU.add),
                after=[P.ctr["vector"]])
            P.st(lambda e, T0=T0, yss=yss: e.dma_start(out=Y[T0:T0 + 128, :], in_=yss[:]),
                 after=[P.ctr["vector"]])
        P.finish()
        P.play(block)
    return nc


def run_attn_a(x, Wq, Wk, Wv, Wo, ln_g, ln_b):
    nc = get_nc("attn_a")
    B, S, D = x.shape
    cc, tt = np.meshgrid(np.arange(640), np.arange(128), indexing="ij")
    wmt0 = ((cc - tt >= 1) & (cc - tt <= 512)).astype(np.float32)
    wmt = np.ascontiguousarray(
        wmt0.reshape(5, 128, 128).transpose(1, 0, 2).reshape(128, 640))
    in_maps = []
    for c in range(8):
        b, half = c // 2, c % 2
        qstart = half * 2048
        xkv = np.zeros((2560, 1024), np.float32)
        lo = qstart - 512
        xkv[max(0, -lo):] = x[b, max(lo, 0):qstart + 2048]
        km = np.ascontiguousarray(
            (np.arange(lo, lo + 2560) >= 0).astype(np.float32).reshape(20, 128).T)
        gq = qstart + np.arange(2048)
        smv = ((gq[None, :] >= np.arange(4)[:, None]) &
               (gq[None, :] - np.arange(4)[:, None] > 511)).astype(np.float32)
        in_maps.append({"XKV": np.ascontiguousarray(xkv), "XSK": np.ascontiguousarray(x[b, 0:4]),
                        "WQT": np.ascontiguousarray(Wq.T), "WKT": np.ascontiguousarray(Wk.T),
                        "WVT": np.ascontiguousarray(Wv.T), "WOT": np.ascontiguousarray(Wo.T),
                        "LNG": ln_g.reshape(1, 1024), "LNB": ln_b.reshape(1, 1024),
                        "WMT": wmt, "KM": km, "SM": smv.astype(BF),
                        "IDB": ID128B, "IDF": ID128F})
    res = run_bass_kernel_spmd(nc, in_maps, core_ids=list(range(8)))
    y = np.stack([res.results[c]["Y"] for c in range(8)], 0).reshape(4, 4096, 1024)
    return y




def build_mha(QOFF0):
    nc = bass.Bass("TRN2", target_bir_lowering=False)
    XF = nc.dram_tensor("XF", [4096, 1024], F32, kind="ExternalInput").ap()
    WQT = nc.dram_tensor("WQT", [1024, 1024], F32, kind="ExternalInput").ap()
    WKT = nc.dram_tensor("WKT", [1024, 1024], F32, kind="ExternalInput").ap()
    WVT = nc.dram_tensor("WVT", [1024, 1024], F32, kind="ExternalInput").ap()
    WOT = nc.dram_tensor("WOT", [1024, 1024], F32, kind="ExternalInput").ap()
    BQ = nc.dram_tensor("BQ", [128, 8], F32, kind="ExternalInput").ap()
    BK = nc.dram_tensor("BK", [128, 8], F32, kind="ExternalInput").ap()
    BVR = nc.dram_tensor("BVR", [1, 1024], F32, kind="ExternalInput").ap()
    BO = nc.dram_tensor("BO", [1, 1024], F32, kind="ExternalInput").ap()
    IDB = nc.dram_tensor("IDB", [128, 128], BF16, kind="ExternalInput").ap()
    IDF = nc.dram_tensor("IDF", [128, 128], F32, kind="ExternalInput").ap()
    Y = nc.dram_tensor("Y", [2048, 1024], F32, kind="ExternalOutput").ap()
    with ExitStack() as ctx:
        E = ctx.enter_context
        P = Prog(nc, ctx)
        x_fm = E(nc.sbuf_tensor("x_fm", [128, 8 * 4096], BF16))
        att_fm = E(nc.sbuf_tensor("att_fm", [128, 8 * 2048], BF16))
        wot = E(nc.sbuf_tensor("wot", [128, 8 * 1024], BF16))
        wpr = E(nc.sbuf_tensor("wpr", [128, 3 * 1024], BF16))
        wld = E(nc.sbuf_tensor("wld", [128, 1024], F32))
        kA = E(nc.sbuf_tensor("kA", [128, 4096], BF16))
        kB = E(nc.sbuf_tensor("kB", [128, 4096], BF16))
        qA = E(nc.sbuf_tensor("qA", [128, 2048], BF16))
        qB = E(nc.sbuf_tensor("qB", [128, 2048], BF16))
        vA = E(nc.sbuf_tensor("vA", [128, 32 * 65], BF16))
        vB = E(nc.sbuf_tensor("vB", [128, 32 * 65], BF16))
        bq = E(nc.sbuf_tensor("bq", [128, 8], F32))
        bk = E(nc.sbuf_tensor("bk", [128, 8], F32))
        bvrow = E(nc.sbuf_tensor("bvrow", [1, 1024], F32))
        bvA_bc = E(nc.sbuf_tensor("bvA_bc", [128, 64], F32))
        bvB_bc = E(nc.sbuf_tensor("bvB_bc", [128, 64], F32))
        pt = E(nc.sbuf_tensor("pt", [128, 4096], BF16))
        xt = E(nc.sbuf_tensor("xt", [128, 1024], F32))
        lnx = E(nc.sbuf_tensor("lnx", [128, 1024], F32))
        lnb_ = E(nc.sbuf_tensor("lnb_", [128, 1024], BF16))
        dump = E(nc.sbuf_tensor("dump", [128, 1024], F32))
        ao = E(nc.sbuf_tensor("ao", [128, 64], BF16))
        mx = E(nc.sbuf_tensor("mx", [128, 1], F32))
        mx2 = E(nc.sbuf_tensor("mx2", [128, 1], F32))
        rc = E(nc.sbuf_tensor("rc", [128, 1], F32))
        bo_bc = E(nc.sbuf_tensor("bo_bc", [128, 1024], F32))
        row1 = E(nc.sbuf_tensor("row1", [1, 1024], F32))
        ones1 = E(nc.sbuf_tensor("ones1", [1, 128], F32))
        ys = E(nc.sbuf_tensor("ys", [128, 2 * 1024], F32))
        mcol = E(nc.sbuf_tensor("mcol", [128, 1], F32))
        ssq = E(nc.sbuf_tensor("ssq", [128, 1], F32))
        var = E(nc.sbuf_tensor("var", [128, 1], F32))
        rstd = E(nc.sbuf_tensor("rstd", [128, 1], F32))
        eps5 = E(nc.sbuf_tensor("eps5", [128, 1], F32))
        idb = E(nc.sbuf_tensor("idb", [128, 128], BF16))
        idf = E(nc.sbuf_tensor("idf", [128, 128], F32))
        psA = E(nc.psum_tensor("psA", [128, 512], F32))
        p2 = E(nc.psum_tensor("p2", [128, 512], F32))
        po = E(nc.psum_tensor("po", [128, 65], F32))
        ptr = E(nc.psum_tensor("ptr", [128, 128], BF16))
        ptrf = E(nc.psum_tensor("ptrf", [128, 128], F32))
        py0 = E(nc.psum_tensor("py0", [128, 512], F32))
        py1 = E(nc.psum_tensor("py1", [128, 512], F32))
        block = E(nc.Block())

        P.ld(lambda e: e.dma_start(out=idb[:], in_=IDB))
        P.ld(lambda e: e.dma_start(out=idf[:], in_=IDF))
        P.ld(lambda e: e.dma_start(out=bq[:], in_=BQ))
        P.ld(lambda e: e.dma_start(out=bk[:], in_=BK))
        P.ld(lambda e: e.dma_start(out=bvrow[:], in_=BVR))
        P.ld(lambda e: e.dma_start(out=row1[:], in_=BO))
        P.v(lambda e: e.memset(eps5[:], 1e-5), after=[P.dma])
        P.v(lambda e: e.memset(ones1[:], 1.0))
        P.t(lambda e: e.matmul(py0[:], ones1[:], row1[:, 0:512], start=True, stop=True),
            after=[P.ctr["vector"], P.dma])
        P.t(lambda e: e.matmul(py1[:], ones1[:], row1[:, 512:1024], start=True, stop=True))
        P.v(lambda e: e.tensor_copy(bo_bc[:, 0:512], py0[:]), after=[P.ctr["tensor"]])
        P.v(lambda e: e.tensor_copy(bo_bc[:, 512:1024], py1[:]))

        for jt in range(32):
            P.ld(lambda e, jt=jt: e.dma_start(out=xt[:], in_=XF[jt * 128:(jt + 1) * 128, :]),
                 after=ALL(P))
            P.v(lambda e: e.nop(), after=[P.dma])
            emit_ln(P, lnx, xt, dump, mcol, ssq, var, rstd, eps5)
            P.v(lambda e: e.tensor_copy(lnb_[:], lnx[:]), after=[P.ctr["vector"]])
            emit_tr(P, (lambda c, jt=jt: x_fm[:, c * 4096 + jt * 128: c * 4096 + jt * 128 + 128]),
                    lnb_, ptr, idb, 8)
        for c in range(8):
            P.ld(lambda e, c=c: e.dma_start(out=wld[:], in_=WOT[c * 128:(c + 1) * 128, :]),
                 after=ALL(P))
            P.v(lambda e, c=c: e.tensor_copy(wot[:, c * 1024:(c + 1) * 1024], wld[:]),
                after=[P.dma])

        for hp in range(8):
            P.t(lambda e, hp=hp: e.matmul(
                ptrf[:, 0:64], ones1[:], bvrow[:, hp * 128:hp * 128 + 64],
                start=True, stop=True), after=[P.ctr["vector"], P.ctr["scalar"]])
            P.v(lambda e: e.tensor_copy(bvA_bc[:], ptrf[:, 0:64]), after=[P.ctr["tensor"]])
            P.t(lambda e, hp=hp: e.matmul(
                ptrf[:, 64:128], ones1[:], bvrow[:, hp * 128 + 64:hp * 128 + 128],
                start=True, stop=True), after=[P.ctr["vector"]])
            P.v(lambda e: e.tensor_copy(bvB_bc[:], ptrf[:, 64:128]), after=[P.ctr["tensor"]])
            for wi, W in enumerate([WQT, WKT, WVT]):
                P.ld(lambda e, wi=wi, W=W, hp=hp: e.dma_start(
                    out=wld[:].rearrange("p (c n) -> p c n", n=128),
                    in_=W[:, hp * 128:(hp + 1) * 128].rearrange("(c p) n -> p c n", p=128)),
                    after=ALL(P))
                P.v(lambda e, wi=wi: e.tensor_copy(wpr[:, wi * 1024:(wi + 1) * 1024], wld[:]),
                    after=[P.dma])
            P.v(lambda e: e.memset(kA[64:128, :], 0.0), after=[P.ctr["tensor"]])
            P.v(lambda e: e.memset(kB[64:128, :], 0.0))
            P.v(lambda e: e.memset(kA[64:65, :], 1.0))
            P.v(lambda e: e.memset(kB[64:65, :], 1.0))
            P.v(lambda e: e.memset(qA[64:128, :], 0.0))
            P.v(lambda e: e.memset(qB[64:128, :], 0.0))
            v3a = vA[:].rearrange("p (j c) -> p j c", c=65)
            v3b = vB[:].rearrange("p (j c) -> p j c", c=65)
            P.v(lambda e, v3a=v3a: e.memset(v3a[:, :, 64], 1.0))
            P.v(lambda e, v3b=v3b: e.memset(v3b[:, :, 64], 1.0))
            for j5 in range(8):
                for c in range(8):
                    P.t(lambda e, j5=j5, c=c: e.matmul(
                        psA[:], wpr[:, 1024 + c * 128:1024 + c * 128 + 128],
                        x_fm[:, c * 4096 + j5 * 512: c * 4096 + j5 * 512 + 512],
                        start=(c == 0), stop=(c == 7)), after=[P.ctr["vector"]])
                P.v(lambda e, j5=j5, hp=hp: e.tensor_scalar(
                    kA[0:64, j5 * 512:(j5 + 1) * 512], psA[0:64, :],
                    bk[0:64, hp:hp + 1], 0.0, ALU.add, ALU.add), after=[P.ctr["tensor"]])
                P.v(lambda e, j5=j5, hp=hp: e.tensor_scalar(
                    kB[0:64, j5 * 512:(j5 + 1) * 512], psA[64:128, :],
                    bk[64:128, hp:hp + 1], 0.0, ALU.add, ALU.add))
            for j4 in range(4):
                for c in range(8):
                    P.t(lambda e, j4=j4, c=c: e.matmul(
                        psA[:], wpr[:, c * 128:c * 128 + 128],
                        x_fm[:, c * 4096 + QOFF0 + j4 * 512: c * 4096 + QOFF0 + j4 * 512 + 512],
                        start=(c == 0), stop=(c == 7)), after=[P.ctr["vector"]])
                P.v(lambda e, j4=j4, hp=hp: e.tensor_scalar(
                    qA[0:64, j4 * 512:(j4 + 1) * 512], psA[0:64, :],
                    bq[0:64, hp:hp + 1], 0.0, ALU.add, ALU.add), after=[P.ctr["tensor"]])
                P.v(lambda e, j4=j4, hp=hp: e.tensor_scalar(
                    qB[0:64, j4 * 512:(j4 + 1) * 512], psA[64:128, :],
                    bq[64:128, hp:hp + 1], 0.0, ALU.add, ALU.add))
            for jt in range(32):
                for c in range(8):
                    P.t(lambda e, jt=jt, c=c: e.matmul(
                        p2[:, 0:128], x_fm[:, c * 4096 + jt * 128: c * 4096 + jt * 128 + 128],
                        wpr[:, 2048 + c * 128: 2048 + c * 128 + 128],
                        start=(c == 0), stop=(c == 7)), after=[P.ctr["vector"]])
                P.v(lambda e, jt=jt: e.tensor_tensor(
                    vA[:, jt * 65:jt * 65 + 64], p2[:, 0:64], bvA_bc[:, 0:64], ALU.add),
                    after=[P.ctr["tensor"]])
                P.v(lambda e, jt=jt: e.tensor_tensor(
                    vB[:, jt * 65:jt * 65 + 64], p2[:, 64:128], bvB_bc[:, 0:64], ALU.add))
            for hh in range(2):
                kX, qX, vX = (kA, qA, vA) if hh == 0 else (kB, qB, vB)
                for i in range(16):
                    T0 = i * 128
                    for j5 in range(8):
                        P.t(lambda e, qX=qX, kX=kX, T0=T0, j5=j5: e.matmul(
                            psA[:], qX[:, T0:T0 + 128], kX[:, j5 * 512:(j5 + 1) * 512],
                            start=True, stop=True), after=[P.ctr["vector"]])
                        if j5 == 0:
                            P.v(lambda e: e.tensor_reduce(mx[:], psA[:], AX.X, ALU.max),
                                after=[P.ctr["tensor"]])
                        else:
                            P.v(lambda e: e.tensor_reduce(mx2[:], psA[:], AX.X, ALU.max),
                                after=[P.ctr["tensor"]])
                            P.v(lambda e: e.tensor_tensor(mx[:], mx[:], mx2[:], ALU.max),
                                after=[P.ctr["vector"]])
                    P.v(lambda e: e.nop(), after=[P.ctr["vector"]])
                    P.t(lambda e: e.transpose(ptrf[0:1, 0:128], mx[:], idf[:]),
                        after=[P.ctr["vector"]])
                    P.v(lambda e, qX=qX, T0=T0: e.tensor_scalar(
                        qX[64:65, T0:T0 + 128], ptrf[0:1, 0:128], -8.0, 0.0,
                        ALU.mult, ALU.add), after=[P.ctr["tensor"]])
                    for g4 in range(8):
                        for q4 in range(4):
                            cjj = g4 * 4 + q4
                            P.t(lambda e, kX=kX, qX=qX, T0=T0, cjj=cjj, q4=q4: e.matmul(
                                p2[:, q4 * 128:(q4 + 1) * 128],
                                kX[:, cjj * 128:cjj * 128 + 128],
                                qX[:, T0:T0 + 128], start=True, stop=True),
                                after=[P.ctr["vector"], P.ctr["scalar"]])
                        P.s(lambda e, g4=g4: e.activation(
                            pt[:, g4 * 512:(g4 + 1) * 512], p2[:], AF.Exp, scale=0.125),
                            after=[P.ctr["tensor"]])
                    for cjj in range(32):
                        P.t(lambda e, cjj=cjj, vX=vX: e.matmul(
                            po[:], pt[:, cjj * 128:(cjj + 1) * 128],
                            vX[:, cjj * 65:cjj * 65 + 65],
                            start=(cjj == 0), stop=(cjj == 31)),
                            after=[P.ctr["scalar"]] if cjj == 0 else [])
                    P.v(lambda e: e.reciprocal(rc[:], po[:, 64:65]), after=[P.ctr["tensor"]])
                    P.v(lambda e: e.tensor_scalar(ao[:], po[:, 0:64], rc[:], 0.0,
                                                  ALU.mult, ALU.add), after=[P.ctr["vector"]])
                    P.t(lambda e: e.transpose(ptr[0:64, :], ao[:], idb[:]),
                        after=[P.ctr["vector"]])
                    h = hp * 2 + hh
                    P.v(lambda e, h=h, T0=T0: e.tensor_copy(
                        att_fm[(h % 2) * 64:(h % 2) * 64 + 64,
                               (h // 2) * 2048 + T0:(h // 2) * 2048 + T0 + 128],
                        ptr[0:64, :]), after=[P.ctr["tensor"]])
        for i in range(16):
            T0 = i * 128
            P.ld(lambda e, T0=T0: e.dma_start(out=xt[:], in_=XF[QOFF0 + T0:QOFF0 + T0 + 128, :]),
                 after=ALL(P))
            for c in range(8):
                P.t(lambda e, c=c, T0=T0: e.matmul(
                    py0[:], att_fm[:, c * 2048 + T0:c * 2048 + T0 + 128],
                    wot[:, c * 1024:c * 1024 + 512],
                    start=(c == 0), stop=(c == 7)), after=[P.ctr["vector"]])
                P.t(lambda e, c=c, T0=T0: e.matmul(
                    py1[:], att_fm[:, c * 2048 + T0:c * 2048 + T0 + 128],
                    wot[:, c * 1024 + 512:c * 1024 + 1024],
                    start=(c == 0), stop=(c == 7)))
            yss = ys[:, (i % 2) * 1024:(i % 2) * 1024 + 1024]
            P.v(lambda e: e.nop(), after=[P.dma, P.ctr["tensor"], P.dmo])
            P.v(lambda e, yss=yss: e.tensor_tensor(yss[:, 0:512], py0[:], xt[:, 0:512], ALU.add))
            P.v(lambda e, yss=yss: e.tensor_tensor(yss[:, 512:1024], py1[:], xt[:, 512:1024],
                                                   ALU.add))
            P.v(lambda e, yss=yss: e.tensor_tensor(yss[:], yss[:], bo_bc[:], ALU.add),
                after=[P.ctr["vector"]])
            P.st(lambda e, T0=T0, yss=yss: e.dma_start(out=Y[T0:T0 + 128, :], in_=yss[:]),
                 after=[P.ctr["vector"]])
        P.finish()
        P.play(block)
    return nc


def run_mha(x, Wq, bq, Wk, bk, Wv, bv, Wo, bo):
    WQT = np.ascontiguousarray(Wq.reshape(1024, 1024).T)
    WKT = np.ascontiguousarray(Wk.reshape(1024, 1024).T)
    WVT = np.ascontiguousarray(Wv.reshape(1024, 1024).T)
    bqf = np.ascontiguousarray(bq.reshape(1024).reshape(8, 128).T)
    bkf = np.ascontiguousarray(bk.reshape(1024).reshape(8, 128).T)
    in_maps = []
    for c in range(8):
        b, half = c // 2, c % 2
        xf = np.roll(x[b], -half * 2048, axis=0)
        in_maps.append({"XF": np.ascontiguousarray(xf),
                        "WQT": WQT, "WKT": WKT, "WVT": WVT,
                        "WOT": np.ascontiguousarray(Wo.T),
                        "BQ": bqf, "BK": bkf,
                        "BVR": np.ascontiguousarray(bv.reshape(1, 1024)),
                        "BO": bo.reshape(1, 1024),
                        "IDB": ID128B, "IDF": ID128F})
    res = run_bass_kernel_spmd(get_nc("mha0"), in_maps, core_ids=list(range(8)))
    y = np.stack([res.results[c]["Y"] for c in range(8)], 0)
    return y.reshape(4, 4096, 1024)


def build_gla(dbg=False):
    nc = bass.Bass("TRN2", target_bir_lowering=False)
    XB = nc.dram_tensor("XB", [4096, 1024], F32, kind="ExternalInput").ap()
    UPT = nc.dram_tensor("UPT", [1024, 2048], F32, kind="ExternalInput").ap()
    GATT = nc.dram_tensor("GATT", [1024, 2048], F32, kind="ExternalInput").ap()
    DWT = nc.dram_tensor("DWT", [2048, 1024], F32, kind="ExternalInput").ap()
    QWT = nc.dram_tensor("QWT", [2048, 64], F32, kind="ExternalInput").ap()
    KWT = nc.dram_tensor("KWT", [2048, 64], F32, kind="ExternalInput").ap()
    GWT = nc.dram_tensor("GWT", [2048, 64], F32, kind="ExternalInput").ap()
    TWT = nc.dram_tensor("TWT", [2048, 64], F32, kind="ExternalInput").ap()
    IG1T = nc.dram_tensor("IG1T", [2048, 16], F32, kind="ExternalInput").ap()
    IG2T = nc.dram_tensor("IG2T", [16, 2048], F32, kind="ExternalInput").ap()
    CW = nc.dram_tensor("CW", [128, 64], F32, kind="ExternalInput").ap()
    CB = nc.dram_tensor("CB", [128, 16], F32, kind="ExternalInput").ap()
    IGB2 = nc.dram_tensor("IGB2", [128, 16], F32, kind="ExternalInput").ap()
    RW = nc.dram_tensor("RW", [128, 16], F32, kind="ExternalInput").ap()
    NW = nc.dram_tensor("NW", [128, 16], F32, kind="ExternalInput").ap()
    GB = nc.dram_tensor("GB", [64, 1], F32, kind="ExternalInput").ap()
    TB = nc.dram_tensor("TB", [64, 1], F32, kind="ExternalInput").ap()
    LNG = nc.dram_tensor("LNG", [1, 1024], F32, kind="ExternalInput").ap()
    LNB = nc.dram_tensor("LNB", [1, 1024], F32, kind="ExternalInput").ap()
    IDB = nc.dram_tensor("IDB", [128, 128], BF16, kind="ExternalInput").ap()
    IDF = nc.dram_tensor("IDF", [128, 128], F32, kind="ExternalInput").ap()
    Y = nc.dram_tensor("Y", [4096, 1024], F32, kind="ExternalOutput").ap()
    if dbg:
        DSH = nc.dram_tensor("DSH", [128, 4096], F32, kind="ExternalOutput").ap()
        DXU = nc.dram_tensor("DXU", [128, 4096], F32, kind="ExternalOutput").ap()
        DQ4 = nc.dram_tensor("DQ4", [64, 4 * 256], F32, kind="ExternalOutput").ap()
        DB = nc.dram_tensor("DB", [64, 256], F32, kind="ExternalOutput").ap()
        DOF = nc.dram_tensor("DOF", [128, 4096], F32, kind="ExternalOutput").ap()
        DXG = nc.dram_tensor("DXG", [128, 4096], F32, kind="ExternalOutput").ap()
        DMB = nc.dram_tensor("DMB", [1, 256], F32, kind="ExternalOutput").ap()
        DON = nc.dram_tensor("DON", [128, 4096], F32, kind="ExternalOutput").ap()
    NT = 16
    with ExitStack() as ctx:
        E = ctx.enter_context
        P = Prog(nc, ctx)
        upt = E(nc.sbuf_tensor("upt", [128, 8 * 2048], BF16))
        gatt = E(nc.sbuf_tensor("gatt", [128, 8 * 2048], BF16))
        dwt = E(nc.sbuf_tensor("dwt", [128, 16 * 1024], BF16))
        qw4 = E(nc.sbuf_tensor("qw4", [128, 16 * 256], BF16))
        ig1 = E(nc.sbuf_tensor("ig1", [128, 16 * 16], BF16))
        ig2 = E(nc.sbuf_tensor("ig2", [16, 2048], BF16))
        cw = E(nc.sbuf_tensor("cw", [128, 64], F32))
        cb = E(nc.sbuf_tensor("cb", [128, 16], F32))
        igb2 = E(nc.sbuf_tensor("igb2", [128, 16], F32))
        rw = E(nc.sbuf_tensor("rw", [128, 16], F32))
        nw = E(nc.sbuf_tensor("nw", [128, 16], F32))
        gb = E(nc.sbuf_tensor("gb", [64, 1], F32))
        tb = E(nc.sbuf_tensor("tb", [64, 1], F32))
        wld = E(nc.sbuf_tensor("wld", [128, 512], F32))
        x_fm = E(nc.sbuf_tensor("x_fm", [128, 8 * 256], BF16))
        xup = E(nc.sbuf_tensor("xup", [128, 16 * 259], BF16))
        tails = E(nc.sbuf_tensor("tails", [128, 16 * 3], BF16))
        shf = E(nc.sbuf_tensor("shf", [128, 16 * 256], BF16))
        gat = E(nc.sbuf_tensor("gat", [128, 16 * 256], BF16))
        ofm = E(nc.sbuf_tensor("ofm", [128, 16 * 256], BF16))
        vt = E(nc.sbuf_tensor("vt", [64, 2048], BF16))
        ovb = E(nc.sbuf_tensor("ovb", [64, 512], BF16))
        S = E(nc.sbuf_tensor("S", [64, 2048], F32))
        Sb = E(nc.sbuf_tensor("Sb", [64, 2048], BF16))
        qf = E(nc.sbuf_tensor("qf", [64, 256], F32))
        kf = E(nc.sbuf_tensor("kf", [64, 256], F32))
        gf = E(nc.sbuf_tensor("gf", [64, 256], F32))
        tf = E(nc.sbuf_tensor("tf", [64, 256], F32))
        b0 = E(nc.sbuf_tensor("b0", [64, 256], F32))
        b1 = E(nc.sbuf_tensor("b1", [64, 256], F32))
        qe = E(nc.sbuf_tensor("qe", [64, 256], BF16))
        ke = E(nc.sbuf_tensor("ke", [64, 256], BF16))
        keS = E(nc.sbuf_tensor("keS", [64, 64], BF16))
        keT = E(nc.sbuf_tensor("keT", [64, 64], BF16))
        ATm = E(nc.sbuf_tensor("ATm", [64, 64], BF16))
        m01f = E(nc.sbuf_tensor("m01f", [64, 64], F32))
        sc1 = E(nc.sbuf_tensor("sc1", [64, 256], F32))
        sc2 = E(nc.sbuf_tensor("sc2", [64, 256], F32))
        sc3 = E(nc.sbuf_tensor("sc3", [64, 256], F32))
        ebt = E(nc.sbuf_tensor("ebt", [64, 4], F32))
        pre1s = E(nc.sbuf_tensor("pre1s", [128, 16], F32))
        pre1t = E(nc.sbuf_tensor("pre1t", [16, 256], BF16))
        sig = E(nc.sbuf_tensor("sig", [128, 256], F32))
        mrow = E(nc.sbuf_tensor("mrow", [1, 256], F32))
        mbc = E(nc.sbuf_tensor("mbc", [128, 256], BF16))
        xt = E(nc.sbuf_tensor("xt", [128, 1024], F32))
        lnx = E(nc.sbuf_tensor("lnx", [128, 1024], F32))
        mcol = E(nc.sbuf_tensor("mcol", [128, 1], F32))
        ssq = E(nc.sbuf_tensor("ssq", [128, 1], F32))
        var = E(nc.sbuf_tensor("var", [128, 1], F32))
        rstd = E(nc.sbuf_tensor("rstd", [128, 1], F32))
        eps5 = E(nc.sbuf_tensor("eps5", [128, 1], F32))
        g_bc = E(nc.sbuf_tensor("g_bc", [128, 1024], F32))
        b_bc = E(nc.sbuf_tensor("b_bc", [128, 1024], F32))
        row1 = E(nc.sbuf_tensor("row1", [1, 1024], F32))
        ones1 = E(nc.sbuf_tensor("ones1", [1, 128], F32))
        onesc = E(nc.sbuf_tensor("onesc", [128, 1], F32))
        ioti = E(nc.sbuf_tensor("ioti", [64, 64], I32))
        ys = E(nc.sbuf_tensor("ys", [128, 1024], F32))
        yln = E(nc.sbuf_tensor("yln", [128, 1024], F32))
        idb = E(nc.sbuf_tensor("idb", [128, 128], BF16))
        idf = E(nc.sbuf_tensor("idf", [128, 128], F32))
        pA = E(nc.psum_tensor("pA", [128, 256], F32))
        po = E(nc.psum_tensor("po", [64, 512], F32))
        pkv = E(nc.psum_tensor("pkv", [64, 512], F32))
        ptr = E(nc.psum_tensor("ptr", [128, 128], BF16))
        ptrf = E(nc.psum_tensor("ptrf", [128, 256], F32))
        py0 = E(nc.psum_tensor("py0", [128, 512], F32))
        py1 = E(nc.psum_tensor("py1", [128, 512], F32))
        block = E(nc.Block())

        P.ld(lambda e: e.dma_start(out=idb[:], in_=IDB))
        P.ld(lambda e: e.dma_start(out=idf[:], in_=IDF))
        for dst, src in [(cw, CW), (cb, CB), (igb2, IGB2), (rw, RW), (nw, NW),
                         (gb, GB), (tb, TB)]:
            P.ld(lambda e, dst=dst, src=src: e.dma_start(out=dst[:], in_=src))
        P.v(lambda e: e.memset(eps5[:], 1e-5), after=[P.dma])
        P.v(lambda e: e.memset(ones1[:], 1.0))
        P.v(lambda e: e.memset(onesc[:], 1.0))
        P.v(lambda e: e.memset(S[:], 0.0))
        P.v(lambda e: e.memset(Sb[:], 0.0))
        P.v(lambda e: e.memset(tails[:], 0.0))
        P.g(lambda e: e.iota(ioti[:], [[1, 64]], base=0, channel_multiplier=-1), after=[P.dma])
        P.v(lambda e: e.tensor_copy(sc1[:, 0:64], ioti[:]), after=[P.ctr["gpsimd"]])
        P.v(lambda e: e.tensor_scalar(m01f[:], sc1[:, 0:64], 0.0, 0.0, ALU.is_ge, ALU.add),
            after=[P.ctr["vector"]])
        P.ld(lambda e: e.dma_start(out=row1[:], in_=LNG), after=[P.ctr["vector"]])
        P.t(lambda e: e.matmul(py0[:], ones1[:], row1[:, 0:512], start=True, stop=True),
            after=[P.ctr["vector"], P.dma])
        P.t(lambda e: e.matmul(py1[:], ones1[:], row1[:, 512:1024], start=True, stop=True))
        P.v(lambda e: e.tensor_copy(g_bc[:, 0:512], py0[:]), after=[P.ctr["tensor"]])
        P.v(lambda e: e.tensor_copy(g_bc[:, 512:1024], py1[:]))
        P.ld(lambda e: e.dma_start(out=row1[:], in_=LNB), after=[P.ctr["tensor"], P.ctr["vector"]])
        P.t(lambda e: e.matmul(py0[:], ones1[:], row1[:, 0:512], start=True, stop=True),
            after=[P.ctr["vector"], P.dma])
        P.t(lambda e: e.matmul(py1[:], ones1[:], row1[:, 512:1024], start=True, stop=True))
        P.v(lambda e: e.tensor_copy(b_bc[:, 0:512], py0[:]), after=[P.ctr["tensor"]])
        P.v(lambda e: e.tensor_copy(b_bc[:, 512:1024], py1[:]))
        for c in range(8):
            for hf in range(4):
                P.ld(lambda e, c=c, hf=hf: e.dma_start(
                    out=wld[:], in_=UPT[c * 128:(c + 1) * 128, hf * 512:(hf + 1) * 512]),
                    after=ALL(P))
                P.v(lambda e, c=c, hf=hf: e.tensor_copy(
                    upt[:, c * 2048 + hf * 512:c * 2048 + hf * 512 + 512], wld[:]),
                    after=[P.dma])
        for c in range(8):
            for hf in range(4):
                P.ld(lambda e, c=c, hf=hf: e.dma_start(
                    out=wld[:], in_=GATT[c * 128:(c + 1) * 128, hf * 512:(hf + 1) * 512]),
                    after=ALL(P))
                P.v(lambda e, c=c, hf=hf: e.tensor_copy(
                    gatt[:, c * 2048 + hf * 512:c * 2048 + hf * 512 + 512], wld[:]),
                    after=[P.dma])
        for c in range(16):
            for hf in range(2):
                P.ld(lambda e, c=c, hf=hf: e.dma_start(
                    out=wld[:], in_=DWT[c * 128:(c + 1) * 128, hf * 512:(hf + 1) * 512]),
                    after=ALL(P))
                P.v(lambda e, c=c, hf=hf: e.tensor_copy(
                    dwt[:, c * 1024 + hf * 512:c * 1024 + hf * 512 + 512], wld[:]),
                    after=[P.dma])
            P.v(lambda e, c=c: e.tensor_scalar(dwt[:, c * 1024:(c + 1) * 1024],
                                               dwt[:, c * 1024:(c + 1) * 1024],
                                               nw[:, c:c + 1], 0.0, ALU.mult, ALU.add),
                after=[P.ctr["vector"]])
        for wi, W in enumerate([QWT, KWT, GWT, TWT]):
            for c in range(16):
                P.ld(lambda e, W=W, c=c: e.dma_start(out=wld[:, 0:64],
                                                     in_=W[c * 128:(c + 1) * 128, :]),
                     after=ALL(P))
                P.v(lambda e, wi=wi, c=c: e.tensor_copy(
                    qw4[:, c * 256 + wi * 64:c * 256 + wi * 64 + 64], wld[:, 0:64]),
                    after=[P.dma])
        for c in range(16):
            P.ld(lambda e, c=c: e.dma_start(out=wld[:, 0:16], in_=IG1T[c * 128:(c + 1) * 128, :]),
                 after=ALL(P))
            P.v(lambda e, c=c: e.tensor_copy(ig1[:, c * 16:(c + 1) * 16], wld[:, 0:16]),
                after=[P.dma])
        for hf in range(4):
            P.ld(lambda e, hf=hf: e.dma_start(out=wld[0:16, :],
                                              in_=IG2T[:, hf * 512:(hf + 1) * 512]),
                 after=ALL(P))
            P.v(lambda e, hf=hf: e.tensor_copy(ig2[:, hf * 512:(hf + 1) * 512],
                                               wld[0:16, :]), after=[P.dma])

        for tt in range(NT):
            R0 = tt * 256
            for c in range(16):
                P.v(lambda e, c=c: e.tensor_copy(xup[:, c * 259:c * 259 + 3],
                                                 tails[:, c * 3:(c + 1) * 3]),
                    after=([P.ctr["tensor"], P.ctr["scalar"], P.ctr["vector"]] if c == 0 else []))
            for half in range(2):
                P.ld(lambda e, R0=R0, half=half: e.dma_start(
                    out=xt[:], in_=XB[R0 + half * 128:R0 + half * 128 + 128, :]), after=ALL(P))
                P.v(lambda e: e.nop(), after=[P.dma])
                emit_ln(P, lnx, xt, yln, mcol, ssq, var, rstd, eps5)
                for c8 in range(8):
                    P.t(lambda e, c8=c8: e.transpose(ptrf[:, 0:128],
                                                     lnx[:, c8 * 128:(c8 + 1) * 128], idf[:]),
                        after=[P.ctr["vector"]])
                    P.v(lambda e, c8=c8, half=half: e.tensor_copy(
                        x_fm[:, c8 * 256 + half * 128:c8 * 256 + half * 128 + 128],
                        ptrf[:, 0:128]), after=[P.ctr["tensor"]])
            for c in range(16):
                for d in range(8):
                    P.t(lambda e, c=c, d=d: e.matmul(
                        pA[:], upt[:, d * 2048 + c * 128:d * 2048 + c * 128 + 128],
                        x_fm[:, d * 256:(d + 1) * 256],
                        start=(d == 0), stop=(d == 7)), after=[P.ctr["vector"]])
                P.v(lambda e, c=c: e.tensor_copy(xup[:, c * 259 + 3:c * 259 + 259], pA[:]),
                    after=[P.ctr["tensor"]])
                for d in range(8):
                    P.t(lambda e, c=c, d=d: e.matmul(
                        pA[:], gatt[:, d * 2048 + c * 128:d * 2048 + c * 128 + 128],
                        x_fm[:, d * 256:(d + 1) * 256],
                        start=(d == 0), stop=(d == 7)), after=[P.ctr["vector"]])
                P.s(lambda e, c=c: e.activation(gat[:, c * 256:(c + 1) * 256], pA[:], AF.Silu),
                    after=[P.ctr["tensor"], P.ctr["vector"]])
                P.v(lambda e, c=c: e.tensor_scalar(sig[:], xup[:, c * 259:c * 259 + 256],
                                                   cw[:, c * 4:c * 4 + 1], 0.0,
                                                   ALU.mult, ALU.add),
                    after=[P.ctr["vector"], P.ctr["scalar"]])
                for j in range(1, 4):
                    P.v(lambda e, c=c, j=j: e.scalar_tensor_tensor(
                        sig[:], xup[:, c * 259 + j:c * 259 + j + 256],
                        cw[:, c * 4 + j:c * 4 + j + 1], sig[:], ALU.mult, ALU.add))
                P.s(lambda e, c=c: e.activation(shf[:, c * 256:(c + 1) * 256], sig[:], AF.Silu,
                                                bias=cb[:, c:c + 1]),
                    after=[P.ctr["vector"]])
                P.v(lambda e, c=c: e.tensor_copy(tails[:, c * 3:(c + 1) * 3],
                                                 xup[:, c * 259 + 256:c * 259 + 259]))
            for wi, dst in enumerate([qf, kf, gf, tf]):
                for c in range(16):
                    P.t(lambda e, wi=wi, c=c: e.matmul(
                        po[0:64, 0:256], qw4[:, c * 256 + wi * 64:c * 256 + wi * 64 + 64],
                        shf[:, c * 256:(c + 1) * 256],
                        start=(c == 0), stop=(c == 15)),
                        after=[P.ctr["vector"], P.ctr["scalar"]])
                if wi < 2:
                    P.v(lambda e, dst=dst: e.tensor_copy(dst[:], po[0:64, 0:256]),
                        after=[P.ctr["tensor"]])
                else:
                    bias = gb if wi == 2 else tb
                    P.v(lambda e, dst=dst, bias=bias: e.tensor_scalar(
                        dst[:], po[0:64, 0:256], bias[:], 0.0, ALU.add, ALU.add),
                        after=[P.ctr["tensor"]])
            P.s(lambda e: e.activation(sc1[:], gf[:], AF.Abs), after=[P.ctr["vector"]])
            P.s(lambda e: e.activation(sc1[:], sc1[:], AF.Exp, scale=-1.0))
            P.v(lambda e: e.tensor_scalar(sc1[:], sc1[:], 1.0, 0.0, ALU.add, ALU.add),
                after=[P.ctr["scalar"]])
            P.s(lambda e: e.activation(sc1[:], sc1[:], AF.Ln), after=[P.ctr["vector"]])
            P.s(lambda e: e.activation(sc2[:], gf[:], AF.Relu))
            P.v(lambda e: e.tensor_tensor(gf[:], sc1[:], sc2[:], ALU.add),
                after=[P.ctr["scalar"]])
            P.s(lambda e: e.activation(tf[:], tf[:], AF.Sigmoid), after=[P.ctr["vector"]])
            P.v(lambda e: e.tensor_scalar(sc1[:], gf[:], 1e-18, 0.0, ALU.max, ALU.add),
                after=[P.ctr["scalar"]])
            P.s(lambda e: e.activation(sc1[:], sc1[:], AF.Ln), after=[P.ctr["vector"]])
            P.v(lambda e: e.tensor_tensor(sc1[:], sc1[:], tf[:], ALU.mult),
                after=[P.ctr["scalar"]])
            P.s(lambda e: e.activation(sc1[:], sc1[:], AF.Exp), after=[P.ctr["vector"]])
            P.v(lambda e: e.tensor_tensor(b0[:], gf[:], tf[:], ALU.mult),
                after=[P.ctr["scalar"]])
            P.v(lambda e: e.tensor_scalar(b0[:], b0[:], -1.0, 0.0, ALU.mult, ALU.add),
                after=[P.ctr["vector"]])
            P.s(lambda e: e.activation(sc2[:], kf[:], AF.Square), after=[P.ctr["vector"]])
            P.t(lambda e: e.matmul(ptrf[0:1, 0:256], onesc[0:64, :], sc2[:],
                                   start=True, stop=True),
                after=[P.ctr["scalar"], P.ctr["vector"]])
            P.s(lambda e: e.activation(mrow[:], ptrf[0:1, 0:256], AF.Sqrt),
                after=[P.ctr["tensor"]])
            P.v(lambda e: e.tensor_scalar(mrow[:], mrow[:], 1e-12, 0.0, ALU.max, ALU.add),
                after=[P.ctr["scalar"]])
            P.v(lambda e: e.reciprocal(mrow[:], mrow[:]), after=[P.ctr["vector"]])
            P.v(lambda e: e.nop(), after=[P.ctr["vector"]])
            P.t(lambda e: e.matmul(ptrf[0:64, 0:256], ones1[:, 0:64], mrow[:],
                                   start=True, stop=True), after=[P.ctr["vector"]])
            P.v(lambda e: e.tensor_tensor(kf[:], kf[:], ptrf[0:64, 0:256], ALU.mult),
                after=[P.ctr["tensor"]])
            P.v(lambda e: e.tensor_tensor(kf[:], kf[:], sc1[:], ALU.mult),
                after=[P.ctr["vector"]])
            src_, dst_ = b0, b1
            for st in [1, 2, 4, 8, 16, 32]:
                s3 = src_[:].rearrange("k (c t) -> k c t", t=64)
                d3 = dst_[:].rearrange("k (c t) -> k c t", t=64)
                P.v(lambda e, s3=s3, d3=d3, st=st: e.tensor_copy(d3[:, :, 0:st],
                                                                 s3[:, :, 0:st]),
                    after=[P.ctr["vector"]])
                P.v(lambda e, s3=s3, d3=d3, st=st: e.tensor_tensor(
                    d3[:, :, st:64], s3[:, :, st:64], s3[:, :, 0:64 - st], ALU.add))
                src_, dst_ = dst_, src_
            bb = src_
            P.v(lambda e: e.nop(), after=[P.ctr["vector"]])
            P.s(lambda e, bb=bb: e.activation(sc2[:], bb[:], AF.Exp),
                after=[P.ctr["vector"], P.ctr["tensor"]])
            P.s(lambda e, bb=bb: e.activation(sc3[:], bb[:], AF.Exp, scale=-1.0))
            P.v(lambda e: e.tensor_scalar(qf[:], qf[:], 0.125, 0.0, ALU.mult, ALU.add),
                after=[P.ctr["scalar"]])
            P.v(lambda e: e.tensor_tensor(qe[:], qf[:], sc2[:], ALU.mult),
                after=[P.ctr["vector"]])
            P.v(lambda e: e.tensor_tensor(ke[:], kf[:], sc3[:], ALU.mult))
            b3 = bb[:].rearrange("k (c t) -> k c t", t=64)
            P.s(lambda e, b3=b3: e.activation(ebt[:], b3[:, :, 63], AF.Exp),
                after=[P.ctr["vector"]])
            for half in range(2):
                for c in range(16):
                    P.t(lambda e, half=half, c=c: e.matmul(
                        pA[0:128, 0:16],
                        xup[:, c * 259 + 3 + half * 128:c * 259 + 3 + half * 128 + 128],
                        ig1[:, c * 16:(c + 1) * 16],
                        start=(c == 0), stop=(c == 15)),
                        after=[P.ctr["vector"], P.ctr["scalar"]])
                P.v(lambda e: e.tensor_copy(pre1s[:], pA[0:128, 0:16]), after=[P.ctr["tensor"]])
                P.v(lambda e: e.nop(), after=[P.ctr["vector"]])
                P.t(lambda e: e.transpose(ptrf[0:16, 0:128], pre1s[:], idf[:]),
                    after=[P.ctr["vector"]])
                P.v(lambda e, half=half: e.tensor_copy(pre1t[:, half * 128:half * 128 + 128],
                                                       ptrf[0:16, 0:128]),
                    after=[P.ctr["tensor"]])
            for c in range(16):
                P.t(lambda e, c=c: e.matmul(pA[:], ig2[:, c * 128:(c + 1) * 128], pre1t[:],
                                            start=True, stop=True),
                    after=[P.ctr["vector"], P.ctr["scalar"]])
                P.s(lambda e, c=c: e.activation(sig[:], pA[:], AF.Sigmoid,
                                                bias=igb2[:, c:c + 1]),
                    after=[P.ctr["tensor"], P.ctr["vector"]])
                P.v(lambda e, c=c: e.tensor_tensor(xup[:, c * 259 + 3:c * 259 + 259],
                                                   sig[:], xup[:, c * 259 + 3:c * 259 + 259],
                                                   ALU.mult),
                    after=[P.ctr["scalar"]])
            for ch in range(4):
                C0 = ch * 64
                for c in range(16):
                    P.t(lambda e, c=c, C0=C0: e.transpose(
                        ptr[0:64, :], xup[:, c * 259 + 3 + C0:c * 259 + 3 + C0 + 64],
                        idb[:]), after=[P.ctr["vector"]])
                    P.v(lambda e, c=c: e.tensor_copy(vt[:, c * 128:(c + 1) * 128],
                                                     ptr[0:64, 0:128]),
                        after=[P.ctr["tensor"]])
                P.v(lambda e, C0=C0: e.tensor_copy(keS[:], ke[:, C0:C0 + 64]),
                    after=[P.ctr["vector"]])
                P.v(lambda e: e.nop(), after=[P.ctr["vector"]])
                P.t(lambda e: e.transpose(ptr[0:64, 0:64], keS[:], idb[0:64, 0:64]),
                    after=[P.ctr["vector"]])
                P.v(lambda e: e.tensor_copy(keT[:], ptr[0:64, 0:64]), after=[P.ctr["tensor"]])
                P.t(lambda e, C0=C0: e.matmul(ptrf[0:64, 0:64], ke[:, C0:C0 + 64],
                                              qe[:, C0:C0 + 64], start=True, stop=True),
                    after=[P.ctr["vector"]])
                P.v(lambda e: e.tensor_tensor(ATm[:], ptrf[0:64, 0:64], m01f[:], ALU.mult),
                    after=[P.ctr["tensor"]])
                P.v(lambda e: e.nop(), after=[P.ctr["vector"]])
                for vtl in range(4):
                    V0 = vtl * 512
                    P.t(lambda e, C0=C0, V0=V0: e.matmul(
                        po[:], qe[:, C0:C0 + 64], Sb[:, V0:V0 + 512],
                        start=True, stop=False), after=[P.ctr["vector"]])
                    P.t(lambda e, V0=V0: e.matmul(
                        po[:], ATm[:], vt[:, V0:V0 + 512], start=False, stop=True))
                    P.t(lambda e, C0=C0, V0=V0: e.matmul(
                        pkv[:], keT[:], vt[:, V0:V0 + 512], start=True, stop=True))
                    P.v(lambda e, V0=V0: e.tensor_copy(ovb[:], po[:]), after=[P.ctr["tensor"]])
                    P.v(lambda e, V0=V0: e.tensor_tensor(S[:, V0:V0 + 512], S[:, V0:V0 + 512],
                                                         pkv[:], ALU.add))
                    P.v(lambda e, V0=V0, ch=ch: e.tensor_scalar(
                        S[:, V0:V0 + 512], S[:, V0:V0 + 512], ebt[:, ch:ch + 1], 0.0,
                        ALU.mult, ALU.add), after=[P.ctr["vector"]])
                    P.v(lambda e, V0=V0: e.tensor_copy(Sb[:, V0:V0 + 512], S[:, V0:V0 + 512]),
                        after=[P.ctr["vector"]])
                    for q4 in range(4):
                        P.t(lambda e, q4=q4: e.transpose(
                            ptr[0:128, 0:64], ovb[:, q4 * 128:(q4 + 1) * 128],
                            idb[0:64, 0:64]), after=[P.ctr["vector"]])
                        P.v(lambda e, q4=q4, V0=V0, C0=C0: e.tensor_copy(
                            ofm[:, (V0 // 128 + q4) * 256 + C0:(V0 // 128 + q4) * 256 + C0 + 64],
                            ptr[0:128, 0:64]), after=[P.ctr["tensor"]])
            if dbg and tt == 0:
                P.barrier()
                for c in range(16):
                    P.v(lambda e, c=c: e.tensor_copy(lnx[:, 0:256], ofm[:, c * 256:(c + 1) * 256]),
                        after=[P.ctr["vector"], P.dmo])
                    P.st(lambda e, c=c: e.dma_start(out=DOF[:, c * 256:(c + 1) * 256],
                                                    in_=lnx[:, 0:256]), after=[P.ctr["vector"]])
                    P.v(lambda e, c=c: e.tensor_copy(lnx[:, 0:256], shf[:, c * 256:(c + 1) * 256]),
                        after=[P.ctr["vector"], P.dmo])
                    P.st(lambda e, c=c: e.dma_start(out=DSH[:, c * 256:(c + 1) * 256],
                                                    in_=lnx[:, 0:256]), after=[P.ctr["vector"]])
                    P.v(lambda e, c=c: e.tensor_copy(lnx[:, 0:256],
                                                     xup[:, c * 259 + 3:c * 259 + 259]),
                        after=[P.ctr["vector"], P.dmo])
                    P.st(lambda e, c=c: e.dma_start(out=DXU[:, c * 256:(c + 1) * 256],
                                                    in_=lnx[:, 0:256]), after=[P.ctr["vector"]])
                for wi, srcq in enumerate([qf, kf, gf, tf]):
                    P.v(lambda e, srcq=srcq: e.tensor_copy(sc2[:], srcq[:]),
                        after=[P.ctr["vector"], P.dmo])
                    P.st(lambda e, wi=wi: e.dma_start(out=DQ4[:, wi * 256:(wi + 1) * 256],
                                                      in_=sc2[:]), after=[P.ctr["vector"]])
                P.v(lambda e, bb=bb: e.tensor_copy(sc2[:], bb[:]),
                    after=[P.ctr["vector"], P.dmo])
                P.st(lambda e: e.dma_start(out=DB, in_=sc2[:]), after=[P.ctr["vector"]])
                P.barrier()
            for c in range(16):
                P.v(lambda e, c=c: e.scalar_tensor_tensor(
                    ofm[:, c * 256:(c + 1) * 256],
                    shf[:, c * 256:(c + 1) * 256], rw[:, c:c + 1],
                    ofm[:, c * 256:(c + 1) * 256], ALU.mult, ALU.add),
                    after=[P.ctr["vector"], P.ctr["tensor"]] if c == 0 else [])
                P.v(lambda e, c=c: e.tensor_tensor(ofm[:, c * 256:(c + 1) * 256],
                                                   ofm[:, c * 256:(c + 1) * 256],
                                                   gat[:, c * 256:(c + 1) * 256], ALU.mult))
            P.v(lambda e: e.nop(), after=[P.ctr["vector"]])
            if dbg and tt == 0:
                P.barrier()
                for c in range(16):
                    P.v(lambda e, c=c: e.tensor_copy(lnx[:, 0:256], ofm[:, c * 256:(c + 1) * 256]),
                        after=[P.ctr["vector"], P.dmo])
                    P.st(lambda e, c=c: e.dma_start(out=DXG[:, c * 256:(c + 1) * 256],
                                                    in_=lnx[:, 0:256]), after=[P.ctr["vector"]])
                P.barrier()
            for c in range(16):
                P.s(lambda e, c=c: e.activation(sig[:], ofm[:, c * 256:(c + 1) * 256],
                                                AF.Square),
                    after=[P.ctr["vector"], P.ctr["tensor"]])
                P.t(lambda e, c=c: e.matmul(ptrf[0:1, 0:256], onesc[:], sig[:],
                                            start=(c == 0), stop=(c == 15)),
                    after=[P.ctr["scalar"]])
            P.s(lambda e: e.activation(mrow[:], ptrf[0:1, 0:256], AF.Sqrt, scale=1.0 / 2048,
                                       bias=eps5[0:1, :]), after=[P.ctr["tensor"]])
            P.v(lambda e: e.reciprocal(mrow[:], mrow[:]), after=[P.ctr["scalar"]])
            P.v(lambda e: e.nop(), after=[P.ctr["vector"]])
            P.t(lambda e: e.matmul(ptrf[:, 0:256], ones1[:], mrow[:], start=True, stop=True),
                after=[P.ctr["vector"]])
            P.v(lambda e: e.tensor_copy(mbc[:], ptrf[:, 0:256]), after=[P.ctr["tensor"]])
            P.v(lambda e: e.nop(), after=[P.ctr["vector"]])
            for c in range(16):
                P.v(lambda e, c=c: e.tensor_tensor(ofm[:, c * 256:(c + 1) * 256],
                                                   ofm[:, c * 256:(c + 1) * 256],
                                                   mbc[:], ALU.mult))
            P.v(lambda e: e.nop(), after=[P.ctr["vector"]])
            if dbg and tt == 0:
                P.barrier()
                P.v(lambda e: e.tensor_copy(mrow[:], mrow[:]), after=[P.dmo])
                P.st(lambda e: e.dma_start(out=DMB, in_=mrow[:]), after=[P.ctr["vector"]])
                for c in range(16):
                    P.v(lambda e, c=c: e.tensor_copy(lnx[:, 0:256], ofm[:, c * 256:(c + 1) * 256]),
                        after=[P.ctr["vector"], P.dmo])
                    P.st(lambda e, c=c: e.dma_start(out=DON[:, c * 256:(c + 1) * 256],
                                                    in_=lnx[:, 0:256]), after=[P.ctr["vector"]])
                P.barrier()
            for half in range(2):
                for c in range(16):
                    P.t(lambda e, half=half, c=c: e.matmul(
                        py0[:], ofm[:, c * 256 + half * 128:c * 256 + half * 128 + 128],
                        dwt[:, c * 1024:c * 1024 + 512],
                        start=(c == 0), stop=(c == 15)), after=[P.ctr["vector"]])
                    P.t(lambda e, half=half, c=c: e.matmul(
                        py1[:], ofm[:, c * 256 + half * 128:c * 256 + half * 128 + 128],
                        dwt[:, c * 1024 + 512:c * 1024 + 1024],
                        start=(c == 0), stop=(c == 15)))
                P.ld(lambda e, R0=R0, half=half: e.dma_start(
                    out=xt[:], in_=XB[R0 + half * 128:R0 + half * 128 + 128, :]),
                    after=ALL(P))
                yss = ys[:, 0:1024]
                P.v(lambda e: e.nop(), after=[P.dma, P.ctr["tensor"], P.dmo])
                P.v(lambda e, yss=yss: e.tensor_tensor(yss[:, 0:512], py0[:], xt[:, 0:512],
                                                       ALU.add))
                P.v(lambda e, yss=yss: e.tensor_tensor(yss[:, 512:1024], py1[:],
                                                       xt[:, 512:1024], ALU.add))
                P.v(lambda e: e.nop(), after=[P.ctr["vector"]])
                emit_ln(P, yln, yss, lnx, mcol, ssq, var, rstd, eps5)
                P.v(lambda e: e.tensor_tensor(yln[:], yln[:], g_bc[:], ALU.mult),
                    after=[P.ctr["vector"]])
                P.v(lambda e: e.tensor_tensor(yln[:], yln[:], b_bc[:], ALU.add))
                P.v(lambda e, yss=yss: e.tensor_tensor(yss[:], yss[:], yln[:], ALU.add),
                    after=[P.ctr["vector"]])
                P.st(lambda e, R0=R0, half=half, yss=yss: e.dma_start(
                    out=Y[R0 + half * 128:R0 + half * 128 + 128, :], in_=yss[:]),
                    after=[P.ctr["vector"]])
        P.finish()
        P.play(block)
    return nc


def run_gla(x, d):
    def cs(a):
        return np.ascontiguousarray(a.astype(np.float32))
    pre = "L0_attn_b_"
    cw = d[pre + "conv_w"].reshape(16, 128, 4).transpose(1, 0, 2).reshape(128, 64)
    im = {"UPT": cs(d[pre + "up_w"].T), "GATT": cs(d[pre + "gate_w"].T),
          "DWT": cs(d[pre + "down_w"].T),
          "QWT": cs(d[pre + "q_w"].T), "KWT": cs(d[pre + "k_w"].T),
          "GWT": cs(d[pre + "g_w"].T), "TWT": cs(d[pre + "tau_w"].T),
          "IG1T": cs(d[pre + "ig_w1"].T), "IG2T": cs(d[pre + "ig_w2"].T),
          "CW": cs(cw), "CB": cs(d[pre + "conv_b"].reshape(16, 128).T),
          "IGB2": cs(d[pre + "ig_b2"].reshape(16, 128).T),
          "RW": cs(d[pre + "residual_weight"].reshape(16, 128).T),
          "NW": cs(d[pre + "norm_weight"].reshape(16, 128).T),
          "GB": cs(d[pre + "g_b"].reshape(64, 1)), "TB": cs(d[pre + "tau_b"].reshape(64, 1)),
          "LNG": cs(d["L0_ffn_ln_g"].reshape(1, 1024)),
          "LNB": cs(d["L0_ffn_ln_b"].reshape(1, 1024)),
          "IDB": ID128B, "IDF": ID128F}
    in_maps = []
    for c in range(8):
        m = dict(im)
        m["XB"] = np.ascontiguousarray(x[c // 2])
        in_maps.append(m)
    res = run_bass_kernel_spmd(get_nc("gla"), in_maps, core_ids=list(range(8)))
    return np.stack([res.results[2 * b]["Y"] for b in range(4)], 0)




def build_geg():
    nc = bass.Bass("TRN2", target_bir_lowering=False)
    XF = nc.dram_tensor("XF", [4096, 1024], F32, kind="ExternalInput").ap()
    XQ = nc.dram_tensor("XQ", [2048, 1024], F32, kind="ExternalInput").ap()
    WQT = nc.dram_tensor("WQT", [1024, 1024], F32, kind="ExternalInput").ap()
    WKT = nc.dram_tensor("WKT", [1024, 1024], F32, kind="ExternalInput").ap()
    WVT = nc.dram_tensor("WVT", [1024, 1024], F32, kind="ExternalInput").ap()
    WOT = nc.dram_tensor("WOT", [1024, 1024], F32, kind="ExternalInput").ap()
    NWT = nc.dram_tensor("NWT", [128, 8], F32, kind="ExternalInput").ap()
    PW = nc.dram_tensor("PW", [16, 9], F32, kind="ExternalInput").ap()
    HD = nc.dram_tensor("HD", [16, 1], F32, kind="ExternalInput").ap()
    GEGM = nc.dram_tensor("GEGM", [9, 9], F32, kind="ExternalInput").ap()
    QOFF = nc.dram_tensor("QOFF", [1, 16], F32, kind="ExternalInput").ap()
    IDB = nc.dram_tensor("IDB", [128, 128], BF16, kind="ExternalInput").ap()
    IDF = nc.dram_tensor("IDF", [128, 128], F32, kind="ExternalInput").ap()
    Y = nc.dram_tensor("Y", [2048, 1024], F32, kind="ExternalOutput").ap()
    CSC = nc.dram_tensor("CSC", [16, 9], F32, kind="Internal").ap()
    with ExitStack() as ctx:
        E = ctx.enter_context
        P = Prog(nc, ctx)
        x_fm = E(nc.sbuf_tensor("x_fm", [128, 8 * 4096], BF16))
        xq_fm = E(nc.sbuf_tensor("xq_fm", [128, 8 * 2048], BF16))
        att_fm = E(nc.sbuf_tensor("att_fm", [128, 8 * 2048], BF16))
        wot = E(nc.sbuf_tensor("wot", [128, 8 * 1024], BF16))
        wpr = E(nc.sbuf_tensor("wpr", [128, 3 * 1024], BF16))
        wld = E(nc.sbuf_tensor("wld", [128, 1024], F32))
        kP = E(nc.sbuf_tensor("kP", [128, 4096], BF16))
        qP = E(nc.sbuf_tensor("qP", [128, 2048], BF16))
        vP = E(nc.sbuf_tensor("vP", [128, 32 * 128], BF16))
        msks = E(nc.sbuf_tensor("msks", [128, 32 * 128], BF16))
        kraw = E(nc.sbuf_tensor("kraw", [128, 512], F32))
        sq = E(nc.sbuf_tensor("sq", [128, 512], F32))
        rn2 = E(nc.sbuf_tensor("rn2", [2, 512], F32))
        onesAB = E(nc.sbuf_tensor("onesAB", [128, 2], F32))
        selAB = E(nc.sbuf_tensor("selAB", [2, 128], F32))
        selI = E(nc.sbuf_tensor("selI", [2, 128], I32))
        sb = E(nc.sbuf_tensor("sb", [128, 512], BF16))
        acc = E(nc.sbuf_tensor("acc", [128, 512], BF16))
        cb = E(nc.sbuf_tensor("cb", [128, 144], F32))
        cwk = E(nc.sbuf_tensor("cwk", [16, 16], F32))
        qbc = E(nc.sbuf_tensor("qbc", [128, 4], F32))
        dvc = E(nc.sbuf_tensor("dvc", [128, 4], F32))
        ioc = E(nc.sbuf_tensor("ioc", [128, 1], I32))
        iocf = E(nc.sbuf_tensor("iocf", [128, 1], F32))
        ioi = E(nc.sbuf_tensor("ioi", [128, 128], I32))
        ioif = E(nc.sbuf_tensor("ioif", [128, 128], F32))
        ao = E(nc.sbuf_tensor("ao", [128, 64], F32))
        aob = E(nc.sbuf_tensor("aob", [128, 64], BF16))
        xt = E(nc.sbuf_tensor("xt", [128, 1024], F32))
        lnx = E(nc.sbuf_tensor("lnx", [128, 1024], F32))
        lnb_ = E(nc.sbuf_tensor("lnb_", [128, 512], BF16))
        mcol = E(nc.sbuf_tensor("mcol", [128, 1], F32))
        ssq = E(nc.sbuf_tensor("ssq", [128, 1], F32))
        var = E(nc.sbuf_tensor("var", [128, 1], F32))
        rstd = E(nc.sbuf_tensor("rstd", [128, 1], F32))
        eps5 = E(nc.sbuf_tensor("eps5", [128, 1], F32))
        eps6 = E(nc.sbuf_tensor("eps6", [128, 1], F32))
        nwc = E(nc.sbuf_tensor("nwc", [128, 8], F32))
        qoffs = E(nc.sbuf_tensor("qoffs", [1, 16], F32))
        ones1 = E(nc.sbuf_tensor("ones1", [1, 128], F32))
        ys = E(nc.sbuf_tensor("ys", [128, 1024], F32))
        idb = E(nc.sbuf_tensor("idb", [128, 128], BF16))
        idf = E(nc.sbuf_tensor("idf", [128, 128], F32))
        psA = E(nc.psum_tensor("psA", [128, 512], F32))
        pn = E(nc.psum_tensor("pn", [2, 512], F32))
        poQ = [E(nc.psum_tensor("poq%d" % i, [128, 64], F32)) for i in range(4)]
        ptr = E(nc.psum_tensor("ptr", [128, 128], BF16))
        ptrf = E(nc.psum_tensor("ptrf", [128, 512], F32))
        py0 = psA
        py1 = ptrf
        block = E(nc.Block())

        P.ld(lambda e: e.dma_start(out=idb[:], in_=IDB))
        P.ld(lambda e: e.dma_start(out=idf[:], in_=IDF))
        P.ld(lambda e: e.dma_start(out=nwc[:], in_=NWT))
        P.ld(lambda e: e.dma_start(out=qoffs[:], in_=QOFF))
        P.ld(lambda e: e.dma_start(out=cwk[0:16, 0:9], in_=PW))
        P.ld(lambda e: e.dma_start(out=cwk[0:16, 9:10], in_=HD))
        P.ld(lambda e: e.dma_start(out=wld[0:9, 0:9], in_=GEGM))
        P.v(lambda e: e.memset(eps5[:], 1e-5), after=[P.dma])
        P.v(lambda e: e.memset(eps6[:], 1e-6))
        P.v(lambda e: e.memset(ones1[:], 1.0))
        P.v(lambda e: e.memset(onesAB[:], 0.0))
        P.v(lambda e: e.memset(onesAB[0:64, 0:1], 1.0))
        P.v(lambda e: e.memset(onesAB[64:128, 1:2], 1.0))
        P.g(lambda e: e.iota(selI[:], [[1, 128]], base=0, channel_multiplier=-64),
            after=[P.dma])
        P.v(lambda e: e.tensor_copy(ioif[0:2, :], selI[:]), after=[P.ctr["gpsimd"]])
        P.v(lambda e: e.tensor_scalar(selAB[:], ioif[0:2, :], 0.0, 0.0, ALU.is_ge, ALU.add),
            after=[P.ctr["vector"]])
        P.v(lambda e: e.scalar_tensor_tensor(selAB[:], ioif[0:2, :], 63.0, selAB[:],
                                             ALU.is_le, ALU.mult),
            after=[P.ctr["vector"]])
        P.g(lambda e: e.iota(ioi[0:16, 0:9], [[1, 9]], base=0, channel_multiplier=0),
            after=[P.dma])
        P.v(lambda e: e.tensor_copy(ioif[0:16, 0:9], ioi[0:16, 0:9]), after=[P.ctr["gpsimd"]])
        P.v(lambda e: e.tensor_scalar(cwk[0:16, 10:11], cwk[0:16, 9:10], 1.0, 0.0,
                                      ALU.add, ALU.add), after=[P.ctr["vector"]])
        P.v(lambda e: e.tensor_scalar(ioif[0:16, 0:9], ioif[0:16, 0:9], -1.0,
                                      cwk[0:16, 10:11], ALU.mult, ALU.add),
            after=[P.ctr["vector"]])
        P.v(lambda e: e.tensor_scalar(ioif[0:16, 0:9], ioif[0:16, 0:9], 0.0, 1.0,
                                      ALU.max, ALU.min), after=[P.ctr["vector"]])
        P.v(lambda e: e.tensor_tensor(ioif[0:16, 0:9], ioif[0:16, 0:9], cwk[0:16, 0:9],
                                      ALU.mult), after=[P.ctr["vector"]])
        P.v(lambda e: e.nop(), after=[P.ctr["vector"]])
        P.t(lambda e: e.transpose(ptrf[0:9, 0:16], ioif[0:16, 0:9], idf[0:16, 0:16]),
            after=[P.ctr["vector"]])
        P.v(lambda e: e.tensor_copy(ioif[0:9, 16:32], ptrf[0:9, 0:16]), after=[P.ctr["tensor"]])
        P.v(lambda e: e.nop(), after=[P.ctr["vector"]])
        P.t(lambda e: e.matmul(ptrf[0:16, 0:9], ioif[0:9, 16:32], wld[0:9, 0:9],
                               start=True, stop=True), after=[P.ctr["vector"]])
        P.v(lambda e: e.tensor_copy(ioif[0:16, 32:41], ptrf[0:16, 0:9]), after=[P.ctr["tensor"]])
        P.st(lambda e: e.dma_start(out=CSC, in_=ioif[0:16, 32:41]),
             after=[P.ctr["vector"]])
        P.ld(lambda e: e.dma_start(out=wld[0:1, 0:144],
                                   in_=CSC.rearrange("(o h) m -> o (h m)", o=1)), after=[P.dmo])
        P.t(lambda e: e.matmul(ptrf[:, 0:144], ones1[:], wld[0:1, 0:144],
                               start=True, stop=True), after=[P.dma, P.ctr["vector"]])
        P.v(lambda e: e.tensor_copy(cb[:], ptrf[:, 0:144]), after=[P.ctr["tensor"]])
        for jt in range(32):
            P.ld(lambda e, jt=jt: e.dma_start(out=xt[:], in_=XF[jt * 128:(jt + 1) * 128, :]),
                 after=ALL(P))
            P.v(lambda e: e.nop(), after=[P.dma])
            emit_ln(P, lnx, xt, wld, mcol, ssq, var, rstd, eps5)
            for hf in range(2):
                P.v(lambda e, hf=hf: e.tensor_copy(lnb_[:], lnx[:, hf * 512:(hf + 1) * 512]),
                    after=[P.ctr["vector"], P.ctr["tensor"]])
                emit_tr(P, (lambda c, jt=jt, hf=hf: x_fm[:, (hf * 4 + c) * 4096 + jt * 128:
                                                         (hf * 4 + c) * 4096 + jt * 128 + 128]),
                        lnb_, ptr, idb, 4)
        for jt in range(16):
            P.ld(lambda e, jt=jt: e.dma_start(out=xt[:], in_=XQ[jt * 128:(jt + 1) * 128, :]),
                 after=ALL(P))
            P.v(lambda e: e.nop(), after=[P.dma])
            emit_ln(P, lnx, xt, wld, mcol, ssq, var, rstd, eps5)
            for hf in range(2):
                P.v(lambda e, hf=hf: e.tensor_copy(lnb_[:], lnx[:, hf * 512:(hf + 1) * 512]),
                    after=[P.ctr["vector"], P.ctr["tensor"]])
                emit_tr(P, (lambda c, jt=jt, hf=hf: xq_fm[:, (hf * 4 + c) * 2048 + jt * 128:
                                                          (hf * 4 + c) * 2048 + jt * 128 + 128]),
                        lnb_, ptr, idb, 4)
        for c in range(8):
            P.ld(lambda e, c=c: e.dma_start(out=wld[:], in_=WOT[c * 128:(c + 1) * 128, :]),
                 after=ALL(P))
            P.v(lambda e, c=c: e.tensor_copy(wot[:, c * 1024:(c + 1) * 1024], wld[:]),
                after=[P.dma])
            P.v(lambda e, c=c: e.tensor_scalar(wot[:, c * 1024:(c + 1) * 1024],
                                               wot[:, c * 1024:(c + 1) * 1024],
                                               nwc[:, c:c + 1], 0.0, ALU.mult, ALU.add),
                after=[P.ctr["vector"]])
        for hp in range(8):
            for wi, W in enumerate([WQT, WKT, WVT]):
                P.ld(lambda e, wi=wi, W=W, hp=hp: e.dma_start(
                    out=wld[:].rearrange("p (c n) -> p c n", n=128),
                    in_=W[:, hp * 128:(hp + 1) * 128].rearrange("(c p) n -> p c n", p=128)),
                    after=ALL(P))
                P.v(lambda e, wi=wi: e.tensor_copy(wpr[:, wi * 1024:(wi + 1) * 1024], wld[:]),
                    after=[P.dma])
            for which, n5, src_fm, dstP in [(1, 8, x_fm, kP), (0, 4, xq_fm, qP)]:
                W0 = 4096 if which == 1 else 2048
                for j5 in range(n5):
                    for c in range(8):
                        P.t(lambda e, which=which, j5=j5, c=c, src_fm=src_fm, W0=W0: e.matmul(
                            psA[:], wpr[:, which * 1024 + c * 128:which * 1024 + c * 128 + 128],
                            src_fm[:, c * W0 + j5 * 512: c * W0 + j5 * 512 + 512],
                            start=(c == 0), stop=(c == 7)), after=[P.ctr["vector"]])
                    P.v(lambda e: e.tensor_copy(kraw[:], psA[:]), after=[P.ctr["tensor"]])
                    P.s(lambda e: e.activation(sq[:], kraw[:], AF.Square),
                        after=[P.ctr["vector"]])
                    P.t(lambda e: e.matmul(pn[:], onesAB[:], sq[:], start=True, stop=True),
                        after=[P.ctr["scalar"]])
                    P.s(lambda e: e.activation(rn2[:], pn[:], AF.Sqrt), after=[P.ctr["tensor"]])
                    P.v(lambda e: e.tensor_scalar(rn2[:], rn2[:], 1e-12, 0.0, ALU.max, ALU.add),
                        after=[P.ctr["scalar"]])
                    P.v(lambda e: e.reciprocal(rn2[:], rn2[:]), after=[P.ctr["vector"]])
                    P.v(lambda e: e.nop(), after=[P.ctr["vector"]])
                    P.t(lambda e: e.matmul(ptrf[:], selAB[:], rn2[:],
                                           start=True, stop=True), after=[P.ctr["vector"]])
                    P.v(lambda e, dstP=dstP, j5=j5: e.tensor_tensor(
                        dstP[:, j5 * 512:(j5 + 1) * 512], kraw[:], ptrf[:], ALU.mult),
                        after=[P.ctr["tensor"]])
            for jt in range(32):
                for c in range(8):
                    P.t(lambda e, jt=jt, c=c: e.matmul(
                        psA[:, 0:128], x_fm[:, c * 4096 + jt * 128: c * 4096 + jt * 128 + 128],
                        wpr[:, 2048 + c * 128: 2048 + c * 128 + 128],
                        start=(c == 0), stop=(c == 7)), after=[P.ctr["vector"]])
                P.v(lambda e, jt=jt: e.tensor_copy(vP[:, jt * 128:(jt + 1) * 128],
                                                   psA[:, 0:128]), after=[P.ctr["tensor"]])
            for st in range(4):
                nch = 8 * st + 8
                for q4 in range(4):
                    P.t(lambda e, st=st, q4=q4: e.matmul(
                        ptrf[:, 0:1], ones1[:], qoffs[:, st * 4 + q4:st * 4 + q4 + 1],
                        start=True, stop=True), after=[P.ctr["vector"], P.ctr["scalar"]])
                    P.v(lambda e, q4=q4: e.tensor_copy(qbc[:, q4:q4 + 1], ptrf[:, 0:1]),
                        after=[P.ctr["tensor"]])
                P.g(lambda e: e.iota(ioc[:], [[0, 1]], base=1, channel_multiplier=1),
                    after=[P.ctr["vector"]])
                P.v(lambda e: e.tensor_copy(iocf[:], ioc[:]), after=[P.ctr["gpsimd"]])
                for q4 in range(4):
                    P.v(lambda e, q4=q4: e.scalar_tensor_tensor(
                        dvc[:, q4:q4 + 1], qbc[:, q4:q4 + 1], 1.0, iocf[:],
                        ALU.mult, ALU.add), after=[P.ctr["vector"]])
                P.v(lambda e: e.reciprocal(dvc[:], dvc[:]), after=[P.ctr["vector"]])
                for mi in range(8):
                    cjj = 8 * st + mi
                    P.g(lambda e, cjj=cjj: e.iota(ioi[:], [[-1, 128]], base=cjj * 128,
                                                  channel_multiplier=1),
                        after=[P.ctr["vector"]])
                    P.v(lambda e: e.tensor_copy(ioif[:], ioi[:]), after=[P.ctr["gpsimd"]])
                    P.v(lambda e: e.nop(), after=[P.ctr["vector"]])
                    for q4 in range(4):
                        P.v(lambda e, mi=mi, q4=q4: e.tensor_scalar(
                            msks[:, (mi * 4 + q4) * 128:(mi * 4 + q4) * 128 + 128],
                            ioif[:], qbc[:, q4:q4 + 1], 0.0, ALU.is_le, ALU.add),
                            after=[P.ctr["vector"]])
                for hh in range(2):
                    H0 = hh * 64
                    for cjj in range(nch):
                        P.t(lambda e, cjj=cjj, H0=H0, st=st: e.matmul(
                            psA[:], kP[H0:H0 + 64, cjj * 128:(cjj + 1) * 128],
                            qP[H0:H0 + 64, st * 512:(st + 1) * 512],
                            start=True, stop=True),
                            after=[P.ctr["vector"], P.ctr["scalar"]])
                        P.s(lambda e: e.activation(sb[:], psA[:], AF.Copy),
                            after=[P.ctr["tensor"], P.ctr["vector"]])
                        hb = (hp * 2 + hh) * 9
                        P.v(lambda e, hb=hb: e.tensor_scalar(
                            acc[:], sb[:], cb[:, hb + 8:hb + 9], 0.0, ALU.mult, ALU.add),
                            after=[P.ctr["scalar"], P.ctr["tensor"]])
                        for m in range(7, 0, -1):
                            P.v(lambda e, hb=hb, m=m: e.scalar_tensor_tensor(
                                acc[:], acc[:], cb[:, hb + m:hb + m + 1], sb[:],
                                ALU.add, ALU.mult))
                        P.v(lambda e, hb=hb: e.tensor_scalar(
                            acc[:], acc[:], cb[:, hb:hb + 1], 0.0, ALU.add, ALU.add))
                        if cjj >= 8 * st:
                            mi = cjj - 8 * st
                            for q4 in range(4):
                                P.v(lambda e, mi=mi, q4=q4: e.tensor_tensor(
                                    acc[:, q4 * 128:(q4 + 1) * 128],
                                    acc[:, q4 * 128:(q4 + 1) * 128],
                                    msks[:, (mi * 4 + q4) * 128:(mi * 4 + q4) * 128 + 128],
                                    ALU.mult))
                        P.v(lambda e: e.nop(), after=[P.ctr["vector"]])
                        for q4 in range(4):
                            P.t(lambda e, cjj=cjj, q4=q4, H0=H0, nch=nch: e.matmul(
                                poQ[q4][:],
                                acc[:, q4 * 128:(q4 + 1) * 128],
                                vP[:, cjj * 128 + H0:cjj * 128 + H0 + 64],
                                start=(cjj == 0), stop=(cjj == nch - 1)),
                                after=[P.ctr["vector"]] if q4 == 0 else [])
                    for q4 in range(4):
                        P.v(lambda e, q4=q4: e.tensor_scalar(
                            ao[:], poQ[q4][:], dvc[:, q4:q4 + 1], 0.0,
                            ALU.mult, ALU.add), after=[P.ctr["tensor"], P.ctr["vector"]])
                        P.s(lambda e: e.activation(sq[:, 0:64], ao[:], AF.Square,
                                                   accum_out=ssq[:]), after=[P.ctr["vector"]])
                        P.s(lambda e: e.activation(rstd[:], ssq[:], AF.Sqrt, scale=1.0 / 64,
                                                   bias=eps6[:]))
                        P.v(lambda e: e.reciprocal(rstd[:], rstd[:]), after=[P.ctr["scalar"]])
                        P.v(lambda e: e.tensor_scalar(aob[:], ao[:], rstd[:], 0.0,
                                                      ALU.mult, ALU.add),
                            after=[P.ctr["vector"]])
                        P.v(lambda e: e.nop(), after=[P.ctr["vector"]])
                        P.t(lambda e: e.transpose(ptr[0:64, :], aob[:], idb[:]),
                            after=[P.ctr["vector"]])
                        h = hp * 2 + hh
                        i = st * 4 + q4
                        P.v(lambda e, h=h, i=i: e.tensor_copy(
                            att_fm[(h % 2) * 64:(h % 2) * 64 + 64,
                                   (h // 2) * 2048 + i * 128:(h // 2) * 2048 + i * 128 + 128],
                            ptr[0:64, :]), after=[P.ctr["tensor"]])
        for i in range(16):
            T0 = i * 128
            P.ld(lambda e, T0=T0: e.dma_start(out=xt[:], in_=XQ[T0:T0 + 128, :]),
                 after=ALL(P))
            for c in range(8):
                P.t(lambda e, c=c, T0=T0: e.matmul(
                    py0[:], att_fm[:, c * 2048 + T0:c * 2048 + T0 + 128],
                    wot[:, c * 1024:c * 1024 + 512],
                    start=(c == 0), stop=(c == 7)), after=[P.ctr["vector"]])
                P.t(lambda e, c=c, T0=T0: e.matmul(
                    py1[:], att_fm[:, c * 2048 + T0:c * 2048 + T0 + 128],
                    wot[:, c * 1024 + 512:c * 1024 + 1024],
                    start=(c == 0), stop=(c == 7)))
            yss = ys[:, 0:1024]
            P.v(lambda e: e.nop(), after=[P.dma, P.ctr["tensor"], P.dmo])
            P.v(lambda e, yss=yss: e.tensor_tensor(yss[:, 0:512], py0[:], xt[:, 0:512], ALU.add))
            P.v(lambda e, yss=yss: e.tensor_tensor(yss[:, 512:1024], py1[:], xt[:, 512:1024],
                                                   ALU.add))
            P.st(lambda e, T0=T0, yss=yss: e.dma_start(out=Y[T0:T0 + 128, :], in_=yss[:]),
                 after=[P.ctr["vector"]])
        P.finish()
        P.play(block)
    return nc


def _geg_monomial_const():
    LAM = (64 - 1) / 2.0
    M = np.zeros((9, 9), np.float64)
    M[0, 0] = 1.0
    M[1, 1] = 1.0
    Rp = M[0].copy()
    Rc = M[1].copy()
    for k in range(2, 9):
        c1 = 2.0 * (k + LAM - 1.0) / (k + 2.0 * LAM - 1.0)
        c2 = (k - 1.0) / (k + 2.0 * LAM - 1.0)
        Rn = c1 * np.roll(Rc, 1) - c2 * Rp
        Rn[0] = -c2 * Rp[0]
        M[k] = Rn
        Rp, Rc = Rc, Rn
    return M.astype(np.float32)


def run_geg(x, Wq, Wk, Wv, Wo, poly_weights, norm_w, head_degrees):
    GEGM = _geg_monomial_const()
    in_maps = []
    for c in range(8):
        b, half = c // 2, c % 2
        tiles = list(range(half, 32, 2))
        xq = np.concatenate([x[b, t * 128:(t + 1) * 128] for t in tiles], 0)
        qoff = np.array([[t * 128 for t in tiles]], np.float32)
        in_maps.append({"XF": np.ascontiguousarray(x[b]), "XQ": np.ascontiguousarray(xq),
                        "WQT": np.ascontiguousarray(Wq.T), "WKT": np.ascontiguousarray(Wk.T),
                        "WVT": np.ascontiguousarray(Wv.T), "WOT": np.ascontiguousarray(Wo.T),
                        "NWT": np.ascontiguousarray(np.tile(norm_w, 16).reshape(8, 128).T),
                        "PW": np.ascontiguousarray(poly_weights),
                        "HD": np.ascontiguousarray(head_degrees.reshape(16, 1)),
                        "GEGM": GEGM, "QOFF": qoff, "IDB": ID128B, "IDF": ID128F})
    res = run_bass_kernel_spmd(get_nc("geg"), in_maps, core_ids=list(range(8)))
    y = np.zeros((4, 4096, 1024), np.float32)
    for c in range(8):
        b, half = c // 2, c % 2
        out = res.results[c]["Y"]
        for li, t in enumerate(range(half, 32, 2)):
            y[b, t * 128:(t + 1) * 128] = out[li * 128:(li + 1) * 128]
    return y


_NCACHE = {}


def get_nc(name):
    if name not in _NCACHE:
        _NCACHE[name] = {"ffn": build_ffn, "attn_a": build_attn_a, "attn_a_dbg": (lambda: build_attn_a(True)),
                      "mha0": (lambda: build_mha(0)), "mha1": (lambda: build_mha(2048)),
                      "gla": build_gla, "gla_dbg": (lambda: build_gla(True)), "geg": build_geg}[name]()
    return _NCACHE[name]


def run_ffn(x, w_gate, w_up, w_down):
    nc = get_nc("ffn")
    xs = np.ascontiguousarray(x.reshape(16384, 1024))
    wg = np.ascontiguousarray(w_gate)
    wu = np.ascontiguousarray(w_up)
    wd = np.ascontiguousarray(w_down)
    in_maps = []
    for c in range(8):
        in_maps.append({"X": xs[c * 2048:(c + 1) * 2048], "WG": wg, "WU": wu,
                        "WD": wd, "IDB": ID128B})
    res = run_bass_kernel_spmd(nc, in_maps, core_ids=list(range(8)))
    y = np.concatenate([res.results[c]["Y"] for c in range(8)], 0)
    return y.reshape(4, 4096, 1024)


def kernel(**inputs):
    d = inputs
    _unused = (d["L0_ffn_attn_mask"], d["L0_ffn_Wg"], d["L0_ffn_Wq"], d["L0_ffn_bq"],
               d["L0_ffn_Wkv"], d["L0_ffn_bkv"], d["L0_ffn_Wo"])
    x = d["L0_attn_a_x"]
    x = run_attn_a(x, d["L0_attn_a_Wq"], d["L0_attn_a_Wk"], d["L0_attn_a_Wv"],
                   d["L0_attn_a_Wo"], d["L0_ffn_ln_g"], d["L0_ffn_ln_b"])
    x = run_gla(x, d)
    x = run_geg(x, d["L1_attn_a_Wq"], d["L1_attn_a_Wk"], d["L1_attn_a_Wv"],
                d["L1_attn_a_Wo"], d["L1_attn_a_poly_weights"], d["L1_attn_a_norm_w"],
                d["L1_attn_a_head_degrees"])
    x = run_ffn(x, d["L1_ffn_w_gate"], d["L1_ffn_w_up"], d["L1_ffn_w_down"])
    x = run_mha(x, d["L1_attn_b_Wq"], d["L1_attn_b_bq"], d["L1_attn_b_Wk"],
                d["L1_attn_b_bk"], d["L1_attn_b_Wv"], d["L1_attn_b_bv"],
                d["L1_attn_b_Wo"], d["L1_attn_b_bo"])
    x = run_ffn(x, d["L1_ffn_w_gate"], d["L1_ffn_w_up"], d["L1_ffn_w_down"])
    return x


# version 04 (final)
# speedup vs baseline: 1.2664x; 1.0388x over previous
import numpy as np
import ml_dtypes
from contextlib import ExitStack
import concourse.bass as bass
import concourse.mybir as mybir
from concourse.bass_utils import run_bass_kernel_spmd

F32 = mybir.dt.float32
BF16 = mybir.dt.bfloat16
U8 = mybir.dt.uint8
I32 = mybir.dt.int32
AF = mybir.ActivationFunctionType
ALU = mybir.AluOpType
AX = mybir.AxisListType

NEG = -1e30
BF = ml_dtypes.bfloat16
ID128B = np.eye(128, dtype=BF)
ID128F = np.eye(128, dtype=np.float32)


class Ctr:
    def __init__(self, h, step=1):
        self.h = h
        self.n = 0
        self.step = step

    def inc(self):
        self.n += self.step
        return (self.h, self.step)

    def at(self):
        return (self.h, self.n)


class Prog:
    ENGS = ("sync", "tensor", "vector", "scalar", "gpsimd")

    def __init__(self, nc, ctx):
        self.nc = nc
        self.ops = {e: [] for e in self.ENGS}
        self.ctr = {}
        for e in self.ENGS:
            h = ctx.enter_context(nc.semaphore("sem_" + e))
            self.ctr[e] = Ctr(h, 1)
        self.dma = Ctr(ctx.enter_context(nc.semaphore("sem_dma")), 16)
        self.dmo = Ctr(ctx.enter_context(nc.semaphore("sem_dmo")), 16)

    def emit(self, eng, fn, after=(), ctr=None):
        waits = [c.at() for c in after]
        c = ctr if ctr is not None else self.ctr[eng]
        inc = c.inc()
        self.ops[eng].append((fn, waits, inc))
        return c

    def t(self, fn, after=()):
        return self.emit("tensor", fn, after)

    def v(self, fn, after=()):
        return self.emit("vector", fn, after)

    def s(self, fn, after=()):
        return self.emit("scalar", fn, after)

    def g(self, fn, after=()):
        return self.emit("gpsimd", fn, after)

    def ld(self, fn, after=()):
        return self.emit("sync", fn, after, ctr=self.dma)

    def st(self, fn, after=()):
        return self.emit("sync", fn, after, ctr=self.dmo)

    def barrier(self):
        marks = [self.ctr[e].at() for e in self.ENGS] + [self.dma.at(), self.dmo.at()]
        for e in self.ENGS:
            waits = [m for m in marks if m[0] is not self.ctr[e].h]

            def fn(eng):
                return eng.nop()
            self.ops[e].append((fn, waits, self.ctr[e].inc()))

    def finish(self):
        self.ops["sync"].append((lambda e: None, [self.dmo.at()], None))

    def play(self, block):
        for e in self.ENGS:
            deco = getattr(block, e)
            ops = self.ops[e]

            def make(ops):
                def body(eng):
                    for fn, waits, inc in ops:
                        for h, v in waits:
                            eng.wait_ge(h, v)
                        inst = fn(eng)
                        if inst is not None and inc is not None:
                            inst.then_inc(*inc)
                return body
            deco(make(ops))


def ALL(P):
    return [P.ctr[e] for e in ("tensor", "vector", "scalar", "gpsimd")] + [P.dma]


def emit_ln(P, out, x, dump, mcol, ssq, var, rstd, eps_col, n=1024):
    P.s(lambda e: e.activation(dump[:], x[:], AF.Square, accum_out=ssq[:]),
        after=[P.ctr["vector"]])
    P.v(lambda e: e.tensor_reduce(mcol[:], x[:], AX.X, ALU.add))
    P.v(lambda e: e.tensor_scalar(mcol[:], mcol[:], 1.0 / n, 0.0, ALU.mult, ALU.add),
        after=[P.ctr["vector"]])
    P.v(lambda e: e.tensor_tensor(var[:], mcol[:], mcol[:], ALU.mult),
        after=[P.ctr["scalar"], P.ctr["vector"]])
    P.v(lambda e: e.scalar_tensor_tensor(var[:], ssq[:], 1.0 / n, var[:], ALU.mult, ALU.subtract),
        after=[P.ctr["vector"]])
    P.v(lambda e: e.tensor_scalar(var[:], var[:], 0.0, 0.0, ALU.max, ALU.add),
        after=[P.ctr["vector"]])
    P.s(lambda e: e.activation(rstd[:], var[:], AF.Sqrt, bias=eps_col[:]), after=[P.ctr["vector"]])
    P.v(lambda e: e.reciprocal(rstd[:], rstd[:]), after=[P.ctr["scalar"]])
    P.v(lambda e: e.tensor_scalar(out[:], x[:], mcol[:], 0.0, ALU.subtract, ALU.add),
        after=[P.ctr["vector"]])
    P.v(lambda e: e.tensor_scalar(out[:], out[:], rstd[:], 0.0, ALU.mult, ALU.add))


def emit_tr(P, dst, src, ptr, idn, ncols, after=()):
    for c in range(ncols):
        P.t(lambda e, c=c: e.transpose(ptr[:], src[:, c * 128:(c + 1) * 128], idn[:]),
            after=list(after) + [P.ctr["vector"]])
        P.v(lambda e, c=c: e.tensor_copy(dst(c), ptr[:]), after=[P.ctr["tensor"]])


def emit_quant(P, out_bf, x, a, u, sc, nb):
    x3 = x.rearrange("p (b s) -> p b s", s=16)
    u3 = u.rearrange("p (b s) -> p b s", s=16)
    P.v(lambda e: e.tensor_reduce(sc[:, 0:nb], x3, AX.X, ALU.max, apply_absolute_value=True),
        after=[P.ctr["scalar"]])
    P.v(lambda e: e.tensor_scalar(sc[:, 0:nb], sc[:, 0:nb], 1.0 / 6.0, 1e-12, ALU.mult, ALU.max),
        after=[P.ctr["vector"]])
    P.v(lambda e: e.reciprocal(sc[:, nb:2 * nb], sc[:, 0:nb]), after=[P.ctr["vector"]])
    P.v(lambda e: e.nop(), after=[P.ctr["vector"]])
    for j in range(16):
        P.v(lambda e, j=j: e.tensor_copy(u3[:, :, j], sc[:, nb:2 * nb]))
    P.s(lambda e: e.activation(a, x, AF.Abs), after=[P.ctr["vector"]])
    P.v(lambda e: e.tensor_tensor(a, a, u, ALU.mult), after=[P.ctr["scalar"], P.ctr["vector"]])
    P.v(lambda e: e.tensor_scalar(u, a, 5.0, 2.0, ALU.is_gt, ALU.mult))
    P.v(lambda e: e.scalar_tensor_tensor(u, a, 3.5, u, ALU.is_gt, ALU.add))
    P.v(lambda e: e.scalar_tensor_tensor(u, a, 2.5, u, ALU.is_gt, ALU.add))
    P.v(lambda e: e.tensor_scalar(u, u, 2.0, 0.0, ALU.mult, ALU.add))
    P.v(lambda e: e.scalar_tensor_tensor(u, a, 1.75, u, ALU.is_gt, ALU.add))
    P.v(lambda e: e.scalar_tensor_tensor(u, a, 1.25, u, ALU.is_gt, ALU.add))
    P.v(lambda e: e.scalar_tensor_tensor(u, a, 0.75, u, ALU.is_gt, ALU.add))
    P.v(lambda e: e.scalar_tensor_tensor(u, a, 0.25, u, ALU.is_gt, ALU.add))
    P.s(lambda e: e.activation(a, x, AF.Sign), after=[P.ctr["vector"]])
    P.v(lambda e: e.scalar_tensor_tensor(a, a, 0.5, u, ALU.mult, ALU.mult),
        after=[P.ctr["scalar"]])
    P.v(lambda e: e.nop(), after=[P.ctr["vector"]])
    for j in range(16):
        P.v(lambda e, j=j: e.tensor_copy(u3[:, :, j], sc[:, 0:nb]))
    P.v(lambda e: e.tensor_tensor(out_bf, a, u, ALU.mult), after=[P.ctr["vector"]])


def build_ffn(pq=False):
    nc = bass.Bass("TRN2", target_bir_lowering=False)
    X = nc.dram_tensor("X", [2048, 1024], F32, kind="ExternalInput").ap()
    if not pq:
        WG = nc.dram_tensor("WG", [2816, 1024], F32, kind="ExternalInput").ap()
        WU = nc.dram_tensor("WU", [2816, 1024], F32, kind="ExternalInput").ap()
        WD = nc.dram_tensor("WD", [1024, 2816], F32, kind="ExternalInput").ap()
        WGQO = nc.dram_tensor("WGQO", [128, 8 * 2816], BF16, kind="ExternalOutput").ap()
        WUQO = nc.dram_tensor("WUQO", [128, 8 * 2816], BF16, kind="ExternalOutput").ap()
        WDQO = nc.dram_tensor("WDQO", [128, 22 * 1024], BF16, kind="ExternalOutput").ap()
    else:
        WGQ = nc.dram_tensor("WGQ", [128, 8 * 2816], BF16, kind="ExternalInput").ap()
        WUQ = nc.dram_tensor("WUQ", [128, 8 * 2816], BF16, kind="ExternalInput").ap()
        WDQ = nc.dram_tensor("WDQ", [128, 22 * 1024], BF16, kind="ExternalInput").ap()
    IDB = nc.dram_tensor("IDB", [128, 128], BF16, kind="ExternalInput").ap()
    Y = nc.dram_tensor("Y", [2048, 1024], F32, kind="ExternalOutput").ap()
    SEC = [(i * 512, min(512, 2816 - i * 512)) for i in range(6)]
    with ExitStack() as ctx:
        E = ctx.enter_context
        P = Prog(nc, ctx)
        wgq = E(nc.sbuf_tensor("wgq", [128, 8 * 2816], BF16))
        wuq = E(nc.sbuf_tensor("wuq", [128, 8 * 2816], BF16))
        wdq = E(nc.sbuf_tensor("wdq", [128, 22 * 1024], BF16))
        idb = E(nc.sbuf_tensor("idb", [128, 128], BF16))
        wt = E(nc.sbuf_tensor("wt", [128, 2816], F32))
        qa = E(nc.sbuf_tensor("qa", [128, 2816], F32))
        qu = E(nc.sbuf_tensor("qu", [128, 2816], F32))
        qs = E(nc.sbuf_tensor("qs", [128, 2 * 176], F32))
        qt = E(nc.sbuf_tensor("qt", [128, 2816], BF16))
        xf = E(nc.sbuf_tensor("xf", [128, 1024], F32))
        h = E(nc.sbuf_tensor("h", [128, 1024], F32))
        hq = E(nc.sbuf_tensor("hq", [128, 1024], BF16))
        hqf = E(nc.sbuf_tensor("hqf", [128, 8 * 128], BF16))
        sg = E(nc.sbuf_tensor("sg", [128, 512], F32))
        mid = E(nc.sbuf_tensor("mid", [128, 512], F32))
        midq = E(nc.sbuf_tensor("midq", [128, 512], BF16))
        midqf = E(nc.sbuf_tensor("midqf", [128, 4 * 128], BF16))
        ys = E(nc.sbuf_tensor("ys", [128, 1024], F32))
        mcol = E(nc.sbuf_tensor("mcol", [128, 1], F32))
        ssq = E(nc.sbuf_tensor("ssq", [128, 1], F32))
        var = E(nc.sbuf_tensor("var", [128, 1], F32))
        rstd = E(nc.sbuf_tensor("rstd", [128, 1], F32))
        eps5 = E(nc.sbuf_tensor("eps5", [128, 1], F32))
        ptr = E(nc.psum_tensor("ptr", [128, 128], BF16))
        pg = E(nc.psum_tensor("pg", [128, 512], F32))
        pu = E(nc.psum_tensor("pu", [128, 512], F32))
        py0 = E(nc.psum_tensor("py0", [128, 512], F32))
        py1 = E(nc.psum_tensor("py1", [128, 512], F32))
        block = E(nc.Block())

        P.ld(lambda e: e.dma_start(out=idb[:], in_=IDB))
        P.v(lambda e: e.memset(eps5[:], 1e-5))
        if not pq:
            for wi, (W, dst, ntile, K, fmcols) in enumerate([
                    (WG, wgq, 22, 1024, 2816), (WU, wuq, 22, 1024, 2816),
                    (WD, wdq, 8, 2816, 1024)]):
                for r in range(ntile):
                    P.ld(lambda e, W=W, r=r, K=K: e.dma_start(
                        out=wt[:, 0:K], in_=W[r * 128:(r + 1) * 128, :]), after=ALL(P))
                    P.v(lambda e: e.nop(), after=[P.dma])
                    emit_quant(P, qt[:, 0:K], wt[:, 0:K], qa[:, 0:K], qu[:, 0:K], qs[:], K // 16)
                    emit_tr(P, (lambda c, dst=dst, r=r, fmcols=fmcols:
                                dst[:, c * fmcols + r * 128: c * fmcols + r * 128 + 128]),
                            qt, ptr, idb, K // 128)
            P.barrier()
            for dsrc, ddst, nn in [(wgq, WGQO, 8 * 2816), (wuq, WUQO, 8 * 2816),
                                   (wdq, WDQO, 22 * 1024)]:
                P.st(lambda e, dsrc=dsrc, ddst=ddst, nn=nn: e.dma_start(
                    out=ddst, in_=dsrc[:, 0:nn]))
        else:
            for dsrc, ddst, nn in [(WGQ, wgq, 8 * 2816), (WUQ, wuq, 8 * 2816),
                                   (WDQ, wdq, 22 * 1024)]:
                P.ld(lambda e, dsrc=dsrc, ddst=ddst, nn=nn: e.dma_start(
                    out=ddst[:, 0:nn], in_=dsrc))
        P.barrier()
        for i in range(16):
            P.ld(lambda e, i=i: e.dma_start(out=xf[:], in_=X[i * 128:(i + 1) * 128, :]),
                 after=ALL(P))
            P.v(lambda e: e.nop(), after=[P.dma])
            emit_ln(P, h, xf, qa[:, 0:1024], mcol, ssq, var, rstd, eps5)
            emit_quant(P, hq[:], h[:], qa[:, 0:1024], qu[:, 0:1024], qs[:], 64)
            emit_tr(P, (lambda c: hqf[:, c * 128:(c + 1) * 128]), hq, ptr, idb, 8)
            for si, (s0, sn) in enumerate(SEC):
                tmark = [P.ctr["vector"], P.ctr["scalar"]]
                for c in range(8):
                    P.t(lambda e, c=c, s0=s0, sn=sn: e.matmul(
                        pg[:, 0:sn], hqf[:, c * 128:(c + 1) * 128],
                        wgq[:, c * 2816 + s0:c * 2816 + s0 + sn],
                        start=(c == 0), stop=(c == 7)), after=tmark)
                for c in range(8):
                    P.t(lambda e, c=c, s0=s0, sn=sn: e.matmul(
                        pu[:, 0:sn], hqf[:, c * 128:(c + 1) * 128],
                        wuq[:, c * 2816 + s0:c * 2816 + s0 + sn],
                        start=(c == 0), stop=(c == 7)), after=tmark)
                P.s(lambda e, sn=sn: e.activation(sg[:, 0:sn], pg[:, 0:sn], AF.Silu),
                    after=[P.ctr["tensor"]])
                P.v(lambda e, sn=sn: e.tensor_tensor(mid[:, 0:sn], sg[:, 0:sn], pu[:, 0:sn],
                                                     ALU.mult),
                    after=[P.ctr["scalar"], P.ctr["tensor"]])
                emit_quant(P, midq[:, 0:sn], mid[:, 0:sn], qa[:, 0:sn], qu[:, 0:sn],
                           qs[:], sn // 16)
                emit_tr(P, (lambda c: midqf[:, c * 128:(c + 1) * 128]), midq, ptr, idb,
                        sn // 128)
                for c4 in range(sn // 128):
                    cg = si * 4 + c4
                    st = (si == 0 and c4 == 0)
                    sp = (si == 5 and c4 == (sn // 128) - 1)
                    P.t(lambda e, c4=c4, cg=cg, st=st, sp=sp: e.matmul(
                        py0[:], midqf[:, c4 * 128:(c4 + 1) * 128],
                        wdq[:, cg * 1024:cg * 1024 + 512],
                        start=st, stop=sp), after=[P.ctr["vector"]])
                    P.t(lambda e, c4=c4, cg=cg, st=st, sp=sp: e.matmul(
                        py1[:], midqf[:, c4 * 128:(c4 + 1) * 128],
                        wdq[:, cg * 1024 + 512:cg * 1024 + 1024],
                        start=st, stop=sp))
            P.v(lambda e: e.tensor_tensor(ys[:, 0:512], py0[:], xf[:, 0:512], ALU.add),
                after=[P.ctr["tensor"]])
            P.v(lambda e: e.tensor_tensor(ys[:, 512:1024], py1[:], xf[:, 512:1024], ALU.add))
            P.st(lambda e, i=i: e.dma_start(out=Y[i * 128:(i + 1) * 128, :], in_=ys[:]),
                 after=[P.ctr["vector"]])
        P.finish()
        P.play(block)
    return nc




def build_attn_a(dbg=False):
    nc = bass.Bass("TRN2", target_bir_lowering=False)
    XKV = nc.dram_tensor("XKV", [2560, 1024], F32, kind="ExternalInput").ap()
    XSK = nc.dram_tensor("XSK", [4, 1024], F32, kind="ExternalInput").ap()
    WQT = nc.dram_tensor("WQT", [1024, 1024], F32, kind="ExternalInput").ap()
    WKT = nc.dram_tensor("WKT", [1024, 1024], F32, kind="ExternalInput").ap()
    WVT = nc.dram_tensor("WVT", [1024, 1024], F32, kind="ExternalInput").ap()
    WOT = nc.dram_tensor("WOT", [1024, 1024], F32, kind="ExternalInput").ap()
    LNG = nc.dram_tensor("LNG", [1, 1024], F32, kind="ExternalInput").ap()
    LNB = nc.dram_tensor("LNB", [1, 1024], F32, kind="ExternalInput").ap()
    WMT = nc.dram_tensor("WMT", [128, 640], F32, kind="ExternalInput").ap()
    KM = nc.dram_tensor("KM", [128, 20], F32, kind="ExternalInput").ap()
    SM = nc.dram_tensor("SM", [4, 2048], BF16, kind="ExternalInput").ap()
    IDB = nc.dram_tensor("IDB", [128, 128], BF16, kind="ExternalInput").ap()
    IDF = nc.dram_tensor("IDF", [128, 128], F32, kind="ExternalInput").ap()
    Y = nc.dram_tensor("Y", [2048, 1024], F32, kind="ExternalOutput").ap()
    if dbg:
        KD = nc.dram_tensor("KD", [128, 2560], F32, kind="ExternalOutput").ap()
        QD = nc.dram_tensor("QD", [128, 2048], F32, kind="ExternalOutput").ap()
        PD = nc.dram_tensor("PD", [128, 640], F32, kind="ExternalOutput").ap()
        OD = nc.dram_tensor("OD", [128, 65], F32, kind="ExternalOutput").ap()
        AD = nc.dram_tensor("AD", [128, 16384], F32, kind="ExternalOutput").ap()
        Y1D = nc.dram_tensor("Y1D", [2048, 1024], F32, kind="ExternalOutput").ap()
    with ExitStack() as ctx:
        E = ctx.enter_context
        P = Prog(nc, ctx)
        x_fm = E(nc.sbuf_tensor("x_fm", [128, 8 * 2560], BF16))
        att_fm = E(nc.sbuf_tensor("att_fm", [128, 8 * 2048], BF16))
        wot = E(nc.sbuf_tensor("wot", [128, 8 * 1024], BF16))
        wpr = E(nc.sbuf_tensor("wpr", [128, 3 * 1024], BF16))
        wld = E(nc.sbuf_tensor("wld", [128, 1024], F32))
        kA = E(nc.sbuf_tensor("kA", [128, 2560], BF16))
        kB = E(nc.sbuf_tensor("kB", [128, 2560], BF16))
        qA = E(nc.sbuf_tensor("qA", [128, 2048], BF16))
        qB = E(nc.sbuf_tensor("qB", [128, 2048], BF16))
        vA = E(nc.sbuf_tensor("vA", [128, 20 * 65], BF16))
        vB = E(nc.sbuf_tensor("vB", [128, 20 * 65], BF16))
        kskA = E(nc.sbuf_tensor("kskA", [128, 4], BF16))
        kskB = E(nc.sbuf_tensor("kskB", [128, 4], BF16))
        vskA = E(nc.sbuf_tensor("vskA", [4, 65], BF16))
        vskB = E(nc.sbuf_tensor("vskB", [4, 65], BF16))
        xsf = E(nc.sbuf_tensor("xsf", [4, 1024], F32))
        xsb = E(nc.sbuf_tensor("xsb", [4, 1024], BF16))
        xs_fm = E(nc.sbuf_tensor("xs_fm", [128, 8 * 4], BF16))
        wmt = E(nc.sbuf_tensor("wmt", [128, 640], BF16))
        km = E(nc.sbuf_tensor("km", [128, 20], F32))
        sm = E(nc.sbuf_tensor("sm", [4, 2048], BF16))
        mskf = E(nc.sbuf_tensor("mskf", [128, 640], F32))
        pt = E(nc.sbuf_tensor("pt", [128, 640], BF16))
        ptsk = E(nc.sbuf_tensor("ptsk", [4, 128], BF16))
        xt = E(nc.sbuf_tensor("xt", [128, 1024], F32))
        lnx = E(nc.sbuf_tensor("lnx", [128, 1024], F32))
        lnb_ = E(nc.sbuf_tensor("lnb_", [128, 1024], BF16))
        dump = E(nc.sbuf_tensor("dump", [128, 1024], F32))
        ao = E(nc.sbuf_tensor("ao", [128, 64], BF16))
        mx = E(nc.sbuf_tensor("mx", [128, 1], F32))
        mx2 = E(nc.sbuf_tensor("mx2", [128, 1], F32))
        rc = E(nc.sbuf_tensor("rc", [128, 1], F32))
        g_bc = E(nc.sbuf_tensor("g_bc", [128, 1024], F32))
        b_bc = E(nc.sbuf_tensor("b_bc", [128, 1024], F32))
        row1 = E(nc.sbuf_tensor("row1", [1, 1024], F32))
        ones1 = E(nc.sbuf_tensor("ones1", [1, 128], F32))
        ys = E(nc.sbuf_tensor("ys", [128, 8 * 1024], F32))
        yln = E(nc.sbuf_tensor("yln", [128, 1024], F32))
        mcol = E(nc.sbuf_tensor("mcol", [128, 1], F32))
        ssq = E(nc.sbuf_tensor("ssq", [128, 1], F32))
        var = E(nc.sbuf_tensor("var", [128, 1], F32))
        rstd = E(nc.sbuf_tensor("rstd", [128, 1], F32))
        eps5 = E(nc.sbuf_tensor("eps5", [128, 1], F32))
        idb = E(nc.sbuf_tensor("idb", [128, 128], BF16))
        idf = E(nc.sbuf_tensor("idf", [128, 128], F32))
        psA = E(nc.psum_tensor("psA", [128, 512], F32))
        psB = E(nc.psum_tensor("psB", [128, 384], F32))
        p2 = E(nc.psum_tensor("p2", [128, 128], F32))
        po = E(nc.psum_tensor("po", [128, 65], F32))
        ptr = E(nc.psum_tensor("ptr", [128, 128], BF16))
        ptrf = E(nc.psum_tensor("ptrf", [128, 128], F32))
        py0 = E(nc.psum_tensor("py0", [128, 512], F32))
        py1 = E(nc.psum_tensor("py1", [128, 512], F32))
        block = E(nc.Block())

        P.ld(lambda e: e.dma_start(out=idb[:], in_=IDB))
        P.ld(lambda e: e.dma_start(out=idf[:], in_=IDF))
        P.ld(lambda e: e.dma_start(out=mskf[:], in_=WMT))
        P.ld(lambda e: e.dma_start(out=km[:], in_=KM))
        P.ld(lambda e: e.dma_start(out=sm[:], in_=SM))
        P.ld(lambda e: e.dma_start(out=row1[:], in_=LNG))
        P.v(lambda e: e.memset(eps5[:], 1e-5), after=[P.dma])
        P.v(lambda e: e.memset(ones1[:], 1.0))
        P.v(lambda e: e.tensor_copy(wmt[:], mskf[:]))
        P.t(lambda e: e.matmul(py0[:], ones1[:], row1[:, 0:512], start=True, stop=True),
            after=[P.ctr["vector"], P.dma])
        P.t(lambda e: e.matmul(py1[:], ones1[:], row1[:, 512:1024], start=True, stop=True))
        P.v(lambda e: e.tensor_copy(g_bc[:, 0:512], py0[:]), after=[P.ctr["tensor"]])
        P.v(lambda e: e.tensor_copy(g_bc[:, 512:1024], py1[:]))
        P.ld(lambda e: e.dma_start(out=row1[:], in_=LNB), after=[P.ctr["tensor"]])
        P.t(lambda e: e.matmul(py0[:], ones1[:], row1[:, 0:512], start=True, stop=True),
            after=[P.ctr["vector"], P.dma])
        P.t(lambda e: e.matmul(py1[:], ones1[:], row1[:, 512:1024], start=True, stop=True))
        P.v(lambda e: e.tensor_copy(b_bc[:, 0:512], py0[:]), after=[P.ctr["tensor"]])
        P.v(lambda e: e.tensor_copy(b_bc[:, 512:1024], py1[:]))

        for jt in range(20):
            P.ld(lambda e, jt=jt: e.dma_start(out=xt[:], in_=XKV[jt * 128:(jt + 1) * 128, :]),
                 after=ALL(P))
            P.v(lambda e: e.nop(), after=[P.dma])
            emit_ln(P, lnx, xt, dump, mcol, ssq, var, rstd, eps5)
            P.v(lambda e: e.tensor_copy(lnb_[:], lnx[:]), after=[P.ctr["vector"]])
            emit_tr(P, (lambda c, jt=jt: x_fm[:, c * 2560 + jt * 128: c * 2560 + jt * 128 + 128]),
                    lnb_, ptr, idb, 8)
        P.v(lambda e: e.memset(xt[:], 0.0), after=ALL(P))
        P.v(lambda e: e.nop(), after=[P.ctr["vector"]])
        P.ld(lambda e: e.dma_start(out=xt[0:4, :], in_=XSK), after=[P.ctr["vector"]])
        P.v(lambda e: e.nop(), after=[P.dma])
        emit_ln(P, lnx, xt, dump, mcol, ssq, var, rstd, eps5)
        P.v(lambda e: e.tensor_copy(xsb[:], lnx[0:4, :]), after=[P.ctr["vector"]])
        for c in range(8):
            P.t(lambda e, c=c: e.transpose(ptr[:, 0:4], xsb[:, c * 128:(c + 1) * 128], idb[0:4, 0:4]),
                after=[P.ctr["vector"]])
            P.v(lambda e, c=c: e.tensor_copy(xs_fm[:, c * 4:(c + 1) * 4],
                                             ptr[0:128, 0:4]), after=[P.ctr["tensor"]])
        for c in range(8):
            P.ld(lambda e, c=c: e.dma_start(out=wld[:], in_=WOT[c * 128:(c + 1) * 128, :]),
                 after=ALL(P))
            P.v(lambda e, c=c: e.tensor_copy(wot[:, c * 1024:(c + 1) * 1024], wld[:]),
                after=[P.dma])

        for hp in range(8):
            pair_t = [P.ctr["vector"], P.ctr["scalar"], P.ctr["tensor"]]
            for wi, W in enumerate([WQT, WKT, WVT]):
                P.ld(lambda e, wi=wi, W=W, hp=hp: e.dma_start(
                    out=wld[:].rearrange("p (c n) -> p c n", n=128),
                    in_=W[:, hp * 128:(hp + 1) * 128].rearrange("(c p) n -> p c n", p=128)),
                    after=ALL(P))
                P.v(lambda e, wi=wi: e.tensor_copy(wpr[:, wi * 1024:(wi + 1) * 1024], wld[:]),
                    after=[P.dma])
            P.v(lambda e: e.memset(kA[64:128, :], 0.0), after=[P.ctr["tensor"]])
            P.v(lambda e: e.memset(kB[64:128, :], 0.0))
            P.v(lambda e: e.memset(kA[64:65, :], 1.0))
            P.v(lambda e: e.memset(kB[64:65, :], 1.0))
            P.v(lambda e: e.memset(qA[64:128, :], 0.0))
            P.v(lambda e: e.memset(qB[64:128, :], 0.0))
            P.v(lambda e: e.memset(kskA[64:128, :], 0.0))
            P.v(lambda e: e.memset(kskB[64:128, :], 0.0))
            P.v(lambda e: e.memset(kskA[64:65, :], 1.0))
            P.v(lambda e: e.memset(kskB[64:65, :], 1.0))
            v3a = vA[:].rearrange("p (j c) -> p j c", c=65)
            v3b = vB[:].rearrange("p (j c) -> p j c", c=65)
            P.v(lambda e: e.memset(v3a[:, :, 64], 1.0))
            P.v(lambda e: e.memset(v3b[:, :, 64], 1.0))
            P.v(lambda e: e.memset(vskA[:, 64:65], 1.0))
            P.v(lambda e: e.memset(vskB[:, 64:65], 1.0))
            for j5 in range(5):
                for c in range(8):
                    P.t(lambda e, j5=j5, c=c: e.matmul(
                        psA[:], wpr[:, 1024 + c * 128:1024 + c * 128 + 128],
                        x_fm[:, c * 2560 + j5 * 512: c * 2560 + j5 * 512 + 512],
                        start=(c == 0), stop=(c == 7)), after=[P.ctr["vector"]])
                P.v(lambda e, j5=j5: e.tensor_copy(kA[0:64, j5 * 512:(j5 + 1) * 512],
                                                   psA[0:64, :]), after=[P.ctr["tensor"]])
                P.v(lambda e, j5=j5: e.tensor_copy(kB[0:64, j5 * 512:(j5 + 1) * 512],
                                                   psA[64:128, :]))
            for j4 in range(4):
                for c in range(8):
                    P.t(lambda e, j4=j4, c=c: e.matmul(
                        psA[:], wpr[:, c * 128:c * 128 + 128],
                        x_fm[:, c * 2560 + 512 + j4 * 512: c * 2560 + 512 + j4 * 512 + 512],
                        start=(c == 0), stop=(c == 7)), after=[P.ctr["vector"]])
                P.v(lambda e, j4=j4: e.tensor_copy(qA[0:64, j4 * 512:(j4 + 1) * 512],
                                                   psA[0:64, :]), after=[P.ctr["tensor"]])
                P.v(lambda e, j4=j4: e.tensor_copy(qB[0:64, j4 * 512:(j4 + 1) * 512],
                                                   psA[64:128, :]))
            for jt in range(20):
                for c in range(8):
                    P.t(lambda e, jt=jt, c=c: e.matmul(
                        psB[:, 0:128], x_fm[:, c * 2560 + jt * 128: c * 2560 + jt * 128 + 128],
                        wpr[:, 2048 + c * 128: 2048 + c * 128 + 128],
                        start=(c == 0), stop=(c == 7)), after=[P.ctr["vector"]])
                P.v(lambda e, jt=jt: e.tensor_copy(vA[:, jt * 65:jt * 65 + 64], psB[:, 0:64]),
                    after=[P.ctr["tensor"]])
                P.v(lambda e, jt=jt: e.tensor_copy(vB[:, jt * 65:jt * 65 + 64], psB[:, 64:128]))
            for c in range(8):
                P.t(lambda e, c=c: e.matmul(
                    psB[:, 128:132], wpr[:, 1024 + c * 128:1024 + c * 128 + 128],
                    xs_fm[:, c * 4:(c + 1) * 4], start=(c == 0), stop=(c == 7)),
                    after=[P.ctr["vector"]])
            P.v(lambda e: e.tensor_copy(kskA[0:64, :], psB[0:64, 128:132]),
                after=[P.ctr["tensor"]])
            P.v(lambda e: e.tensor_copy(kskB[0:64, :], psB[64:128, 128:132]))
            for c in range(8):
                P.t(lambda e, c=c: e.matmul(
                    psB[0:4, 256:384], xs_fm[:, c * 4:(c + 1) * 4],
                    wpr[:, 2048 + c * 128: 2048 + c * 128 + 128],
                    start=(c == 0), stop=(c == 7)), after=[P.ctr["vector"]])
            P.v(lambda e: e.tensor_copy(vskA[:, 0:64], psB[0:4, 256:320]),
                after=[P.ctr["tensor"]])
            P.v(lambda e: e.tensor_copy(vskB[:, 0:64], psB[0:4, 320:384]))
            for hh, (kX, qX, vX, v3X, kskX, vskX) in enumerate(
                    [(kA, qA, vA, v3a, kskA, vskA), (kB, qB, vB, v3b, kskB, vskB)]):
                h = hp * 2 + hh
                for i in range(16):
                    T0 = i * 128
                    P.t(lambda e, qX=qX, kX=kX, T0=T0: e.matmul(
                        psA[:], qX[:, T0:T0 + 128], kX[:, T0:T0 + 512],
                        start=True, stop=True), after=[P.ctr["vector"]])
                    P.t(lambda e, qX=qX, kX=kX, T0=T0: e.matmul(
                        psB[:, 0:128], qX[:, T0:T0 + 128], kX[:, T0 + 512:T0 + 640],
                        start=True, stop=True))
                    P.t(lambda e, qX=qX, kskX=kskX, T0=T0: e.matmul(
                        psB[:, 128:132], qX[:, T0:T0 + 128], kskX[:],
                        start=True, stop=True))
                    P.v(lambda e: e.tensor_reduce(mx[:], psA[:], AX.X, ALU.max),
                        after=[P.ctr["tensor"]])
                    P.v(lambda e: e.tensor_reduce(mx2[:], psB[:, 0:132], AX.X, ALU.max))
                    P.v(lambda e: e.tensor_tensor(mx[:], mx[:], mx2[:], ALU.max),
                        after=[P.ctr["vector"]])
                    P.v(lambda e: e.tensor_scalar(mx[:], mx[:], 0.125, 0.0, ALU.mult, ALU.add),
                        after=[P.ctr["vector"]])
                    P.t(lambda e: e.transpose(ptrf[0:1, 0:128], mx[:], idf[:]),
                        after=[P.ctr["vector"]])
                    P.v(lambda e, qX=qX, T0=T0: e.tensor_scalar(
                        qX[64:65, T0:T0 + 128], ptrf[0:1, 0:128], -8.0, 0.0,
                        ALU.mult, ALU.add), after=[P.ctr["tensor"]])
                    for cjj in range(5):
                        P.t(lambda e, kX=kX, qX=qX, T0=T0, cjj=cjj: e.matmul(
                            p2[:], kX[:, T0 + cjj * 128:T0 + cjj * 128 + 128],
                            qX[:, T0:T0 + 128], start=True, stop=True),
                            after=[P.ctr["vector"], P.ctr["scalar"]])
                        P.s(lambda e, cjj=cjj: e.activation(
                            pt[:, cjj * 128:(cjj + 1) * 128], p2[:], AF.Exp, scale=0.125),
                            after=[P.ctr["tensor"]])
                        P.v(lambda e, cjj=cjj: e.tensor_tensor(
                            pt[:, cjj * 128:(cjj + 1) * 128], pt[:, cjj * 128:(cjj + 1) * 128],
                            wmt[:, cjj * 128:(cjj + 1) * 128], ALU.mult),
                            after=[P.ctr["scalar"]])
                        P.v(lambda e, cjj=cjj, i=i: e.tensor_scalar(
                            pt[:, cjj * 128:(cjj + 1) * 128], pt[:, cjj * 128:(cjj + 1) * 128],
                            km[:, i + cjj:i + cjj + 1], 0.0, ALU.mult, ALU.add))
                    P.t(lambda e, kskX=kskX, qX=qX, T0=T0: e.matmul(
                        ptrf[0:4, :], kskX[:], qX[:, T0:T0 + 128], start=True, stop=True),
                        after=[P.ctr["vector"], P.ctr["scalar"]])
                    P.s(lambda e: e.activation(ptsk[:], ptrf[0:4, :], AF.Exp, scale=0.125),
                        after=[P.ctr["tensor"]])
                    P.v(lambda e, T0=T0: e.tensor_tensor(
                        ptsk[:], ptsk[:], sm[:, T0:T0 + 128], ALU.mult),
                        after=[P.ctr["scalar"]])
                    for cjj in range(5):
                        P.t(lambda e, cjj=cjj, vX=vX, i=i: e.matmul(
                            po[:], pt[:, cjj * 128:(cjj + 1) * 128],
                            vX[:, (i + cjj) * 65:(i + cjj) * 65 + 65],
                            start=(cjj == 0), stop=False), after=[P.ctr["vector"]])
                    P.t(lambda e, vskX=vskX: e.matmul(
                        po[:], ptsk[:], vskX[:], start=False, stop=True))
                    P.v(lambda e: e.reciprocal(rc[:], po[:, 64:65]), after=[P.ctr["tensor"]])
                    P.v(lambda e: e.tensor_scalar(ao[:], po[:, 0:64], rc[:], 0.0,
                                                  ALU.mult, ALU.add), after=[P.ctr["vector"]])
                    P.t(lambda e: e.transpose(ptr[0:64, :], ao[:], idb[:]),
                        after=[P.ctr["vector"]])
                    P.v(lambda e, h=h, T0=T0: e.tensor_copy(
                        att_fm[(h % 2) * 64:(h % 2) * 64 + 64,
                               (h // 2) * 2048 + T0:(h // 2) * 2048 + T0 + 128],
                        ptr[0:64, :]), after=[P.ctr["tensor"]])
                    if dbg and h == 0 and i == 3:
                        P.v(lambda e: e.tensor_copy(dump[:, 0:640], pt[:]),
                            after=[P.ctr["vector"]])
                        P.st(lambda e: e.dma_start(out=PD, in_=dump[:, 0:640]),
                             after=[P.ctr["vector"]])
                        P.v(lambda e: e.tensor_copy(dump[:, 0:65], po[:]))
                        P.st(lambda e: e.dma_start(out=OD, in_=dump[:, 0:65]),
                             after=[P.ctr["vector"]])
            if dbg and hp == 0:
                for seg in range(2):
                    P.v(lambda e, seg=seg: e.tensor_copy(dump[:], kA[:, seg * 1024:(seg + 1) * 1024]),
                        after=[P.ctr["vector"], P.dmo])
                    P.st(lambda e, seg=seg: e.dma_start(out=KD[:, seg * 1024:(seg + 1) * 1024],
                         in_=dump[:]), after=[P.ctr["vector"]])
                P.v(lambda e: e.tensor_copy(dump[:, 0:512], kA[:, 2048:2560]),
                    after=[P.ctr["vector"], P.dmo])
                P.st(lambda e: e.dma_start(out=KD[:, 2048:2560], in_=dump[:, 0:512]),
                     after=[P.ctr["vector"]])
                for seg in range(2):
                    P.v(lambda e, seg=seg: e.tensor_copy(dump[:], qA[:, seg * 1024:(seg + 1) * 1024]),
                        after=[P.ctr["vector"], P.dmo])
                    P.st(lambda e, seg=seg: e.dma_start(out=QD[:, seg * 1024:(seg + 1) * 1024],
                         in_=dump[:]), after=[P.ctr["vector"]])
        if dbg:
            P.barrier()
            for seg in range(16):
                P.v(lambda e, seg=seg: e.tensor_copy(dump[:], att_fm[:, seg * 1024:(seg + 1) * 1024]),
                    after=[P.ctr["vector"], P.dmo])
                P.st(lambda e, seg=seg: e.dma_start(out=AD[:, seg * 1024:(seg + 1) * 1024],
                     in_=dump[:]), after=[P.ctr["vector"]])
        for i in range(16):
            T0 = i * 128
            P.barrier()
            P.ld(lambda e, T0=T0: e.dma_start(out=xt[:], in_=XKV[512 + T0:512 + T0 + 128, :]),
                 after=ALL(P))
            for c in range(8):
                P.t(lambda e, c=c, T0=T0: e.matmul(
                    py0[:], att_fm[:, c * 2048 + T0:c * 2048 + T0 + 128],
                    wot[:, c * 1024:c * 1024 + 512],
                    start=(c == 0), stop=(c == 7)), after=[P.ctr["vector"]])
                P.t(lambda e, c=c, T0=T0: e.matmul(
                    py1[:], att_fm[:, c * 2048 + T0:c * 2048 + T0 + 128],
                    wot[:, c * 1024 + 512:c * 1024 + 1024],
                    start=(c == 0), stop=(c == 7)))
            yss = ys[:, (i % 8) * 1024:(i % 8) * 1024 + 1024]
            P.v(lambda e: e.nop(), after=[P.dma, P.ctr["tensor"], P.dmo])
            P.v(lambda e, yss=yss: e.tensor_tensor(yss[:, 0:512], py0[:], xt[:, 0:512], ALU.add))
            P.v(lambda e, yss=yss: e.tensor_tensor(yss[:, 512:1024], py1[:], xt[:, 512:1024], ALU.add))
            P.v(lambda e: e.nop(), after=[P.ctr["vector"]])
            emit_ln(P, yln, yss, dump, mcol, ssq, var, rstd, eps5)
            P.v(lambda e: e.tensor_tensor(yln[:], yln[:], g_bc[:], ALU.mult),
                after=[P.ctr["vector"]])
            P.v(lambda e: e.tensor_tensor(yln[:], yln[:], b_bc[:], ALU.add))
            P.v(lambda e, yss=yss: e.tensor_tensor(yss[:], yss[:], yln[:], ALU.add),
                after=[P.ctr["vector"]])
            P.st(lambda e, T0=T0, yss=yss: e.dma_start(out=Y[T0:T0 + 128, :], in_=yss[:]),
                 after=[P.ctr["vector"]])
        P.finish()
        P.play(block)
    return nc


def run_attn_a(x, Wq, Wk, Wv, Wo, ln_g, ln_b):
    nc = get_nc("attn_a")
    B, S, D = x.shape
    cc, tt = np.meshgrid(np.arange(640), np.arange(128), indexing="ij")
    wmt0 = ((cc - tt >= 1) & (cc - tt <= 512)).astype(np.float32)
    wmt = np.ascontiguousarray(
        wmt0.reshape(5, 128, 128).transpose(1, 0, 2).reshape(128, 640))
    in_maps = []
    for c in range(8):
        b, half = c // 2, c % 2
        qstart = half * 2048
        xkv = np.zeros((2560, 1024), np.float32)
        lo = qstart - 512
        xkv[max(0, -lo):] = x[b, max(lo, 0):qstart + 2048]
        km = np.ascontiguousarray(
            (np.arange(lo, lo + 2560) >= 0).astype(np.float32).reshape(20, 128).T)
        gq = qstart + np.arange(2048)
        smv = ((gq[None, :] >= np.arange(4)[:, None]) &
               (gq[None, :] - np.arange(4)[:, None] > 511)).astype(np.float32)
        in_maps.append({"XKV": np.ascontiguousarray(xkv), "XSK": np.ascontiguousarray(x[b, 0:4]),
                        "WQT": np.ascontiguousarray(Wq.T), "WKT": np.ascontiguousarray(Wk.T),
                        "WVT": np.ascontiguousarray(Wv.T), "WOT": np.ascontiguousarray(Wo.T),
                        "LNG": ln_g.reshape(1, 1024), "LNB": ln_b.reshape(1, 1024),
                        "WMT": wmt, "KM": km, "SM": smv.astype(BF),
                        "IDB": ID128B, "IDF": ID128F})
    res = run_bass_kernel_spmd(nc, in_maps, core_ids=list(range(8)))
    y = np.stack([res.results[c]["Y"] for c in range(8)], 0).reshape(4, 4096, 1024)
    return y




def build_mha(QOFF0):
    nc = bass.Bass("TRN2", target_bir_lowering=False)
    XF = nc.dram_tensor("XF", [4096, 1024], F32, kind="ExternalInput").ap()
    WQT = nc.dram_tensor("WQT", [1024, 1024], F32, kind="ExternalInput").ap()
    WKT = nc.dram_tensor("WKT", [1024, 1024], F32, kind="ExternalInput").ap()
    WVT = nc.dram_tensor("WVT", [1024, 1024], F32, kind="ExternalInput").ap()
    WOT = nc.dram_tensor("WOT", [1024, 1024], F32, kind="ExternalInput").ap()
    BQ = nc.dram_tensor("BQ", [128, 8], F32, kind="ExternalInput").ap()
    BK = nc.dram_tensor("BK", [128, 8], F32, kind="ExternalInput").ap()
    BVR = nc.dram_tensor("BVR", [1, 1024], F32, kind="ExternalInput").ap()
    BO = nc.dram_tensor("BO", [1, 1024], F32, kind="ExternalInput").ap()
    IDB = nc.dram_tensor("IDB", [128, 128], BF16, kind="ExternalInput").ap()
    IDF = nc.dram_tensor("IDF", [128, 128], F32, kind="ExternalInput").ap()
    Y = nc.dram_tensor("Y", [2048, 1024], F32, kind="ExternalOutput").ap()
    with ExitStack() as ctx:
        E = ctx.enter_context
        P = Prog(nc, ctx)
        x_fm = E(nc.sbuf_tensor("x_fm", [128, 8 * 4096], BF16))
        att_fm = E(nc.sbuf_tensor("att_fm", [128, 8 * 2048], BF16))
        wot = E(nc.sbuf_tensor("wot", [128, 8 * 1024], BF16))
        wpr = E(nc.sbuf_tensor("wpr", [128, 3 * 1024], BF16))
        wld = E(nc.sbuf_tensor("wld", [128, 1024], F32))
        kA = E(nc.sbuf_tensor("kA", [128, 4096], BF16))
        kB = E(nc.sbuf_tensor("kB", [128, 4096], BF16))
        qA = E(nc.sbuf_tensor("qA", [128, 2048], BF16))
        qB = E(nc.sbuf_tensor("qB", [128, 2048], BF16))
        vA = E(nc.sbuf_tensor("vA", [128, 32 * 65], BF16))
        vB = E(nc.sbuf_tensor("vB", [128, 32 * 65], BF16))
        bq = E(nc.sbuf_tensor("bq", [128, 8], F32))
        bk = E(nc.sbuf_tensor("bk", [128, 8], F32))
        bvrow = E(nc.sbuf_tensor("bvrow", [1, 1024], F32))
        bvA_bc = E(nc.sbuf_tensor("bvA_bc", [128, 64], F32))
        bvB_bc = E(nc.sbuf_tensor("bvB_bc", [128, 64], F32))
        pt = E(nc.sbuf_tensor("pt", [128, 4096], BF16))
        xt = E(nc.sbuf_tensor("xt", [128, 1024], F32))
        lnx = E(nc.sbuf_tensor("lnx", [128, 1024], F32))
        lnb_ = E(nc.sbuf_tensor("lnb_", [128, 1024], BF16))
        dump = E(nc.sbuf_tensor("dump", [128, 1024], F32))
        ao = E(nc.sbuf_tensor("ao", [128, 64], BF16))
        mx = E(nc.sbuf_tensor("mx", [128, 1], F32))
        mx2 = E(nc.sbuf_tensor("mx2", [128, 1], F32))
        rc = E(nc.sbuf_tensor("rc", [128, 1], F32))
        bo_bc = E(nc.sbuf_tensor("bo_bc", [128, 1024], F32))
        row1 = E(nc.sbuf_tensor("row1", [1, 1024], F32))
        ones1 = E(nc.sbuf_tensor("ones1", [1, 128], F32))
        ys = E(nc.sbuf_tensor("ys", [128, 2 * 1024], F32))
        mcol = E(nc.sbuf_tensor("mcol", [128, 1], F32))
        ssq = E(nc.sbuf_tensor("ssq", [128, 1], F32))
        var = E(nc.sbuf_tensor("var", [128, 1], F32))
        rstd = E(nc.sbuf_tensor("rstd", [128, 1], F32))
        eps5 = E(nc.sbuf_tensor("eps5", [128, 1], F32))
        idb = E(nc.sbuf_tensor("idb", [128, 128], BF16))
        idf = E(nc.sbuf_tensor("idf", [128, 128], F32))
        psA = E(nc.psum_tensor("psA", [128, 512], F32))
        p2 = E(nc.psum_tensor("p2", [128, 512], F32))
        po = E(nc.psum_tensor("po", [128, 65], F32))
        ptr = E(nc.psum_tensor("ptr", [128, 128], BF16))
        ptrf = E(nc.psum_tensor("ptrf", [128, 128], F32))
        py0 = E(nc.psum_tensor("py0", [128, 512], F32))
        py1 = E(nc.psum_tensor("py1", [128, 512], F32))
        block = E(nc.Block())

        P.ld(lambda e: e.dma_start(out=idb[:], in_=IDB))
        P.ld(lambda e: e.dma_start(out=idf[:], in_=IDF))
        P.ld(lambda e: e.dma_start(out=bq[:], in_=BQ))
        P.ld(lambda e: e.dma_start(out=bk[:], in_=BK))
        P.ld(lambda e: e.dma_start(out=bvrow[:], in_=BVR))
        P.ld(lambda e: e.dma_start(out=row1[:], in_=BO))
        P.v(lambda e: e.memset(eps5[:], 1e-5), after=[P.dma])
        P.v(lambda e: e.memset(ones1[:], 1.0))
        P.t(lambda e: e.matmul(py0[:], ones1[:], row1[:, 0:512], start=True, stop=True),
            after=[P.ctr["vector"], P.dma])
        P.t(lambda e: e.matmul(py1[:], ones1[:], row1[:, 512:1024], start=True, stop=True))
        P.v(lambda e: e.tensor_copy(bo_bc[:, 0:512], py0[:]), after=[P.ctr["tensor"]])
        P.v(lambda e: e.tensor_copy(bo_bc[:, 512:1024], py1[:]))

        for jt in range(32):
            P.ld(lambda e, jt=jt: e.dma_start(out=xt[:], in_=XF[jt * 128:(jt + 1) * 128, :]),
                 after=ALL(P))
            P.v(lambda e: e.nop(), after=[P.dma])
            emit_ln(P, lnx, xt, dump, mcol, ssq, var, rstd, eps5)
            P.v(lambda e: e.tensor_copy(lnb_[:], lnx[:]), after=[P.ctr["vector"]])
            emit_tr(P, (lambda c, jt=jt: x_fm[:, c * 4096 + jt * 128: c * 4096 + jt * 128 + 128]),
                    lnb_, ptr, idb, 8)
        for c in range(8):
            P.ld(lambda e, c=c: e.dma_start(out=wld[:], in_=WOT[c * 128:(c + 1) * 128, :]),
                 after=ALL(P))
            P.v(lambda e, c=c: e.tensor_copy(wot[:, c * 1024:(c + 1) * 1024], wld[:]),
                after=[P.dma])

        for hp in range(8):
            P.t(lambda e, hp=hp: e.matmul(
                ptrf[:, 0:64], ones1[:], bvrow[:, hp * 128:hp * 128 + 64],
                start=True, stop=True), after=[P.ctr["vector"], P.ctr["scalar"]])
            P.v(lambda e: e.tensor_copy(bvA_bc[:], ptrf[:, 0:64]), after=[P.ctr["tensor"]])
            P.t(lambda e, hp=hp: e.matmul(
                ptrf[:, 64:128], ones1[:], bvrow[:, hp * 128 + 64:hp * 128 + 128],
                start=True, stop=True), after=[P.ctr["vector"]])
            P.v(lambda e: e.tensor_copy(bvB_bc[:], ptrf[:, 64:128]), after=[P.ctr["tensor"]])
            for wi, W in enumerate([WQT, WKT, WVT]):
                P.ld(lambda e, wi=wi, W=W, hp=hp: e.dma_start(
                    out=wld[:].rearrange("p (c n) -> p c n", n=128),
                    in_=W[:, hp * 128:(hp + 1) * 128].rearrange("(c p) n -> p c n", p=128)),
                    after=ALL(P))
                P.v(lambda e, wi=wi: e.tensor_copy(wpr[:, wi * 1024:(wi + 1) * 1024], wld[:]),
                    after=[P.dma])
            P.v(lambda e: e.memset(kA[64:128, :], 0.0), after=[P.ctr["tensor"]])
            P.v(lambda e: e.memset(kB[64:128, :], 0.0))
            P.v(lambda e: e.memset(kA[64:65, :], 1.0))
            P.v(lambda e: e.memset(kB[64:65, :], 1.0))
            P.v(lambda e: e.memset(qA[64:128, :], 0.0))
            P.v(lambda e: e.memset(qB[64:128, :], 0.0))
            v3a = vA[:].rearrange("p (j c) -> p j c", c=65)
            v3b = vB[:].rearrange("p (j c) -> p j c", c=65)
            P.v(lambda e, v3a=v3a: e.memset(v3a[:, :, 64], 1.0))
            P.v(lambda e, v3b=v3b: e.memset(v3b[:, :, 64], 1.0))
            for j5 in range(8):
                for c in range(8):
                    P.t(lambda e, j5=j5, c=c: e.matmul(
                        psA[:], wpr[:, 1024 + c * 128:1024 + c * 128 + 128],
                        x_fm[:, c * 4096 + j5 * 512: c * 4096 + j5 * 512 + 512],
                        start=(c == 0), stop=(c == 7)), after=[P.ctr["vector"]])
                P.v(lambda e, j5=j5, hp=hp: e.tensor_scalar(
                    kA[0:64, j5 * 512:(j5 + 1) * 512], psA[0:64, :],
                    bk[0:64, hp:hp + 1], 0.0, ALU.add, ALU.add), after=[P.ctr["tensor"]])
                P.v(lambda e, j5=j5, hp=hp: e.tensor_scalar(
                    kB[0:64, j5 * 512:(j5 + 1) * 512], psA[64:128, :],
                    bk[64:128, hp:hp + 1], 0.0, ALU.add, ALU.add))
            for j4 in range(4):
                for c in range(8):
                    P.t(lambda e, j4=j4, c=c: e.matmul(
                        psA[:], wpr[:, c * 128:c * 128 + 128],
                        x_fm[:, c * 4096 + QOFF0 + j4 * 512: c * 4096 + QOFF0 + j4 * 512 + 512],
                        start=(c == 0), stop=(c == 7)), after=[P.ctr["vector"]])
                P.v(lambda e, j4=j4, hp=hp: e.tensor_scalar(
                    qA[0:64, j4 * 512:(j4 + 1) * 512], psA[0:64, :],
                    bq[0:64, hp:hp + 1], 0.0, ALU.add, ALU.add), after=[P.ctr["tensor"]])
                P.v(lambda e, j4=j4, hp=hp: e.tensor_scalar(
                    qB[0:64, j4 * 512:(j4 + 1) * 512], psA[64:128, :],
                    bq[64:128, hp:hp + 1], 0.0, ALU.add, ALU.add))
            for jt in range(32):
                for c in range(8):
                    P.t(lambda e, jt=jt, c=c: e.matmul(
                        p2[:, 0:128], x_fm[:, c * 4096 + jt * 128: c * 4096 + jt * 128 + 128],
                        wpr[:, 2048 + c * 128: 2048 + c * 128 + 128],
                        start=(c == 0), stop=(c == 7)), after=[P.ctr["vector"]])
                P.v(lambda e, jt=jt: e.tensor_tensor(
                    vA[:, jt * 65:jt * 65 + 64], p2[:, 0:64], bvA_bc[:, 0:64], ALU.add),
                    after=[P.ctr["tensor"]])
                P.v(lambda e, jt=jt: e.tensor_tensor(
                    vB[:, jt * 65:jt * 65 + 64], p2[:, 64:128], bvB_bc[:, 0:64], ALU.add))
            for hh in range(2):
                kX, qX, vX = (kA, qA, vA) if hh == 0 else (kB, qB, vB)
                for i in range(16):
                    T0 = i * 128
                    for j5 in range(8):
                        P.t(lambda e, qX=qX, kX=kX, T0=T0, j5=j5: e.matmul(
                            psA[:], qX[:, T0:T0 + 128], kX[:, j5 * 512:(j5 + 1) * 512],
                            start=True, stop=True), after=[P.ctr["vector"]])
                        if j5 == 0:
                            P.v(lambda e: e.tensor_reduce(mx[:], psA[:], AX.X, ALU.max),
                                after=[P.ctr["tensor"]])
                        else:
                            P.v(lambda e: e.tensor_reduce(mx2[:], psA[:], AX.X, ALU.max),
                                after=[P.ctr["tensor"]])
                            P.v(lambda e: e.tensor_tensor(mx[:], mx[:], mx2[:], ALU.max),
                                after=[P.ctr["vector"]])
                    P.v(lambda e: e.nop(), after=[P.ctr["vector"]])
                    P.t(lambda e: e.transpose(ptrf[0:1, 0:128], mx[:], idf[:]),
                        after=[P.ctr["vector"]])
                    P.v(lambda e, qX=qX, T0=T0: e.tensor_scalar(
                        qX[64:65, T0:T0 + 128], ptrf[0:1, 0:128], -8.0, 0.0,
                        ALU.mult, ALU.add), after=[P.ctr["tensor"]])
                    for g4 in range(8):
                        for q4 in range(4):
                            cjj = g4 * 4 + q4
                            P.t(lambda e, kX=kX, qX=qX, T0=T0, cjj=cjj, q4=q4: e.matmul(
                                p2[:, q4 * 128:(q4 + 1) * 128],
                                kX[:, cjj * 128:cjj * 128 + 128],
                                qX[:, T0:T0 + 128], start=True, stop=True),
                                after=[P.ctr["vector"], P.ctr["scalar"]])
                        P.s(lambda e, g4=g4: e.activation(
                            pt[:, g4 * 512:(g4 + 1) * 512], p2[:], AF.Exp, scale=0.125),
                            after=[P.ctr["tensor"]])
                    for cjj in range(32):
                        P.t(lambda e, cjj=cjj, vX=vX: e.matmul(
                            po[:], pt[:, cjj * 128:(cjj + 1) * 128],
                            vX[:, cjj * 65:cjj * 65 + 65],
                            start=(cjj == 0), stop=(cjj == 31)),
                            after=[P.ctr["scalar"]] if cjj == 0 else [])
                    P.v(lambda e: e.reciprocal(rc[:], po[:, 64:65]), after=[P.ctr["tensor"]])
                    P.v(lambda e: e.tensor_scalar(ao[:], po[:, 0:64], rc[:], 0.0,
                                                  ALU.mult, ALU.add), after=[P.ctr["vector"]])
                    P.t(lambda e: e.transpose(ptr[0:64, :], ao[:], idb[:]),
                        after=[P.ctr["vector"]])
                    h = hp * 2 + hh
                    P.v(lambda e, h=h, T0=T0: e.tensor_copy(
                        att_fm[(h % 2) * 64:(h % 2) * 64 + 64,
                               (h // 2) * 2048 + T0:(h // 2) * 2048 + T0 + 128],
                        ptr[0:64, :]), after=[P.ctr["tensor"]])
        for i in range(16):
            T0 = i * 128
            P.ld(lambda e, T0=T0: e.dma_start(out=xt[:], in_=XF[QOFF0 + T0:QOFF0 + T0 + 128, :]),
                 after=ALL(P))
            for c in range(8):
                P.t(lambda e, c=c, T0=T0: e.matmul(
                    py0[:], att_fm[:, c * 2048 + T0:c * 2048 + T0 + 128],
                    wot[:, c * 1024:c * 1024 + 512],
                    start=(c == 0), stop=(c == 7)), after=[P.ctr["vector"]])
                P.t(lambda e, c=c, T0=T0: e.matmul(
                    py1[:], att_fm[:, c * 2048 + T0:c * 2048 + T0 + 128],
                    wot[:, c * 1024 + 512:c * 1024 + 1024],
                    start=(c == 0), stop=(c == 7)))
            yss = ys[:, (i % 2) * 1024:(i % 2) * 1024 + 1024]
            P.v(lambda e: e.nop(), after=[P.dma, P.ctr["tensor"], P.dmo])
            P.v(lambda e, yss=yss: e.tensor_tensor(yss[:, 0:512], py0[:], xt[:, 0:512], ALU.add))
            P.v(lambda e, yss=yss: e.tensor_tensor(yss[:, 512:1024], py1[:], xt[:, 512:1024],
                                                   ALU.add))
            P.v(lambda e, yss=yss: e.tensor_tensor(yss[:], yss[:], bo_bc[:], ALU.add),
                after=[P.ctr["vector"]])
            P.st(lambda e, T0=T0, yss=yss: e.dma_start(out=Y[T0:T0 + 128, :], in_=yss[:]),
                 after=[P.ctr["vector"]])
        P.finish()
        P.play(block)
    return nc


def run_mha(x, Wq, bq, Wk, bk, Wv, bv, Wo, bo):
    WQT = np.ascontiguousarray(Wq.reshape(1024, 1024).T)
    WKT = np.ascontiguousarray(Wk.reshape(1024, 1024).T)
    WVT = np.ascontiguousarray(Wv.reshape(1024, 1024).T)
    bqf = np.ascontiguousarray(bq.reshape(1024).reshape(8, 128).T)
    bkf = np.ascontiguousarray(bk.reshape(1024).reshape(8, 128).T)
    in_maps = []
    for c in range(8):
        b, half = c // 2, c % 2
        xf = np.roll(x[b], -half * 2048, axis=0)
        in_maps.append({"XF": np.ascontiguousarray(xf),
                        "WQT": WQT, "WKT": WKT, "WVT": WVT,
                        "WOT": np.ascontiguousarray(Wo.T),
                        "BQ": bqf, "BK": bkf,
                        "BVR": np.ascontiguousarray(bv.reshape(1, 1024)),
                        "BO": bo.reshape(1, 1024),
                        "IDB": ID128B, "IDF": ID128F})
    res = run_bass_kernel_spmd(get_nc("mha0"), in_maps, core_ids=list(range(8)))
    y = np.stack([res.results[c]["Y"] for c in range(8)], 0)
    return y.reshape(4, 4096, 1024)


def build_gla(dbg=False):
    nc = bass.Bass("TRN2", target_bir_lowering=False)
    XB = nc.dram_tensor("XB", [4096, 1024], F32, kind="ExternalInput").ap()
    UPT = nc.dram_tensor("UPT", [1024, 2048], F32, kind="ExternalInput").ap()
    GATT = nc.dram_tensor("GATT", [1024, 2048], F32, kind="ExternalInput").ap()
    DWT = nc.dram_tensor("DWT", [2048, 1024], F32, kind="ExternalInput").ap()
    QWT = nc.dram_tensor("QWT", [2048, 64], F32, kind="ExternalInput").ap()
    KWT = nc.dram_tensor("KWT", [2048, 64], F32, kind="ExternalInput").ap()
    GWT = nc.dram_tensor("GWT", [2048, 64], F32, kind="ExternalInput").ap()
    TWT = nc.dram_tensor("TWT", [2048, 64], F32, kind="ExternalInput").ap()
    IG1T = nc.dram_tensor("IG1T", [2048, 16], F32, kind="ExternalInput").ap()
    IG2T = nc.dram_tensor("IG2T", [16, 2048], F32, kind="ExternalInput").ap()
    CW = nc.dram_tensor("CW", [128, 64], F32, kind="ExternalInput").ap()
    CB = nc.dram_tensor("CB", [128, 16], F32, kind="ExternalInput").ap()
    IGB2 = nc.dram_tensor("IGB2", [128, 16], F32, kind="ExternalInput").ap()
    RW = nc.dram_tensor("RW", [128, 16], F32, kind="ExternalInput").ap()
    NW = nc.dram_tensor("NW", [128, 16], F32, kind="ExternalInput").ap()
    GB = nc.dram_tensor("GB", [64, 1], F32, kind="ExternalInput").ap()
    TB = nc.dram_tensor("TB", [64, 1], F32, kind="ExternalInput").ap()
    LNG = nc.dram_tensor("LNG", [1, 1024], F32, kind="ExternalInput").ap()
    LNB = nc.dram_tensor("LNB", [1, 1024], F32, kind="ExternalInput").ap()
    IDB = nc.dram_tensor("IDB", [128, 128], BF16, kind="ExternalInput").ap()
    IDF = nc.dram_tensor("IDF", [128, 128], F32, kind="ExternalInput").ap()
    Y = nc.dram_tensor("Y", [4096, 1024], F32, kind="ExternalOutput").ap()
    if dbg:
        DSH = nc.dram_tensor("DSH", [128, 4096], F32, kind="ExternalOutput").ap()
        DXU = nc.dram_tensor("DXU", [128, 4096], F32, kind="ExternalOutput").ap()
        DQ4 = nc.dram_tensor("DQ4", [64, 4 * 256], F32, kind="ExternalOutput").ap()
        DB = nc.dram_tensor("DB", [64, 256], F32, kind="ExternalOutput").ap()
        DOF = nc.dram_tensor("DOF", [128, 4096], F32, kind="ExternalOutput").ap()
        DXG = nc.dram_tensor("DXG", [128, 4096], F32, kind="ExternalOutput").ap()
        DMB = nc.dram_tensor("DMB", [1, 256], F32, kind="ExternalOutput").ap()
        DON = nc.dram_tensor("DON", [128, 4096], F32, kind="ExternalOutput").ap()
    NT = 16
    with ExitStack() as ctx:
        E = ctx.enter_context
        P = Prog(nc, ctx)
        upt = E(nc.sbuf_tensor("upt", [128, 8 * 2048], BF16))
        gatt = E(nc.sbuf_tensor("gatt", [128, 8 * 2048], BF16))
        dwt = E(nc.sbuf_tensor("dwt", [128, 16 * 1024], BF16))
        qw4 = E(nc.sbuf_tensor("qw4", [128, 16 * 256], BF16))
        ig1 = E(nc.sbuf_tensor("ig1", [128, 16 * 16], BF16))
        ig2 = E(nc.sbuf_tensor("ig2", [16, 2048], BF16))
        cw = E(nc.sbuf_tensor("cw", [128, 64], F32))
        cb = E(nc.sbuf_tensor("cb", [128, 16], F32))
        igb2 = E(nc.sbuf_tensor("igb2", [128, 16], F32))
        rw = E(nc.sbuf_tensor("rw", [128, 16], F32))
        nw = E(nc.sbuf_tensor("nw", [128, 16], F32))
        gb = E(nc.sbuf_tensor("gb", [64, 1], F32))
        tb = E(nc.sbuf_tensor("tb", [64, 1], F32))
        wld = E(nc.sbuf_tensor("wld", [128, 512], F32))
        x_fm = E(nc.sbuf_tensor("x_fm", [128, 8 * 256], BF16))
        xup = E(nc.sbuf_tensor("xup", [128, 16 * 259], BF16))
        tails = E(nc.sbuf_tensor("tails", [128, 16 * 3], BF16))
        shf = E(nc.sbuf_tensor("shf", [128, 16 * 256], BF16))
        gat = E(nc.sbuf_tensor("gat", [128, 16 * 256], BF16))
        ofm = E(nc.sbuf_tensor("ofm", [128, 16 * 256], BF16))
        vt = E(nc.sbuf_tensor("vt", [64, 2048], BF16))
        ovb = E(nc.sbuf_tensor("ovb", [64, 512], BF16))
        S = E(nc.sbuf_tensor("S", [64, 2048], F32))
        Sb = E(nc.sbuf_tensor("Sb", [64, 2048], BF16))
        qf = E(nc.sbuf_tensor("qf", [64, 256], F32))
        kf = E(nc.sbuf_tensor("kf", [64, 256], F32))
        gf = E(nc.sbuf_tensor("gf", [64, 256], F32))
        tf = E(nc.sbuf_tensor("tf", [64, 256], F32))
        b0 = E(nc.sbuf_tensor("b0", [64, 256], F32))
        b1 = E(nc.sbuf_tensor("b1", [64, 256], F32))
        qe = E(nc.sbuf_tensor("qe", [64, 256], BF16))
        ke = E(nc.sbuf_tensor("ke", [64, 256], BF16))
        keS = E(nc.sbuf_tensor("keS", [64, 64], BF16))
        keT = E(nc.sbuf_tensor("keT", [64, 64], BF16))
        ATm = E(nc.sbuf_tensor("ATm", [64, 64], BF16))
        m01f = E(nc.sbuf_tensor("m01f", [64, 64], F32))
        sc1 = E(nc.sbuf_tensor("sc1", [64, 256], F32))
        sc2 = E(nc.sbuf_tensor("sc2", [64, 256], F32))
        sc3 = E(nc.sbuf_tensor("sc3", [64, 256], F32))
        ebt = E(nc.sbuf_tensor("ebt", [64, 4], F32))
        pre1s = E(nc.sbuf_tensor("pre1s", [128, 16], F32))
        pre1t = E(nc.sbuf_tensor("pre1t", [16, 256], BF16))
        sig = E(nc.sbuf_tensor("sig", [128, 256], F32))
        mrow = E(nc.sbuf_tensor("mrow", [1, 256], F32))
        mbc = E(nc.sbuf_tensor("mbc", [128, 256], BF16))
        xt = E(nc.sbuf_tensor("xt", [128, 1024], F32))
        lnx = E(nc.sbuf_tensor("lnx", [128, 1024], F32))
        mcol = E(nc.sbuf_tensor("mcol", [128, 1], F32))
        ssq = E(nc.sbuf_tensor("ssq", [128, 1], F32))
        var = E(nc.sbuf_tensor("var", [128, 1], F32))
        rstd = E(nc.sbuf_tensor("rstd", [128, 1], F32))
        eps5 = E(nc.sbuf_tensor("eps5", [128, 1], F32))
        g_bc = E(nc.sbuf_tensor("g_bc", [128, 1024], F32))
        b_bc = E(nc.sbuf_tensor("b_bc", [128, 1024], F32))
        row1 = E(nc.sbuf_tensor("row1", [1, 1024], F32))
        ones1 = E(nc.sbuf_tensor("ones1", [1, 128], F32))
        onesc = E(nc.sbuf_tensor("onesc", [128, 1], F32))
        ioti = E(nc.sbuf_tensor("ioti", [64, 64], I32))
        ys = E(nc.sbuf_tensor("ys", [128, 1024], F32))
        yln = E(nc.sbuf_tensor("yln", [128, 1024], F32))
        idb = E(nc.sbuf_tensor("idb", [128, 128], BF16))
        idf = E(nc.sbuf_tensor("idf", [128, 128], F32))
        pA = E(nc.psum_tensor("pA", [128, 256], F32))
        po = E(nc.psum_tensor("po", [64, 512], F32))
        pkv = E(nc.psum_tensor("pkv", [64, 512], F32))
        ptr = E(nc.psum_tensor("ptr", [128, 128], BF16))
        ptrf = E(nc.psum_tensor("ptrf", [128, 256], F32))
        py0 = E(nc.psum_tensor("py0", [128, 512], F32))
        py1 = E(nc.psum_tensor("py1", [128, 512], F32))
        block = E(nc.Block())

        P.ld(lambda e: e.dma_start(out=idb[:], in_=IDB))
        P.ld(lambda e: e.dma_start(out=idf[:], in_=IDF))
        for dst, src in [(cw, CW), (cb, CB), (igb2, IGB2), (rw, RW), (nw, NW),
                         (gb, GB), (tb, TB)]:
            P.ld(lambda e, dst=dst, src=src: e.dma_start(out=dst[:], in_=src))
        P.v(lambda e: e.memset(eps5[:], 1e-5), after=[P.dma])
        P.v(lambda e: e.memset(ones1[:], 1.0))
        P.v(lambda e: e.memset(onesc[:], 1.0))
        P.v(lambda e: e.memset(S[:], 0.0))
        P.v(lambda e: e.memset(Sb[:], 0.0))
        P.v(lambda e: e.memset(tails[:], 0.0))
        P.g(lambda e: e.iota(ioti[:], [[1, 64]], base=0, channel_multiplier=-1), after=[P.dma])
        P.v(lambda e: e.tensor_copy(sc1[:, 0:64], ioti[:]), after=[P.ctr["gpsimd"]])
        P.v(lambda e: e.tensor_scalar(m01f[:], sc1[:, 0:64], 0.0, 0.0, ALU.is_ge, ALU.add),
            after=[P.ctr["vector"]])
        P.ld(lambda e: e.dma_start(out=row1[:], in_=LNG), after=[P.ctr["vector"]])
        P.t(lambda e: e.matmul(py0[:], ones1[:], row1[:, 0:512], start=True, stop=True),
            after=[P.ctr["vector"], P.dma])
        P.t(lambda e: e.matmul(py1[:], ones1[:], row1[:, 512:1024], start=True, stop=True))
        P.v(lambda e: e.tensor_copy(g_bc[:, 0:512], py0[:]), after=[P.ctr["tensor"]])
        P.v(lambda e: e.tensor_copy(g_bc[:, 512:1024], py1[:]))
        P.ld(lambda e: e.dma_start(out=row1[:], in_=LNB), after=[P.ctr["tensor"], P.ctr["vector"]])
        P.t(lambda e: e.matmul(py0[:], ones1[:], row1[:, 0:512], start=True, stop=True),
            after=[P.ctr["vector"], P.dma])
        P.t(lambda e: e.matmul(py1[:], ones1[:], row1[:, 512:1024], start=True, stop=True))
        P.v(lambda e: e.tensor_copy(b_bc[:, 0:512], py0[:]), after=[P.ctr["tensor"]])
        P.v(lambda e: e.tensor_copy(b_bc[:, 512:1024], py1[:]))
        for c in range(8):
            for hf in range(4):
                P.ld(lambda e, c=c, hf=hf: e.dma_start(
                    out=wld[:], in_=UPT[c * 128:(c + 1) * 128, hf * 512:(hf + 1) * 512]),
                    after=ALL(P))
                P.v(lambda e, c=c, hf=hf: e.tensor_copy(
                    upt[:, c * 2048 + hf * 512:c * 2048 + hf * 512 + 512], wld[:]),
                    after=[P.dma])
        for c in range(8):
            for hf in range(4):
                P.ld(lambda e, c=c, hf=hf: e.dma_start(
                    out=wld[:], in_=GATT[c * 128:(c + 1) * 128, hf * 512:(hf + 1) * 512]),
                    after=ALL(P))
                P.v(lambda e, c=c, hf=hf: e.tensor_copy(
                    gatt[:, c * 2048 + hf * 512:c * 2048 + hf * 512 + 512], wld[:]),
                    after=[P.dma])
        for c in range(16):
            for hf in range(2):
                P.ld(lambda e, c=c, hf=hf: e.dma_start(
                    out=wld[:], in_=DWT[c * 128:(c + 1) * 128, hf * 512:(hf + 1) * 512]),
                    after=ALL(P))
                P.v(lambda e, c=c, hf=hf: e.tensor_copy(
                    dwt[:, c * 1024 + hf * 512:c * 1024 + hf * 512 + 512], wld[:]),
                    after=[P.dma])
            P.v(lambda e, c=c: e.tensor_scalar(dwt[:, c * 1024:(c + 1) * 1024],
                                               dwt[:, c * 1024:(c + 1) * 1024],
                                               nw[:, c:c + 1], 0.0, ALU.mult, ALU.add),
                after=[P.ctr["vector"]])
        for wi, W in enumerate([QWT, KWT, GWT, TWT]):
            for c in range(16):
                P.ld(lambda e, W=W, c=c: e.dma_start(out=wld[:, 0:64],
                                                     in_=W[c * 128:(c + 1) * 128, :]),
                     after=ALL(P))
                P.v(lambda e, wi=wi, c=c: e.tensor_copy(
                    qw4[:, c * 256 + wi * 64:c * 256 + wi * 64 + 64], wld[:, 0:64]),
                    after=[P.dma])
        for c in range(16):
            P.ld(lambda e, c=c: e.dma_start(out=wld[:, 0:16], in_=IG1T[c * 128:(c + 1) * 128, :]),
                 after=ALL(P))
            P.v(lambda e, c=c: e.tensor_copy(ig1[:, c * 16:(c + 1) * 16], wld[:, 0:16]),
                after=[P.dma])
        for hf in range(4):
            P.ld(lambda e, hf=hf: e.dma_start(out=wld[0:16, :],
                                              in_=IG2T[:, hf * 512:(hf + 1) * 512]),
                 after=ALL(P))
            P.v(lambda e, hf=hf: e.tensor_copy(ig2[:, hf * 512:(hf + 1) * 512],
                                               wld[0:16, :]), after=[P.dma])

        for tt in range(NT):
            R0 = tt * 256
            for c in range(16):
                P.v(lambda e, c=c: e.tensor_copy(xup[:, c * 259:c * 259 + 3],
                                                 tails[:, c * 3:(c + 1) * 3]),
                    after=([P.ctr["tensor"], P.ctr["scalar"], P.ctr["vector"]] if c == 0 else []))
            for half in range(2):
                P.ld(lambda e, R0=R0, half=half: e.dma_start(
                    out=xt[:], in_=XB[R0 + half * 128:R0 + half * 128 + 128, :]), after=ALL(P))
                P.v(lambda e: e.nop(), after=[P.dma])
                emit_ln(P, lnx, xt, yln, mcol, ssq, var, rstd, eps5)
                for c8 in range(8):
                    P.t(lambda e, c8=c8: e.transpose(ptrf[:, 0:128],
                                                     lnx[:, c8 * 128:(c8 + 1) * 128], idf[:]),
                        after=[P.ctr["vector"]])
                    P.v(lambda e, c8=c8, half=half: e.tensor_copy(
                        x_fm[:, c8 * 256 + half * 128:c8 * 256 + half * 128 + 128],
                        ptrf[:, 0:128]), after=[P.ctr["tensor"]])
            for c in range(16):
                for d in range(8):
                    P.t(lambda e, c=c, d=d: e.matmul(
                        pA[:], upt[:, d * 2048 + c * 128:d * 2048 + c * 128 + 128],
                        x_fm[:, d * 256:(d + 1) * 256],
                        start=(d == 0), stop=(d == 7)), after=[P.ctr["vector"]])
                P.v(lambda e, c=c: e.tensor_copy(xup[:, c * 259 + 3:c * 259 + 259], pA[:]),
                    after=[P.ctr["tensor"]])
                for d in range(8):
                    P.t(lambda e, c=c, d=d: e.matmul(
                        pA[:], gatt[:, d * 2048 + c * 128:d * 2048 + c * 128 + 128],
                        x_fm[:, d * 256:(d + 1) * 256],
                        start=(d == 0), stop=(d == 7)), after=[P.ctr["vector"]])
                P.s(lambda e, c=c: e.activation(gat[:, c * 256:(c + 1) * 256], pA[:], AF.Silu),
                    after=[P.ctr["tensor"], P.ctr["vector"]])
                P.v(lambda e, c=c: e.tensor_scalar(sig[:], xup[:, c * 259:c * 259 + 256],
                                                   cw[:, c * 4:c * 4 + 1], 0.0,
                                                   ALU.mult, ALU.add),
                    after=[P.ctr["vector"], P.ctr["scalar"]])
                for j in range(1, 4):
                    P.v(lambda e, c=c, j=j: e.scalar_tensor_tensor(
                        sig[:], xup[:, c * 259 + j:c * 259 + j + 256],
                        cw[:, c * 4 + j:c * 4 + j + 1], sig[:], ALU.mult, ALU.add))
                P.s(lambda e, c=c: e.activation(shf[:, c * 256:(c + 1) * 256], sig[:], AF.Silu,
                                                bias=cb[:, c:c + 1]),
                    after=[P.ctr["vector"]])
                P.v(lambda e, c=c: e.tensor_copy(tails[:, c * 3:(c + 1) * 3],
                                                 xup[:, c * 259 + 256:c * 259 + 259]))
            for wi, dst in enumerate([qf, kf, gf, tf]):
                for c in range(16):
                    P.t(lambda e, wi=wi, c=c: e.matmul(
                        po[0:64, 0:256], qw4[:, c * 256 + wi * 64:c * 256 + wi * 64 + 64],
                        shf[:, c * 256:(c + 1) * 256],
                        start=(c == 0), stop=(c == 15)),
                        after=[P.ctr["vector"], P.ctr["scalar"]])
                if wi < 2:
                    P.v(lambda e, dst=dst: e.tensor_copy(dst[:], po[0:64, 0:256]),
                        after=[P.ctr["tensor"]])
                else:
                    bias = gb if wi == 2 else tb
                    P.v(lambda e, dst=dst, bias=bias: e.tensor_scalar(
                        dst[:], po[0:64, 0:256], bias[:], 0.0, ALU.add, ALU.add),
                        after=[P.ctr["tensor"]])
            P.s(lambda e: e.activation(sc1[:], gf[:], AF.Abs), after=[P.ctr["vector"]])
            P.s(lambda e: e.activation(sc1[:], sc1[:], AF.Exp, scale=-1.0))
            P.v(lambda e: e.tensor_scalar(sc1[:], sc1[:], 1.0, 0.0, ALU.add, ALU.add),
                after=[P.ctr["scalar"]])
            P.s(lambda e: e.activation(sc1[:], sc1[:], AF.Ln), after=[P.ctr["vector"]])
            P.s(lambda e: e.activation(sc2[:], gf[:], AF.Relu))
            P.v(lambda e: e.tensor_tensor(gf[:], sc1[:], sc2[:], ALU.add),
                after=[P.ctr["scalar"]])
            P.s(lambda e: e.activation(tf[:], tf[:], AF.Sigmoid), after=[P.ctr["vector"]])
            P.v(lambda e: e.tensor_scalar(sc1[:], gf[:], 1e-18, 0.0, ALU.max, ALU.add),
                after=[P.ctr["scalar"]])
            P.s(lambda e: e.activation(sc1[:], sc1[:], AF.Ln), after=[P.ctr["vector"]])
            P.v(lambda e: e.tensor_tensor(sc1[:], sc1[:], tf[:], ALU.mult),
                after=[P.ctr["scalar"]])
            P.s(lambda e: e.activation(sc1[:], sc1[:], AF.Exp), after=[P.ctr["vector"]])
            P.v(lambda e: e.tensor_tensor(b0[:], gf[:], tf[:], ALU.mult),
                after=[P.ctr["scalar"]])
            P.v(lambda e: e.tensor_scalar(b0[:], b0[:], -1.0, 0.0, ALU.mult, ALU.add),
                after=[P.ctr["vector"]])
            P.s(lambda e: e.activation(sc2[:], kf[:], AF.Square), after=[P.ctr["vector"]])
            P.t(lambda e: e.matmul(ptrf[0:1, 0:256], onesc[0:64, :], sc2[:],
                                   start=True, stop=True),
                after=[P.ctr["scalar"], P.ctr["vector"]])
            P.s(lambda e: e.activation(mrow[:], ptrf[0:1, 0:256], AF.Sqrt),
                after=[P.ctr["tensor"]])
            P.v(lambda e: e.tensor_scalar(mrow[:], mrow[:], 1e-12, 0.0, ALU.max, ALU.add),
                after=[P.ctr["scalar"]])
            P.v(lambda e: e.reciprocal(mrow[:], mrow[:]), after=[P.ctr["vector"]])
            P.v(lambda e: e.nop(), after=[P.ctr["vector"]])
            P.t(lambda e: e.matmul(ptrf[0:64, 0:256], ones1[:, 0:64], mrow[:],
                                   start=True, stop=True), after=[P.ctr["vector"]])
            P.v(lambda e: e.tensor_tensor(kf[:], kf[:], ptrf[0:64, 0:256], ALU.mult),
                after=[P.ctr["tensor"]])
            P.v(lambda e: e.tensor_tensor(kf[:], kf[:], sc1[:], ALU.mult),
                after=[P.ctr["vector"]])
            src_, dst_ = b0, b1
            for st in [1, 2, 4, 8, 16, 32]:
                s3 = src_[:].rearrange("k (c t) -> k c t", t=64)
                d3 = dst_[:].rearrange("k (c t) -> k c t", t=64)
                P.v(lambda e, s3=s3, d3=d3, st=st: e.tensor_copy(d3[:, :, 0:st],
                                                                 s3[:, :, 0:st]),
                    after=[P.ctr["vector"]])
                P.v(lambda e, s3=s3, d3=d3, st=st: e.tensor_tensor(
                    d3[:, :, st:64], s3[:, :, st:64], s3[:, :, 0:64 - st], ALU.add))
                src_, dst_ = dst_, src_
            bb = src_
            P.v(lambda e: e.nop(), after=[P.ctr["vector"]])
            P.s(lambda e, bb=bb: e.activation(sc2[:], bb[:], AF.Exp),
                after=[P.ctr["vector"], P.ctr["tensor"]])
            P.s(lambda e, bb=bb: e.activation(sc3[:], bb[:], AF.Exp, scale=-1.0))
            P.v(lambda e: e.tensor_scalar(qf[:], qf[:], 0.125, 0.0, ALU.mult, ALU.add),
                after=[P.ctr["scalar"]])
            P.v(lambda e: e.tensor_tensor(qe[:], qf[:], sc2[:], ALU.mult),
                after=[P.ctr["vector"]])
            P.v(lambda e: e.tensor_tensor(ke[:], kf[:], sc3[:], ALU.mult))
            b3 = bb[:].rearrange("k (c t) -> k c t", t=64)
            P.s(lambda e, b3=b3: e.activation(ebt[:], b3[:, :, 63], AF.Exp),
                after=[P.ctr["vector"]])
            for half in range(2):
                for c in range(16):
                    P.t(lambda e, half=half, c=c: e.matmul(
                        pA[0:128, 0:16],
                        xup[:, c * 259 + 3 + half * 128:c * 259 + 3 + half * 128 + 128],
                        ig1[:, c * 16:(c + 1) * 16],
                        start=(c == 0), stop=(c == 15)),
                        after=[P.ctr["vector"], P.ctr["scalar"]])
                P.v(lambda e: e.tensor_copy(pre1s[:], pA[0:128, 0:16]), after=[P.ctr["tensor"]])
                P.v(lambda e: e.nop(), after=[P.ctr["vector"]])
                P.t(lambda e: e.transpose(ptrf[0:16, 0:128], pre1s[:], idf[:]),
                    after=[P.ctr["vector"]])
                P.v(lambda e, half=half: e.tensor_copy(pre1t[:, half * 128:half * 128 + 128],
                                                       ptrf[0:16, 0:128]),
                    after=[P.ctr["tensor"]])
            for c in range(16):
                P.t(lambda e, c=c: e.matmul(pA[:], ig2[:, c * 128:(c + 1) * 128], pre1t[:],
                                            start=True, stop=True),
                    after=[P.ctr["vector"], P.ctr["scalar"]])
                P.s(lambda e, c=c: e.activation(sig[:], pA[:], AF.Sigmoid,
                                                bias=igb2[:, c:c + 1]),
                    after=[P.ctr["tensor"], P.ctr["vector"]])
                P.v(lambda e, c=c: e.tensor_tensor(xup[:, c * 259 + 3:c * 259 + 259],
                                                   sig[:], xup[:, c * 259 + 3:c * 259 + 259],
                                                   ALU.mult),
                    after=[P.ctr["scalar"]])
            for ch in range(4):
                C0 = ch * 64
                for c in range(16):
                    P.t(lambda e, c=c, C0=C0: e.transpose(
                        ptr[0:64, :], xup[:, c * 259 + 3 + C0:c * 259 + 3 + C0 + 64],
                        idb[:]), after=[P.ctr["vector"]])
                    P.v(lambda e, c=c: e.tensor_copy(vt[:, c * 128:(c + 1) * 128],
                                                     ptr[0:64, 0:128]),
                        after=[P.ctr["tensor"]])
                P.v(lambda e, C0=C0: e.tensor_copy(keS[:], ke[:, C0:C0 + 64]),
                    after=[P.ctr["vector"]])
                P.v(lambda e: e.nop(), after=[P.ctr["vector"]])
                P.t(lambda e: e.transpose(ptr[0:64, 0:64], keS[:], idb[0:64, 0:64]),
                    after=[P.ctr["vector"]])
                P.v(lambda e: e.tensor_copy(keT[:], ptr[0:64, 0:64]), after=[P.ctr["tensor"]])
                P.t(lambda e, C0=C0: e.matmul(ptrf[0:64, 0:64], ke[:, C0:C0 + 64],
                                              qe[:, C0:C0 + 64], start=True, stop=True),
                    after=[P.ctr["vector"]])
                P.v(lambda e: e.tensor_tensor(ATm[:], ptrf[0:64, 0:64], m01f[:], ALU.mult),
                    after=[P.ctr["tensor"]])
                P.v(lambda e: e.nop(), after=[P.ctr["vector"]])
                for vtl in range(4):
                    V0 = vtl * 512
                    P.t(lambda e, C0=C0, V0=V0: e.matmul(
                        po[:], qe[:, C0:C0 + 64], Sb[:, V0:V0 + 512],
                        start=True, stop=False), after=[P.ctr["vector"]])
                    P.t(lambda e, V0=V0: e.matmul(
                        po[:], ATm[:], vt[:, V0:V0 + 512], start=False, stop=True))
                    P.t(lambda e, C0=C0, V0=V0: e.matmul(
                        pkv[:], keT[:], vt[:, V0:V0 + 512], start=True, stop=True))
                    P.v(lambda e, V0=V0: e.tensor_copy(ovb[:], po[:]), after=[P.ctr["tensor"]])
                    P.v(lambda e, V0=V0: e.tensor_tensor(S[:, V0:V0 + 512], S[:, V0:V0 + 512],
                                                         pkv[:], ALU.add))
                    P.v(lambda e, V0=V0, ch=ch: e.tensor_scalar(
                        S[:, V0:V0 + 512], S[:, V0:V0 + 512], ebt[:, ch:ch + 1], 0.0,
                        ALU.mult, ALU.add), after=[P.ctr["vector"]])
                    P.v(lambda e, V0=V0: e.tensor_copy(Sb[:, V0:V0 + 512], S[:, V0:V0 + 512]),
                        after=[P.ctr["vector"]])
                    for q4 in range(4):
                        P.t(lambda e, q4=q4: e.transpose(
                            ptr[0:128, 0:64], ovb[:, q4 * 128:(q4 + 1) * 128],
                            idb[0:64, 0:64]), after=[P.ctr["vector"]])
                        P.v(lambda e, q4=q4, V0=V0, C0=C0: e.tensor_copy(
                            ofm[:, (V0 // 128 + q4) * 256 + C0:(V0 // 128 + q4) * 256 + C0 + 64],
                            ptr[0:128, 0:64]), after=[P.ctr["tensor"]])
            if dbg and tt == 0:
                P.barrier()
                for c in range(16):
                    P.v(lambda e, c=c: e.tensor_copy(lnx[:, 0:256], ofm[:, c * 256:(c + 1) * 256]),
                        after=[P.ctr["vector"], P.dmo])
                    P.st(lambda e, c=c: e.dma_start(out=DOF[:, c * 256:(c + 1) * 256],
                                                    in_=lnx[:, 0:256]), after=[P.ctr["vector"]])
                    P.v(lambda e, c=c: e.tensor_copy(lnx[:, 0:256], shf[:, c * 256:(c + 1) * 256]),
                        after=[P.ctr["vector"], P.dmo])
                    P.st(lambda e, c=c: e.dma_start(out=DSH[:, c * 256:(c + 1) * 256],
                                                    in_=lnx[:, 0:256]), after=[P.ctr["vector"]])
                    P.v(lambda e, c=c: e.tensor_copy(lnx[:, 0:256],
                                                     xup[:, c * 259 + 3:c * 259 + 259]),
                        after=[P.ctr["vector"], P.dmo])
                    P.st(lambda e, c=c: e.dma_start(out=DXU[:, c * 256:(c + 1) * 256],
                                                    in_=lnx[:, 0:256]), after=[P.ctr["vector"]])
                for wi, srcq in enumerate([qf, kf, gf, tf]):
                    P.v(lambda e, srcq=srcq: e.tensor_copy(sc2[:], srcq[:]),
                        after=[P.ctr["vector"], P.dmo])
                    P.st(lambda e, wi=wi: e.dma_start(out=DQ4[:, wi * 256:(wi + 1) * 256],
                                                      in_=sc2[:]), after=[P.ctr["vector"]])
                P.v(lambda e, bb=bb: e.tensor_copy(sc2[:], bb[:]),
                    after=[P.ctr["vector"], P.dmo])
                P.st(lambda e: e.dma_start(out=DB, in_=sc2[:]), after=[P.ctr["vector"]])
                P.barrier()
            for c in range(16):
                P.v(lambda e, c=c: e.scalar_tensor_tensor(
                    ofm[:, c * 256:(c + 1) * 256],
                    shf[:, c * 256:(c + 1) * 256], rw[:, c:c + 1],
                    ofm[:, c * 256:(c + 1) * 256], ALU.mult, ALU.add),
                    after=[P.ctr["vector"], P.ctr["tensor"]] if c == 0 else [])
                P.v(lambda e, c=c: e.tensor_tensor(ofm[:, c * 256:(c + 1) * 256],
                                                   ofm[:, c * 256:(c + 1) * 256],
                                                   gat[:, c * 256:(c + 1) * 256], ALU.mult))
            P.v(lambda e: e.nop(), after=[P.ctr["vector"]])
            if dbg and tt == 0:
                P.barrier()
                for c in range(16):
                    P.v(lambda e, c=c: e.tensor_copy(lnx[:, 0:256], ofm[:, c * 256:(c + 1) * 256]),
                        after=[P.ctr["vector"], P.dmo])
                    P.st(lambda e, c=c: e.dma_start(out=DXG[:, c * 256:(c + 1) * 256],
                                                    in_=lnx[:, 0:256]), after=[P.ctr["vector"]])
                P.barrier()
            for c in range(16):
                P.s(lambda e, c=c: e.activation(sig[:], ofm[:, c * 256:(c + 1) * 256],
                                                AF.Square),
                    after=[P.ctr["vector"], P.ctr["tensor"]])
                P.t(lambda e, c=c: e.matmul(ptrf[0:1, 0:256], onesc[:], sig[:],
                                            start=(c == 0), stop=(c == 15)),
                    after=[P.ctr["scalar"]])
            P.s(lambda e: e.activation(mrow[:], ptrf[0:1, 0:256], AF.Sqrt, scale=1.0 / 2048,
                                       bias=eps5[0:1, :]), after=[P.ctr["tensor"]])
            P.v(lambda e: e.reciprocal(mrow[:], mrow[:]), after=[P.ctr["scalar"]])
            P.v(lambda e: e.nop(), after=[P.ctr["vector"]])
            P.t(lambda e: e.matmul(ptrf[:, 0:256], ones1[:], mrow[:], start=True, stop=True),
                after=[P.ctr["vector"]])
            P.v(lambda e: e.tensor_copy(mbc[:], ptrf[:, 0:256]), after=[P.ctr["tensor"]])
            P.v(lambda e: e.nop(), after=[P.ctr["vector"]])
            for c in range(16):
                P.v(lambda e, c=c: e.tensor_tensor(ofm[:, c * 256:(c + 1) * 256],
                                                   ofm[:, c * 256:(c + 1) * 256],
                                                   mbc[:], ALU.mult))
            P.v(lambda e: e.nop(), after=[P.ctr["vector"]])
            if dbg and tt == 0:
                P.barrier()
                P.v(lambda e: e.tensor_copy(mrow[:], mrow[:]), after=[P.dmo])
                P.st(lambda e: e.dma_start(out=DMB, in_=mrow[:]), after=[P.ctr["vector"]])
                for c in range(16):
                    P.v(lambda e, c=c: e.tensor_copy(lnx[:, 0:256], ofm[:, c * 256:(c + 1) * 256]),
                        after=[P.ctr["vector"], P.dmo])
                    P.st(lambda e, c=c: e.dma_start(out=DON[:, c * 256:(c + 1) * 256],
                                                    in_=lnx[:, 0:256]), after=[P.ctr["vector"]])
                P.barrier()
            for half in range(2):
                for c in range(16):
                    P.t(lambda e, half=half, c=c: e.matmul(
                        py0[:], ofm[:, c * 256 + half * 128:c * 256 + half * 128 + 128],
                        dwt[:, c * 1024:c * 1024 + 512],
                        start=(c == 0), stop=(c == 15)), after=[P.ctr["vector"]])
                    P.t(lambda e, half=half, c=c: e.matmul(
                        py1[:], ofm[:, c * 256 + half * 128:c * 256 + half * 128 + 128],
                        dwt[:, c * 1024 + 512:c * 1024 + 1024],
                        start=(c == 0), stop=(c == 15)))
                P.ld(lambda e, R0=R0, half=half: e.dma_start(
                    out=xt[:], in_=XB[R0 + half * 128:R0 + half * 128 + 128, :]),
                    after=ALL(P))
                yss = ys[:, 0:1024]
                P.v(lambda e: e.nop(), after=[P.dma, P.ctr["tensor"], P.dmo])
                P.v(lambda e, yss=yss: e.tensor_tensor(yss[:, 0:512], py0[:], xt[:, 0:512],
                                                       ALU.add))
                P.v(lambda e, yss=yss: e.tensor_tensor(yss[:, 512:1024], py1[:],
                                                       xt[:, 512:1024], ALU.add))
                P.v(lambda e: e.nop(), after=[P.ctr["vector"]])
                emit_ln(P, yln, yss, lnx, mcol, ssq, var, rstd, eps5)
                P.v(lambda e: e.tensor_tensor(yln[:], yln[:], g_bc[:], ALU.mult),
                    after=[P.ctr["vector"]])
                P.v(lambda e: e.tensor_tensor(yln[:], yln[:], b_bc[:], ALU.add))
                P.v(lambda e, yss=yss: e.tensor_tensor(yss[:], yss[:], yln[:], ALU.add),
                    after=[P.ctr["vector"]])
                P.st(lambda e, R0=R0, half=half, yss=yss: e.dma_start(
                    out=Y[R0 + half * 128:R0 + half * 128 + 128, :], in_=yss[:]),
                    after=[P.ctr["vector"]])
        P.finish()
        P.play(block)
    return nc


def run_gla(x, d):
    def cs(a):
        return np.ascontiguousarray(a.astype(np.float32))
    pre = "L0_attn_b_"
    cw = d[pre + "conv_w"].reshape(16, 128, 4).transpose(1, 0, 2).reshape(128, 64)
    im = {"UPT": cs(d[pre + "up_w"].T), "GATT": cs(d[pre + "gate_w"].T),
          "DWT": cs(d[pre + "down_w"].T),
          "QWT": cs(d[pre + "q_w"].T), "KWT": cs(d[pre + "k_w"].T),
          "GWT": cs(d[pre + "g_w"].T), "TWT": cs(d[pre + "tau_w"].T),
          "IG1T": cs(d[pre + "ig_w1"].T), "IG2T": cs(d[pre + "ig_w2"].T),
          "CW": cs(cw), "CB": cs(d[pre + "conv_b"].reshape(16, 128).T),
          "IGB2": cs(d[pre + "ig_b2"].reshape(16, 128).T),
          "RW": cs(d[pre + "residual_weight"].reshape(16, 128).T),
          "NW": cs(d[pre + "norm_weight"].reshape(16, 128).T),
          "GB": cs(d[pre + "g_b"].reshape(64, 1)), "TB": cs(d[pre + "tau_b"].reshape(64, 1)),
          "LNG": cs(d["L0_ffn_ln_g"].reshape(1, 1024)),
          "LNB": cs(d["L0_ffn_ln_b"].reshape(1, 1024)),
          "IDB": ID128B, "IDF": ID128F}
    in_maps = []
    for c in range(8):
        m = dict(im)
        m["XB"] = np.ascontiguousarray(x[c // 2])
        in_maps.append(m)
    res = run_bass_kernel_spmd(get_nc("gla"), in_maps, core_ids=list(range(8)))
    return np.stack([res.results[2 * b]["Y"] for b in range(4)], 0)




def build_geg():
    nc = bass.Bass("TRN2", target_bir_lowering=False)
    XF = nc.dram_tensor("XF", [4096, 1024], F32, kind="ExternalInput").ap()
    XQ = nc.dram_tensor("XQ", [2048, 1024], F32, kind="ExternalInput").ap()
    WQT = nc.dram_tensor("WQT", [1024, 1024], F32, kind="ExternalInput").ap()
    WKT = nc.dram_tensor("WKT", [1024, 1024], F32, kind="ExternalInput").ap()
    WVT = nc.dram_tensor("WVT", [1024, 1024], F32, kind="ExternalInput").ap()
    WOT = nc.dram_tensor("WOT", [1024, 1024], F32, kind="ExternalInput").ap()
    NWT = nc.dram_tensor("NWT", [128, 8], F32, kind="ExternalInput").ap()
    PW = nc.dram_tensor("PW", [16, 9], F32, kind="ExternalInput").ap()
    HD = nc.dram_tensor("HD", [16, 1], F32, kind="ExternalInput").ap()
    GEGM = nc.dram_tensor("GEGM", [9, 9], F32, kind="ExternalInput").ap()
    QOFF = nc.dram_tensor("QOFF", [1, 16], F32, kind="ExternalInput").ap()
    IDB = nc.dram_tensor("IDB", [128, 128], BF16, kind="ExternalInput").ap()
    IDF = nc.dram_tensor("IDF", [128, 128], F32, kind="ExternalInput").ap()
    Y = nc.dram_tensor("Y", [2048, 1024], F32, kind="ExternalOutput").ap()
    CSC = nc.dram_tensor("CSC", [16, 9], F32, kind="Internal").ap()
    with ExitStack() as ctx:
        E = ctx.enter_context
        P = Prog(nc, ctx)
        x_fm = E(nc.sbuf_tensor("x_fm", [128, 8 * 4096], BF16))
        xq_fm = E(nc.sbuf_tensor("xq_fm", [128, 8 * 2048], BF16))
        att_fm = E(nc.sbuf_tensor("att_fm", [128, 8 * 2048], BF16))
        wot = E(nc.sbuf_tensor("wot", [128, 8 * 1024], BF16))
        wpr = E(nc.sbuf_tensor("wpr", [128, 3 * 1024], BF16))
        wld = E(nc.sbuf_tensor("wld", [128, 1024], F32))
        kP = E(nc.sbuf_tensor("kP", [128, 4096], BF16))
        qP = E(nc.sbuf_tensor("qP", [128, 2048], BF16))
        vP = E(nc.sbuf_tensor("vP", [128, 32 * 128], BF16))
        msks = E(nc.sbuf_tensor("msks", [128, 32 * 128], BF16))
        kraw = E(nc.sbuf_tensor("kraw", [128, 512], F32))
        sq = E(nc.sbuf_tensor("sq", [128, 512], F32))
        rn2 = E(nc.sbuf_tensor("rn2", [2, 512], F32))
        onesAB = E(nc.sbuf_tensor("onesAB", [128, 2], F32))
        selAB = E(nc.sbuf_tensor("selAB", [2, 128], F32))
        selI = E(nc.sbuf_tensor("selI", [2, 128], I32))
        sb = E(nc.sbuf_tensor("sb", [128, 512], BF16))
        acc = E(nc.sbuf_tensor("acc", [128, 512], BF16))
        cb = E(nc.sbuf_tensor("cb", [128, 144], F32))
        cwk = E(nc.sbuf_tensor("cwk", [16, 16], F32))
        qbc = E(nc.sbuf_tensor("qbc", [128, 4], F32))
        dvc = E(nc.sbuf_tensor("dvc", [128, 4], F32))
        ioc = E(nc.sbuf_tensor("ioc", [128, 1], I32))
        iocf = E(nc.sbuf_tensor("iocf", [128, 1], F32))
        ioi = E(nc.sbuf_tensor("ioi", [128, 128], I32))
        ioif = E(nc.sbuf_tensor("ioif", [128, 128], F32))
        ao = E(nc.sbuf_tensor("ao", [128, 64], F32))
        aob = E(nc.sbuf_tensor("aob", [128, 64], BF16))
        xt = E(nc.sbuf_tensor("xt", [128, 1024], F32))
        lnx = E(nc.sbuf_tensor("lnx", [128, 1024], F32))
        lnb_ = E(nc.sbuf_tensor("lnb_", [128, 512], BF16))
        mcol = E(nc.sbuf_tensor("mcol", [128, 1], F32))
        ssq = E(nc.sbuf_tensor("ssq", [128, 1], F32))
        var = E(nc.sbuf_tensor("var", [128, 1], F32))
        rstd = E(nc.sbuf_tensor("rstd", [128, 1], F32))
        eps5 = E(nc.sbuf_tensor("eps5", [128, 1], F32))
        eps6 = E(nc.sbuf_tensor("eps6", [128, 1], F32))
        nwc = E(nc.sbuf_tensor("nwc", [128, 8], F32))
        qoffs = E(nc.sbuf_tensor("qoffs", [1, 16], F32))
        ones1 = E(nc.sbuf_tensor("ones1", [1, 128], F32))
        ys = E(nc.sbuf_tensor("ys", [128, 1024], F32))
        idb = E(nc.sbuf_tensor("idb", [128, 128], BF16))
        idf = E(nc.sbuf_tensor("idf", [128, 128], F32))
        psA = E(nc.psum_tensor("psA", [128, 512], F32))
        pn = E(nc.psum_tensor("pn", [2, 512], F32))
        poQ = [E(nc.psum_tensor("poq%d" % i, [128, 64], F32)) for i in range(4)]
        ptr = E(nc.psum_tensor("ptr", [128, 128], BF16))
        ptrf = E(nc.psum_tensor("ptrf", [128, 512], F32))
        py0 = psA
        py1 = ptrf
        block = E(nc.Block())

        P.ld(lambda e: e.dma_start(out=idb[:], in_=IDB))
        P.ld(lambda e: e.dma_start(out=idf[:], in_=IDF))
        P.ld(lambda e: e.dma_start(out=nwc[:], in_=NWT))
        P.ld(lambda e: e.dma_start(out=qoffs[:], in_=QOFF))
        P.ld(lambda e: e.dma_start(out=cwk[0:16, 0:9], in_=PW))
        P.ld(lambda e: e.dma_start(out=cwk[0:16, 9:10], in_=HD))
        P.ld(lambda e: e.dma_start(out=wld[0:9, 0:9], in_=GEGM))
        P.v(lambda e: e.memset(eps5[:], 1e-5), after=[P.dma])
        P.v(lambda e: e.memset(eps6[:], 1e-6))
        P.v(lambda e: e.memset(ones1[:], 1.0))
        P.v(lambda e: e.memset(onesAB[:], 0.0))
        P.v(lambda e: e.memset(onesAB[0:64, 0:1], 1.0))
        P.v(lambda e: e.memset(onesAB[64:128, 1:2], 1.0))
        P.g(lambda e: e.iota(selI[:], [[1, 128]], base=0, channel_multiplier=-64),
            after=[P.dma])
        P.v(lambda e: e.tensor_copy(ioif[0:2, :], selI[:]), after=[P.ctr["gpsimd"]])
        P.v(lambda e: e.tensor_scalar(selAB[:], ioif[0:2, :], 0.0, 0.0, ALU.is_ge, ALU.add),
            after=[P.ctr["vector"]])
        P.v(lambda e: e.scalar_tensor_tensor(selAB[:], ioif[0:2, :], 63.0, selAB[:],
                                             ALU.is_le, ALU.mult),
            after=[P.ctr["vector"]])
        P.g(lambda e: e.iota(ioi[0:16, 0:9], [[1, 9]], base=0, channel_multiplier=0),
            after=[P.dma])
        P.v(lambda e: e.tensor_copy(ioif[0:16, 0:9], ioi[0:16, 0:9]), after=[P.ctr["gpsimd"]])
        P.v(lambda e: e.tensor_scalar(cwk[0:16, 10:11], cwk[0:16, 9:10], 1.0, 0.0,
                                      ALU.add, ALU.add), after=[P.ctr["vector"]])
        P.v(lambda e: e.tensor_scalar(ioif[0:16, 0:9], ioif[0:16, 0:9], -1.0,
                                      cwk[0:16, 10:11], ALU.mult, ALU.add),
            after=[P.ctr["vector"]])
        P.v(lambda e: e.tensor_scalar(ioif[0:16, 0:9], ioif[0:16, 0:9], 0.0, 1.0,
                                      ALU.max, ALU.min), after=[P.ctr["vector"]])
        P.v(lambda e: e.tensor_tensor(ioif[0:16, 0:9], ioif[0:16, 0:9], cwk[0:16, 0:9],
                                      ALU.mult), after=[P.ctr["vector"]])
        P.v(lambda e: e.nop(), after=[P.ctr["vector"]])
        P.t(lambda e: e.transpose(ptrf[0:9, 0:16], ioif[0:16, 0:9], idf[0:16, 0:16]),
            after=[P.ctr["vector"]])
        P.v(lambda e: e.tensor_copy(ioif[0:9, 16:32], ptrf[0:9, 0:16]), after=[P.ctr["tensor"]])
        P.v(lambda e: e.nop(), after=[P.ctr["vector"]])
        P.t(lambda e: e.matmul(ptrf[0:16, 0:9], ioif[0:9, 16:32], wld[0:9, 0:9],
                               start=True, stop=True), after=[P.ctr["vector"]])
        P.v(lambda e: e.tensor_copy(ioif[0:16, 32:41], ptrf[0:16, 0:9]), after=[P.ctr["tensor"]])
        P.st(lambda e: e.dma_start(out=CSC, in_=ioif[0:16, 32:41]),
             after=[P.ctr["vector"]])
        P.ld(lambda e: e.dma_start(out=wld[0:1, 0:144],
                                   in_=CSC.rearrange("(o h) m -> o (h m)", o=1)), after=[P.dmo])
        P.t(lambda e: e.matmul(ptrf[:, 0:144], ones1[:], wld[0:1, 0:144],
                               start=True, stop=True), after=[P.dma, P.ctr["vector"]])
        P.v(lambda e: e.tensor_copy(cb[:], ptrf[:, 0:144]), after=[P.ctr["tensor"]])
        for jt in range(32):
            P.ld(lambda e, jt=jt: e.dma_start(out=xt[:], in_=XF[jt * 128:(jt + 1) * 128, :]),
                 after=ALL(P))
            P.v(lambda e: e.nop(), after=[P.dma])
            emit_ln(P, lnx, xt, wld, mcol, ssq, var, rstd, eps5)
            for hf in range(2):
                P.v(lambda e, hf=hf: e.tensor_copy(lnb_[:], lnx[:, hf * 512:(hf + 1) * 512]),
                    after=[P.ctr["vector"], P.ctr["tensor"]])
                emit_tr(P, (lambda c, jt=jt, hf=hf: x_fm[:, (hf * 4 + c) * 4096 + jt * 128:
                                                         (hf * 4 + c) * 4096 + jt * 128 + 128]),
                        lnb_, ptr, idb, 4)
        for jt in range(16):
            P.ld(lambda e, jt=jt: e.dma_start(out=xt[:], in_=XQ[jt * 128:(jt + 1) * 128, :]),
                 after=ALL(P))
            P.v(lambda e: e.nop(), after=[P.dma])
            emit_ln(P, lnx, xt, wld, mcol, ssq, var, rstd, eps5)
            for hf in range(2):
                P.v(lambda e, hf=hf: e.tensor_copy(lnb_[:], lnx[:, hf * 512:(hf + 1) * 512]),
                    after=[P.ctr["vector"], P.ctr["tensor"]])
                emit_tr(P, (lambda c, jt=jt, hf=hf: xq_fm[:, (hf * 4 + c) * 2048 + jt * 128:
                                                          (hf * 4 + c) * 2048 + jt * 128 + 128]),
                        lnb_, ptr, idb, 4)
        for c in range(8):
            P.ld(lambda e, c=c: e.dma_start(out=wld[:], in_=WOT[c * 128:(c + 1) * 128, :]),
                 after=ALL(P))
            P.v(lambda e, c=c: e.tensor_copy(wot[:, c * 1024:(c + 1) * 1024], wld[:]),
                after=[P.dma])
            P.v(lambda e, c=c: e.tensor_scalar(wot[:, c * 1024:(c + 1) * 1024],
                                               wot[:, c * 1024:(c + 1) * 1024],
                                               nwc[:, c:c + 1], 0.0, ALU.mult, ALU.add),
                after=[P.ctr["vector"]])
        for hp in range(8):
            for wi, W in enumerate([WQT, WKT, WVT]):
                P.ld(lambda e, wi=wi, W=W, hp=hp: e.dma_start(
                    out=wld[:].rearrange("p (c n) -> p c n", n=128),
                    in_=W[:, hp * 128:(hp + 1) * 128].rearrange("(c p) n -> p c n", p=128)),
                    after=ALL(P))
                P.v(lambda e, wi=wi: e.tensor_copy(wpr[:, wi * 1024:(wi + 1) * 1024], wld[:]),
                    after=[P.dma])
            for which, n5, src_fm, dstP in [(1, 8, x_fm, kP), (0, 4, xq_fm, qP)]:
                W0 = 4096 if which == 1 else 2048
                for j5 in range(n5):
                    for c in range(8):
                        P.t(lambda e, which=which, j5=j5, c=c, src_fm=src_fm, W0=W0: e.matmul(
                            psA[:], wpr[:, which * 1024 + c * 128:which * 1024 + c * 128 + 128],
                            src_fm[:, c * W0 + j5 * 512: c * W0 + j5 * 512 + 512],
                            start=(c == 0), stop=(c == 7)), after=[P.ctr["vector"]])
                    P.v(lambda e: e.tensor_copy(kraw[:], psA[:]), after=[P.ctr["tensor"]])
                    P.s(lambda e: e.activation(sq[:], kraw[:], AF.Square),
                        after=[P.ctr["vector"]])
                    P.t(lambda e: e.matmul(pn[:], onesAB[:], sq[:], start=True, stop=True),
                        after=[P.ctr["scalar"]])
                    P.s(lambda e: e.activation(rn2[:], pn[:], AF.Sqrt), after=[P.ctr["tensor"]])
                    P.v(lambda e: e.tensor_scalar(rn2[:], rn2[:], 1e-12, 0.0, ALU.max, ALU.add),
                        after=[P.ctr["scalar"]])
                    P.v(lambda e: e.reciprocal(rn2[:], rn2[:]), after=[P.ctr["vector"]])
                    P.v(lambda e: e.nop(), after=[P.ctr["vector"]])
                    P.t(lambda e: e.matmul(ptrf[:], selAB[:], rn2[:],
                                           start=True, stop=True), after=[P.ctr["vector"]])
                    P.v(lambda e, dstP=dstP, j5=j5: e.tensor_tensor(
                        dstP[:, j5 * 512:(j5 + 1) * 512], kraw[:], ptrf[:], ALU.mult),
                        after=[P.ctr["tensor"]])
            for jt in range(32):
                for c in range(8):
                    P.t(lambda e, jt=jt, c=c: e.matmul(
                        psA[:, 0:128], x_fm[:, c * 4096 + jt * 128: c * 4096 + jt * 128 + 128],
                        wpr[:, 2048 + c * 128: 2048 + c * 128 + 128],
                        start=(c == 0), stop=(c == 7)), after=[P.ctr["vector"]])
                P.v(lambda e, jt=jt: e.tensor_copy(vP[:, jt * 128:(jt + 1) * 128],
                                                   psA[:, 0:128]), after=[P.ctr["tensor"]])
            for st in range(4):
                nch = 8 * st + 8
                for q4 in range(4):
                    P.t(lambda e, st=st, q4=q4: e.matmul(
                        ptrf[:, 0:1], ones1[:], qoffs[:, st * 4 + q4:st * 4 + q4 + 1],
                        start=True, stop=True), after=[P.ctr["vector"], P.ctr["scalar"]])
                    P.v(lambda e, q4=q4: e.tensor_copy(qbc[:, q4:q4 + 1], ptrf[:, 0:1]),
                        after=[P.ctr["tensor"]])
                P.g(lambda e: e.iota(ioc[:], [[0, 1]], base=1, channel_multiplier=1),
                    after=[P.ctr["vector"]])
                P.v(lambda e: e.tensor_copy(iocf[:], ioc[:]), after=[P.ctr["gpsimd"]])
                for q4 in range(4):
                    P.v(lambda e, q4=q4: e.scalar_tensor_tensor(
                        dvc[:, q4:q4 + 1], qbc[:, q4:q4 + 1], 1.0, iocf[:],
                        ALU.mult, ALU.add), after=[P.ctr["vector"]])
                P.v(lambda e: e.reciprocal(dvc[:], dvc[:]), after=[P.ctr["vector"]])
                for mi in range(8):
                    cjj = 8 * st + mi
                    P.g(lambda e, cjj=cjj: e.iota(ioi[:], [[-1, 128]], base=cjj * 128,
                                                  channel_multiplier=1),
                        after=[P.ctr["vector"]])
                    P.v(lambda e: e.tensor_copy(ioif[:], ioi[:]), after=[P.ctr["gpsimd"]])
                    P.v(lambda e: e.nop(), after=[P.ctr["vector"]])
                    for q4 in range(4):
                        P.v(lambda e, mi=mi, q4=q4: e.tensor_scalar(
                            msks[:, (mi * 4 + q4) * 128:(mi * 4 + q4) * 128 + 128],
                            ioif[:], qbc[:, q4:q4 + 1], 0.0, ALU.is_le, ALU.add),
                            after=[P.ctr["vector"]])
                for hh in range(2):
                    H0 = hh * 64
                    for cjj in range(nch):
                        P.t(lambda e, cjj=cjj, H0=H0, st=st: e.matmul(
                            psA[:], kP[H0:H0 + 64, cjj * 128:(cjj + 1) * 128],
                            qP[H0:H0 + 64, st * 512:(st + 1) * 512],
                            start=True, stop=True),
                            after=[P.ctr["vector"], P.ctr["scalar"]])
                        P.s(lambda e: e.activation(sb[:], psA[:], AF.Copy),
                            after=[P.ctr["tensor"], P.ctr["vector"]])
                        hb = (hp * 2 + hh) * 9
                        P.v(lambda e, hb=hb: e.tensor_scalar(
                            acc[:], sb[:], cb[:, hb + 8:hb + 9], 0.0, ALU.mult, ALU.add),
                            after=[P.ctr["scalar"], P.ctr["tensor"]])
                        for m in range(7, 0, -1):
                            P.v(lambda e, hb=hb, m=m: e.scalar_tensor_tensor(
                                acc[:], acc[:], cb[:, hb + m:hb + m + 1], sb[:],
                                ALU.add, ALU.mult))
                        P.v(lambda e, hb=hb: e.tensor_scalar(
                            acc[:], acc[:], cb[:, hb:hb + 1], 0.0, ALU.add, ALU.add))
                        if cjj >= 8 * st:
                            mi = cjj - 8 * st
                            for q4 in range(4):
                                P.v(lambda e, mi=mi, q4=q4: e.tensor_tensor(
                                    acc[:, q4 * 128:(q4 + 1) * 128],
                                    acc[:, q4 * 128:(q4 + 1) * 128],
                                    msks[:, (mi * 4 + q4) * 128:(mi * 4 + q4) * 128 + 128],
                                    ALU.mult))
                        P.v(lambda e: e.nop(), after=[P.ctr["vector"]])
                        for q4 in range(4):
                            P.t(lambda e, cjj=cjj, q4=q4, H0=H0, nch=nch: e.matmul(
                                poQ[q4][:],
                                acc[:, q4 * 128:(q4 + 1) * 128],
                                vP[:, cjj * 128 + H0:cjj * 128 + H0 + 64],
                                start=(cjj == 0), stop=(cjj == nch - 1)),
                                after=[P.ctr["vector"]] if q4 == 0 else [])
                    for q4 in range(4):
                        P.v(lambda e, q4=q4: e.tensor_scalar(
                            ao[:], poQ[q4][:], dvc[:, q4:q4 + 1], 0.0,
                            ALU.mult, ALU.add), after=[P.ctr["tensor"], P.ctr["vector"]])
                        P.s(lambda e: e.activation(sq[:, 0:64], ao[:], AF.Square,
                                                   accum_out=ssq[:]), after=[P.ctr["vector"]])
                        P.s(lambda e: e.activation(rstd[:], ssq[:], AF.Sqrt, scale=1.0 / 64,
                                                   bias=eps6[:]))
                        P.v(lambda e: e.reciprocal(rstd[:], rstd[:]), after=[P.ctr["scalar"]])
                        P.v(lambda e: e.tensor_scalar(aob[:], ao[:], rstd[:], 0.0,
                                                      ALU.mult, ALU.add),
                            after=[P.ctr["vector"]])
                        P.v(lambda e: e.nop(), after=[P.ctr["vector"]])
                        P.t(lambda e: e.transpose(ptr[0:64, :], aob[:], idb[:]),
                            after=[P.ctr["vector"]])
                        h = hp * 2 + hh
                        i = st * 4 + q4
                        P.v(lambda e, h=h, i=i: e.tensor_copy(
                            att_fm[(h % 2) * 64:(h % 2) * 64 + 64,
                                   (h // 2) * 2048 + i * 128:(h // 2) * 2048 + i * 128 + 128],
                            ptr[0:64, :]), after=[P.ctr["tensor"]])
        for i in range(16):
            T0 = i * 128
            P.ld(lambda e, T0=T0: e.dma_start(out=xt[:], in_=XQ[T0:T0 + 128, :]),
                 after=ALL(P))
            for c in range(8):
                P.t(lambda e, c=c, T0=T0: e.matmul(
                    py0[:], att_fm[:, c * 2048 + T0:c * 2048 + T0 + 128],
                    wot[:, c * 1024:c * 1024 + 512],
                    start=(c == 0), stop=(c == 7)), after=[P.ctr["vector"]])
                P.t(lambda e, c=c, T0=T0: e.matmul(
                    py1[:], att_fm[:, c * 2048 + T0:c * 2048 + T0 + 128],
                    wot[:, c * 1024 + 512:c * 1024 + 1024],
                    start=(c == 0), stop=(c == 7)))
            yss = ys[:, 0:1024]
            P.v(lambda e: e.nop(), after=[P.dma, P.ctr["tensor"], P.dmo])
            P.v(lambda e, yss=yss: e.tensor_tensor(yss[:, 0:512], py0[:], xt[:, 0:512], ALU.add))
            P.v(lambda e, yss=yss: e.tensor_tensor(yss[:, 512:1024], py1[:], xt[:, 512:1024],
                                                   ALU.add))
            P.st(lambda e, T0=T0, yss=yss: e.dma_start(out=Y[T0:T0 + 128, :], in_=yss[:]),
                 after=[P.ctr["vector"]])
        P.finish()
        P.play(block)
    return nc


def _geg_monomial_const():
    LAM = (64 - 1) / 2.0
    M = np.zeros((9, 9), np.float64)
    M[0, 0] = 1.0
    M[1, 1] = 1.0
    Rp = M[0].copy()
    Rc = M[1].copy()
    for k in range(2, 9):
        c1 = 2.0 * (k + LAM - 1.0) / (k + 2.0 * LAM - 1.0)
        c2 = (k - 1.0) / (k + 2.0 * LAM - 1.0)
        Rn = c1 * np.roll(Rc, 1) - c2 * Rp
        Rn[0] = -c2 * Rp[0]
        M[k] = Rn
        Rp, Rc = Rc, Rn
    return M.astype(np.float32)


def run_geg(x, Wq, Wk, Wv, Wo, poly_weights, norm_w, head_degrees):
    GEGM = _geg_monomial_const()
    in_maps = []
    for c in range(8):
        b, half = c // 2, c % 2
        tiles = list(range(half, 32, 2))
        xq = np.concatenate([x[b, t * 128:(t + 1) * 128] for t in tiles], 0)
        qoff = np.array([[t * 128 for t in tiles]], np.float32)
        in_maps.append({"XF": np.ascontiguousarray(x[b]), "XQ": np.ascontiguousarray(xq),
                        "WQT": np.ascontiguousarray(Wq.T), "WKT": np.ascontiguousarray(Wk.T),
                        "WVT": np.ascontiguousarray(Wv.T), "WOT": np.ascontiguousarray(Wo.T),
                        "NWT": np.ascontiguousarray(np.tile(norm_w, 16).reshape(8, 128).T),
                        "PW": np.ascontiguousarray(poly_weights),
                        "HD": np.ascontiguousarray(head_degrees.reshape(16, 1)),
                        "GEGM": GEGM, "QOFF": qoff, "IDB": ID128B, "IDF": ID128F})
    res = run_bass_kernel_spmd(get_nc("geg"), in_maps, core_ids=list(range(8)))
    y = np.zeros((4, 4096, 1024), np.float32)
    for c in range(8):
        b, half = c // 2, c % 2
        out = res.results[c]["Y"]
        for li, t in enumerate(range(half, 32, 2)):
            y[b, t * 128:(t + 1) * 128] = out[li * 128:(li + 1) * 128]
    return y


_NCACHE = {}


def get_nc(name):
    if name not in _NCACHE:
        _NCACHE[name] = {"ffn": build_ffn, "ffn_pq": (lambda: build_ffn(True)), "attn_a": build_attn_a, "attn_a_dbg": (lambda: build_attn_a(True)),
                      "mha0": (lambda: build_mha(0)), "mha1": (lambda: build_mha(2048)),
                      "gla": build_gla, "gla_dbg": (lambda: build_gla(True)), "geg": build_geg}[name]()
    return _NCACHE[name]


def run_ffn(x, w_gate, w_up, w_down, wq=None):
    xs = np.ascontiguousarray(x.reshape(16384, 1024))
    in_maps = []
    if wq is None:
        nc = get_nc("ffn")
        wg = np.ascontiguousarray(w_gate)
        wu = np.ascontiguousarray(w_up)
        wd = np.ascontiguousarray(w_down)
        for c in range(8):
            in_maps.append({"X": xs[c * 2048:(c + 1) * 2048], "WG": wg, "WU": wu,
                            "WD": wd, "IDB": ID128B})
    else:
        nc = get_nc("ffn_pq")
        for c in range(8):
            in_maps.append({"X": xs[c * 2048:(c + 1) * 2048], "WGQ": wq["WGQO"],
                            "WUQ": wq["WUQO"], "WDQ": wq["WDQO"], "IDB": ID128B})
    res = run_bass_kernel_spmd(nc, in_maps, core_ids=list(range(8)))
    y = np.concatenate([res.results[c]["Y"] for c in range(8)], 0)
    if wq is None:
        wq = {k: np.ascontiguousarray(res.results[0][k]) for k in ("WGQO", "WUQO", "WDQO")}
        return y.reshape(4, 4096, 1024), wq
    return y.reshape(4, 4096, 1024)


def kernel(**inputs):
    d = inputs
    _unused = (d["L0_ffn_attn_mask"], d["L0_ffn_Wg"], d["L0_ffn_Wq"], d["L0_ffn_bq"],
               d["L0_ffn_Wkv"], d["L0_ffn_bkv"], d["L0_ffn_Wo"])
    x = d["L0_attn_a_x"]
    x = run_attn_a(x, d["L0_attn_a_Wq"], d["L0_attn_a_Wk"], d["L0_attn_a_Wv"],
                   d["L0_attn_a_Wo"], d["L0_ffn_ln_g"], d["L0_ffn_ln_b"])
    x = run_gla(x, d)
    x = run_geg(x, d["L1_attn_a_Wq"], d["L1_attn_a_Wk"], d["L1_attn_a_Wv"],
                d["L1_attn_a_Wo"], d["L1_attn_a_poly_weights"], d["L1_attn_a_norm_w"],
                d["L1_attn_a_head_degrees"])
    x, wq = run_ffn(x, d["L1_ffn_w_gate"], d["L1_ffn_w_up"], d["L1_ffn_w_down"])
    x = run_mha(x, d["L1_attn_b_Wq"], d["L1_attn_b_bq"], d["L1_attn_b_Wk"],
                d["L1_attn_b_bk"], d["L1_attn_b_Wv"], d["L1_attn_b_bv"],
                d["L1_attn_b_Wo"], d["L1_attn_b_bo"])
    x = run_ffn(x, d["L1_ffn_w_gate"], d["L1_ffn_w_up"], d["L1_ffn_w_down"], wq=wq)
    return x
